# version 01
# baseline (speedup 1.0000x reference)
import math
from contextlib import ExitStack

import numpy as np
import ml_dtypes
import concourse.bass as bass
import concourse.mybir as mybir
from concourse.bass_utils import run_bass_kernel_spmd

F32 = mybir.dt.float32
BF16 = mybir.dt.bfloat16
U8 = mybir.dt.uint8
AF = mybir.ActivationFunctionType
ALU = mybir.AluOpType
AX = mybir.AxisListType

L = 4096
D = 1024
NEG = -1e9


class Ctr:
    def __init__(self, h):
        self.h = h
        self.n = 0

    def inc(self, k=1):
        self.n += k
        return self.n


class K:
    pass


def _nm(ap):
    try:
        return ap.name()
    except Exception:
        return None


class Eng:

    def __init__(self, k, raw, ctr, is_pe=False):
        self.k = k
        self.raw = raw
        self.ctr = ctr
        self.is_pe = is_pe

    def _waits(self, outs, ins):
        k = self.k
        waits = {}

        def add(tok):
            if tok is None:
                return
            h, v = tok
            cur = waits.get(id(h))
            if cur is None or cur[1] < v:
                waits[id(h)] = (h, v)

        for ap in ins:
            add(k.lastw.get(_nm(ap)))
        for ap in outs:
            n = _nm(ap)
            add(k.lastw.get(n))
            for t in k.readers.get(n, ()):
                add(t)
        wm = k.watermark.setdefault(id(self.raw), {})
        for h, v in waits.values():
            if self.is_pe and h is k.sT.h:
                continue
            if wm.get(id(h), -1) >= v:
                continue
            self.raw.wait_ge(h, v)
            wm[id(h)] = v

    def _record(self, tok, outs, ins):
        k = self.k
        for ap in outs:
            n = _nm(ap)
            k.lastw[n] = tok
            k.readers[n] = []
        for ap in ins:
            k.readers.setdefault(_nm(ap), []).append(tok)

    def op(self, outs, ins, call, quiet=False, amt=1):
        self._waits(outs, ins)
        inst = call()
        if quiet:
            self.k.pending.setdefault(id(self), []).append((outs, ins))
            return None
        inst.then_inc(self.ctr.h, amt)
        tok = (self.ctr.h, self.ctr.inc(amt))
        for (o2, i2) in self.k.pending.pop(id(self), []):
            self._record(tok, o2, i2)
        self._record(tok, outs, ins)
        return tok

    def dma(self, out, in_):
        return self.op([out], [in_], lambda: self.raw.dma_start(out=out, in_=in_), amt=16)

    def tt(self, out, a, b, alu):
        return self.op([out], [a, b], lambda: self.raw.tensor_tensor(out, a, b, alu))

    def ts(self, out, a, s1, s2, op0, op1=None, accum=None):
        ins = [a] + [x for x in (s1, s2) if hasattr(x, "name")]
        outs = [out] + ([accum] if accum is not None else [])
        if op1 is None:
            return self.op(outs, ins, lambda: self.raw.tensor_scalar(
                out, a, s1, None, op0, accum_out=accum))
        return self.op(outs, ins, lambda: self.raw.tensor_scalar(
            out, a, s1, s2, op0, op1, accum_out=accum))

    def stt(self, out, a, s, b, op0, op1):
        ins = [a, b] + ([s] if hasattr(s, "name") else [])
        return self.op([out], ins, lambda: self.raw.scalar_tensor_tensor(out, a, s, b, op0, op1))

    def scan(self, out, d0, d1, init, op0, op1):
        ins = [d0, d1] + ([init] if hasattr(init, "name") else [])
        return self.op([out], ins, lambda: self.raw.tensor_tensor_scan(out, d0, d1, init, op0, op1))

    def red(self, out, a, alu, absval=False):
        return self.op([out], [a], lambda: self.raw.tensor_reduce(
            out, a, AX.X, alu, apply_absolute_value=absval))

    def copy(self, out, a):
        return self.op([out], [a], lambda: self.raw.tensor_copy(out, a))

    def memset(self, out, v):
        return self.op([out], [], lambda: self.raw.memset(out, v))

    def recip(self, out, a):
        return self.op([out], [a], lambda: self.raw.reciprocal(out, a))

    def act(self, out, in_, func, bias=0.0, scale=1.0, accum=None):
        ins = [in_] + [x for x in (bias, scale) if hasattr(x, "name")]
        outs = [out] + ([accum] if accum is not None else [])
        return self.op(outs, ins, lambda: self.raw.activation(
            out, in_, func, bias=bias, scale=scale, accum_out=accum))

    def mm(self, out, lhsT, rhs, start, stop):
        return self.op([out], [lhsT, rhs],
                       lambda: self.raw.matmul(out, lhsT, rhs, start=start, stop=stop),
                       quiet=not stop)

    def transpose(self, out, in_, ident):
        return self.op([out], [in_, ident], lambda: self.raw.transpose(out, in_, ident))

    def barrier(self, toks):
        wm = self.k.watermark.setdefault(id(self.raw), {})
        for t in toks:
            if t is None:
                continue
            h, v = t
            if wm.get(id(h), -1) >= v:
                continue
            self.raw.wait_ge(h, v)
            wm[id(h)] = v

    def wait_all(self, tensors):
        toks = []
        for ap in tensors:
            t = self.k.lastw.get(_nm(ap))
            if t:
                toks.append(t)
        self.barrier(toks)



def build_program(upto="all"):
    nc = bass.Bass("TRN2", target_bir_lowering=False, debug=False)
    k = K()
    k.nc = nc
    k.lastw = {}
    k.readers = {}
    k.watermark = {}
    k.pending = {}
    dt_in = {}

    def din(name, shape, dtype=F32):
        dt_in[name] = nc.dram_tensor(name, list(shape), dtype, kind="ExternalInput").ap()
        return dt_in[name]

    din("x", [L, D])
    din("qkv_w", [D, 3 * D]); din("qkv_b", [3 * D])
    din("rq_w", [D, 256]); din("rq_b", [256])
    din("rkv_w", [D, 512]); din("rkv_b", [512])
    din("rout_w", [256, D]); din("rout_b", [D])
    din("out_w", [D, D]); din("out_b", [D])
    din("gl_w", [64, 1]); din("gr_w", [64, 1])
    din("nattn_w", [D]); din("Wq", [D, D]); din("Wk", [D, 256]); din("Wv", [D, 256])
    din("Wo", [D, D]); din("qn_w", [64]); din("kn_w", [64]); din("ssmn_w", [D])
    din("Wx", [D, 320]); din("Wdt", [64, D]); din("bdt", [D]); din("A_log", [D, 128])
    din("Dp", [D]); din("Wsel", [D, D]); din("bsel", [D]); din("selg", [D])
    din("Wimp", [D, 128]); din("bimp", [128]); din("Wg", [D, D]); din("bg", [D])
    din("Wout", [D, D]); din("nffn_w", [D]); din("W1", [D, 8192]); din("W2", [4096, D])
    din("k0", [2048, 4096], U8); din("k1", [4096, 4096], U8); din("k2", [4096, D], U8)
    din("s0", [1]); din("s1", [1]); din("s2", [1])
    din("ident_b", [128, 128], BF16)
    din("cosT", [128, L], BF16); din("sinT", [128, L], BF16)
    din("maskband", [128, 256])
    din("masklatT", [64, L])
    k.din = dt_in

    k.out_d = nc.dram_tensor("out", [L, D], F32, kind="ExternalOutput").ap()
    k.xres = nc.dram_tensor("xres", [L, D], F32, kind="Internal").ap()
    k.attnd = nc.dram_tensor("attnd", [L, D], BF16, kind="Internal").ap()
    k.mixd = nc.dram_tensor("mixd", [D, L], BF16, kind="Internal").ap()
    k.bounce = nc.dram_tensor("bounce", [16, L], F32, kind="Internal").ap()
    k.dramB = nc.dram_tensor("dramB", [256, L], BF16, kind="Internal").ap()
    k.xln2 = nc.dram_tensor("xln2", [L, D], F32, kind="Internal").ap()

    stack = ExitStack()
    k.stack = stack
    k.sD = Ctr(stack.enter_context(nc.semaphore("sD")))
    k.sG = Ctr(stack.enter_context(nc.semaphore("sG")))
    k.sT = Ctr(stack.enter_context(nc.semaphore("sT")))
    k.sV = Ctr(stack.enter_context(nc.semaphore("sV")))
    k.sA = Ctr(stack.enter_context(nc.semaphore("sA")))

    sb = lambda name, shape, dtype: stack.enter_context(nc.sbuf_tensor(name, list(shape), dtype))
    k.sb = sb
    k.identb = sb("identb", [128, 128], BF16)
    k.vec = sb("vec", [128, 96], F32)
    k.vecmap = {}
    k.veccol = [0]
    k.glcol = sb("glcol", [128, 2], BF16)
    k.rowtok = sb("rowtok", [128, 160], F32)
    k.cpool = sb("cpool", [128, 16], F32)
    k.junk = sb("junkt", [128, 1024], BF16)

    k.blk = stack.enter_context(nc.Block())
    stack.enter_context(nc.allow_low_precision(
        reason="bf16 PE transposes and bf16 copies; matmul accumulation is fp32 PSUM"))

    k.T = Eng(k, nc.tensor, k.sT, is_pe=True)
    k.V = Eng(k, nc.vector, k.sV)
    k.A = Eng(k, nc.scalar, k.sA)
    k.S = Eng(k, nc.sync, k.sD)
    k.G = Eng(k, nc.gpsimd, k.sG)

    emit_setup(k)
    emit_attn_a(k, src=k.din["x"], dst=(k.out_d if upto == "a" else k.xres))
    if upto != "a":
        emit_ffn(k, src=k.xres, dst=(k.out_d if upto == "f1" else k.xres))
    if upto not in ("a", "f1"):
        emit_attn_b(k, src=k.xres, dst=None)
        emit_swiglu(k, src=k.xln2, src_outer=k.xres,
                    dst=(k.out_d if upto == "b" else k.xres))
    if upto not in ("a", "f1", "b"):
        emit_ffn(k, src=k.xres, dst=k.out_d, tag="B")
    k.S.wait_all([k.out_d])
    nc.sync.wait_ge(k.sD.h, k.sD.n)
    nc.sync.wait_ge(k.sG.h, k.sG.n)
    stack.close()
    return k



CONSTS = [0.0, 1.0, -1.0, 0.5, -0.5, 1e-5, 1e-6, 1e-30, 0.125, 1.0 / 64,
          1.0 / 1024, 2.0, 4.0, 0.25, -0.125, 1e30]


def alloc_vec(k, name, ncols):
    c0 = k.veccol[0]
    assert c0 + ncols <= 96, (name, c0)
    k.vecmap[name] = (c0, ncols)
    k.veccol[0] = c0 + ncols
    return c0


def vcol(k, name, j=0):
    c0, n = k.vecmap[name]
    assert j < n, (name, j, n)
    return k.vec[:, c0 + j:c0 + j + 1]


def load_vec_chunks(k, name, dram, F):
    nchunk = (F + 127) // 128
    c0 = alloc_vec(k, name, nchunk)
    for c in range(nchunk):
        n = min(128, F - c * 128)
        k.S.dma(k.vec[0:n, c0 + c:c0 + c + 1], dram[c * 128:c * 128 + n].unsqueeze(1))


def load_vec_pair(k, name, dram, off_top, off_bot):
    c0 = alloc_vec(k, name, 1)
    k.S.dma(k.vec[0:64, c0:c0 + 1], dram[off_top:off_top + 64].unsqueeze(1))
    k.S.dma(k.vec[64:128, c0:c0 + 1], dram[off_bot:off_bot + 64].unsqueeze(1))


def bcast_row(k, dram_row, sb_tile, n_part):
    return k.G.dma(sb_tile, dram_row.partition_broadcast(n_part).squeeze(1))


def emit_setup(k):
    nc = k.nc
    for i, v in enumerate(CONSTS):
        k.V.memset(k.cpool[:, i:i + 1], v)
        nc.const_aps.register(F32, v, k.cpool[:, i:i + 1])
    k.S.dma(k.identb[:], k.din["ident_b"])
    k.G.dma(k.glcol[0:64, 0:1], k.din["gl_w"])
    k.G.dma(k.glcol[0:64, 1:2], k.din["gr_w"])
    k.G.dma(k.glcol[64:128, 0:1], k.din["gl_w"])
    k.G.dma(k.glcol[64:128, 1:2], k.din["gr_w"])

    qkv_b = k.din["qkv_b"]
    for hp in range(8):
        h0, h1 = 2 * hp, 2 * hp + 1
        load_vec_pair(k, f"qb{hp}_0", qkv_b, 64 * h0, 64 * h1)
        load_vec_pair(k, f"qb{hp}_1", qkv_b, D + 64 * h0, D + 64 * h1)
        load_vec_pair(k, f"qb{hp}_2", qkv_b, 2 * D + 64 * h0, 2 * D + 64 * h1)
    load_vec_chunks(k, "rq_b", k.din["rq_b"], 256)
    load_vec_chunks(k, "rkv_b", k.din["rkv_b"], 512)
    load_vec_chunks(k, "rout_b", k.din["rout_b"], D)
    load_vec_chunks(k, "out_b", k.din["out_b"], D)
    load_vec_chunks(k, "bdt", k.din["bdt"], D)
    load_vec_chunks(k, "bsel", k.din["bsel"], D)
    load_vec_chunks(k, "selg", k.din["selg"], D)
    load_vec_chunks(k, "bg", k.din["bg"], D)
    load_vec_chunks(k, "bimp", k.din["bimp"], 128)
    for s in ("s0", "s1", "s2"):
        c0 = alloc_vec(k, s, 1)
        bcast_row(k, k.din[s].unsqueeze(0), k.vec[:, c0:c0 + 1], 128)
    c_bselg = alloc_vec(k, "bselg", 8)
    c_s0x4 = alloc_vec(k, "s0x4", 1)
    c_s1x2 = alloc_vec(k, "s1x2", 1)
    c_mb = alloc_vec(k, "Mb", 1)
    k.S.dma(k.rowtok[0:1, 96:160], k.din["qn_w"].unsqueeze(0))
    k.S.dma(k.rowtok[0:1, 32:96], k.din["kn_w"].unsqueeze(0))

    b0 = k.vecmap["bsel"][0]
    g0 = k.vecmap["selg"][0]
    k.V.tt(k.vec[:, c_bselg:c_bselg + 8], k.vec[:, b0:b0 + 8], k.vec[:, g0:g0 + 8], ALU.mult)
    k.V.ts(k.vec[:, c_s0x4:c_s0x4 + 1], vcol(k, "s0"), 4.0, None, ALU.mult)
    k.V.ts(k.vec[:, c_s1x2:c_s1x2 + 1], vcol(k, "s1"), 2.0, None, ALU.mult)
    r = k.rowtok
    k.V.red(r[0:1, 0:1], r[0:1, 96:160], ALU.max, absval=True)
    k.V.red(r[0:1, 1:2], r[0:1, 32:96], ALU.max, absval=True)
    k.V.tt(r[0:1, 2:3], r[0:1, 0:1], r[0:1, 1:2], ALU.mult)
    k.V.ts(r[0:1, 2:3], r[0:1, 2:3], 8.0, -40.0, ALU.mult, ALU.add)
    k.V.ts(r[0:1, 2:3], r[0:1, 2:3], 0.0, None, ALU.max)
    k.V.ts(r[0:1, 2:3], r[0:1, 2:3], -1.0, None, ALU.mult)
    k.S.dma(k.bounce[0:1, 0:1], r[0:1, 2:3])
    bcast_row(k, k.bounce[0:1, 0:1], k.vec[:, c_mb:c_mb + 1], 128)



def ln_tile(k, xb, xlnt, stat, eps=1e-5):
    k.V.red(stat[:, 0:1], xb, ALU.add)
    k.V.ts(stat[:, 3:4], stat[:, 0:1], 1.0 / D, None, ALU.mult)
    k.V.ts(xb, xb, stat[:, 3:4], None, ALU.subtract)
    k.A.act(k.junk[:], xb, AF.Square, accum=stat[:, 1:2])
    k.A.act(stat[:, 2:3], stat[:, 1:2], AF.Ln, bias=eps, scale=1.0 / D)
    k.A.act(stat[:, 2:3], stat[:, 2:3], AF.Exp, scale=-0.5)
    k.V.ts(xlnt, xb, stat[:, 2:3], None, ALU.mult)


def rms_rstd(k, xb, stat, eps=1e-6):
    k.A.act(k.junk[:], xb, AF.Square, accum=stat[:, 1:2])
    k.A.act(stat[:, 2:3], stat[:, 1:2], AF.Ln, bias=eps, scale=1.0 / D)
    k.A.act(stat[:, 2:3], stat[:, 2:3], AF.Exp, scale=-0.5)



def emit_attn_a(k, src, dst):
    nc = k.nc
    ph = ExitStack()
    sbt = lambda name, shape, dtype: ph.enter_context(nc.sbuf_tensor(name, list(shape), dtype))
    pst = lambda name, shape, dtype=F32: ph.enter_context(nc.psum_tensor(name, list(shape), dtype))

    xlnT = sbt("xlnT", [128, 8, L], BF16)
    latoutT = sbt("latoutT", [128, 2, L], BF16)
    rowbuf = sbt("rowbufa", [1, L], F32)

    ps1 = pst("ps1", [128, 512])
    ps2 = pst("ps2", [128, 512])
    pt1 = pst("pt1", [128, 128], BF16)
    pt2 = pst("pt2", [128, 128], BF16)
    pso = pst("pso", [128, 512])

    sc1 = ExitStack()
    sb1 = lambda name, shape, dtype: sc1.enter_context(nc.sbuf_tensor(name, list(shape), dtype))
    xta = sb1("xta", [128, 1024], F32)
    xtb = sb1("xtb", [128, 1024], F32)
    stat = sb1("stat", [128, 4], F32)
    xlnt = sb1("xlnt", [128, 1024], BF16)
    for it in range(32):
        xb = (xta, xtb)[it % 2][:]
        k.S.dma(xb, src[it * 128:(it + 1) * 128, :])
        ln_tile(k, xb, xlnt[:], stat)
        for c in range(8):
            pt = (pt1, pt2)[c % 2]
            k.T.transpose(pt[:], xlnt[:, c * 128:(c + 1) * 128], k.identb[:])
            k.V.copy(xlnT[:, c, it * 128:(it + 1) * 128], pt[:])
    sc1.close()

    sc2 = ExitStack()
    sb2 = lambda name, shape, dtype: sc2.enter_context(nc.sbuf_tensor(name, list(shape), dtype))
    rqT = sb2("rqT", [128, 2, L], BF16)
    pooledT = sb2("pooledT", [128, 8, 64], BF16)
    rkvT = sb2("rkvT", [128, 4, 64], BF16)
    rvtok = sb2("rvtok", [64, 2, 130], BF16)
    wrq = sb2("wrq", [128, 8, 256], BF16)
    wrkv = sb2("wrkv", [128, 8, 512], BF16)
    platil = sb2("platil", [64, 512], F32)
    plat = sb2("plat", [64, 512], BF16)
    lat0 = sb2("lat0", [64, 512], BF16)
    sigtmp = sb2("sigtmp", [128, 512], F32)
    fbl = sb2("fbl", [64, 512], F32)
    masklatT = sb2("masklatTs", [64, L], F32)
    k.S.dma(masklatT[:], k.din["masklatT"])

    for c in range(8):
        k.V.red(pooledT[:, c, :].unsqueeze(2),
                xlnT[:, c, :].rearrange("p (a b) -> p a b", b=64), ALU.add)

    k.G.dma(wrq[:], k.din["rq_w"].rearrange("(c p) m -> p c m", p=128))
    for m in range(2):
        for t8 in range(8):
            ps = (ps1, ps2)[t8 % 2]
            for c in range(8):
                k.T.mm(ps[:], wrq[:, c, m * 128:(m + 1) * 128],
                       xlnT[:, c, t8 * 512:(t8 + 1) * 512], start=(c == 0), stop=(c == 7))
            k.A.act(rqT[:, m, t8 * 512:(t8 + 1) * 512], ps[:], AF.Identity,
                    bias=vcol(k, "rq_b", m))
    k.G.dma(wrkv[:], k.din["rkv_w"].rearrange("(c p) m -> p c m", p=128))
    for m in range(4):
        for c in range(8):
            k.T.mm(ps1[:, 0:64], wrkv[:, c, m * 128:(m + 1) * 128], pooledT[:, c, :],
                   start=(c == 0), stop=(c == 7))
        k.A.act(rkvT[:, m, :], ps1[:, 0:64], AF.Identity,
                bias=vcol(k, "rkv_b", m), scale=1.0 / 64)

    for j in range(2):
        pt = (pt1, pt2)[j % 2]
        k.T.transpose(pt[0:64, :], rkvT[:, 2 + j, :], k.identb[:])
        k.V.copy(rvtok[:, j, 0:64], pt[0:64, 0:64])
        k.V.copy(rvtok[:, j, 65:129], pt[0:64, 64:128])
        k.V.memset(rvtok[:, j, 64:65], 1.0)
        k.V.memset(rvtok[:, j, 129:130], 1.0)

    for h in range(4):
        rkh = rkvT[:, h // 2, :][(h % 2) * 64:(h % 2) * 64 + 64, :]
        rqh = rqT[:, h // 2, :][(h % 2) * 64:(h % 2) * 64 + 64, :]
        for t8 in range(8):
            q0 = t8 * 512
            k.T.mm(ps1[0:64, :], rkh, rqh[:, q0:q0 + 512], start=True, stop=True)
            k.V.tt(platil[:], ps1[0:64, :], masklatT[:, q0:q0 + 512], ALU.add)
            k.A.act(plat[:], platil[:], AF.Exp, scale=0.125)
            k.T.mm(pso[0:65, :], rvtok[:, h // 2, (h % 2) * 65:(h % 2) * 65 + 65],
                   plat[:], start=True, stop=True)
            k.V.ts(sigtmp[64:65, :], pso[64:65, :], 1e-30, None, ALU.add)
            k.V.recip(sigtmp[64:65, :], sigtmp[64:65, :])
            k.S.dma(k.bounce[0:1, q0:q0 + 512], sigtmp[64:65, :])
            bcast_row(k, k.bounce[0:1, q0:q0 + 512], fbl[:, :], 64)
            if h % 2 == 0:
                k.V.tt(latoutT[0:64, h // 2, q0:q0 + 512], pso[0:64, :], fbl[:, :], ALU.mult)
            else:
                k.V.tt(lat0[:], pso[0:64, :], fbl[:, :], ALU.mult)
                k.G.dma(latoutT[64:128, h // 2, q0:q0 + 512], lat0[:])
    sc2.close()

    sc3 = ExitStack()
    sb3 = lambda name, shape, dtype: sc3.enter_context(nc.sbuf_tensor(name, list(shape), dtype))
    qkvT = sb3("qkvT", [128, 3, L], BF16)
    wq3 = sb3("wq3", [128, 8, 384], BF16)
    wro = sb3("wro", [128, 2, 128], BF16)
    vtok = sb3("vtok", [128, 32, 130], BF16)
    ptil = sb3("ptil", [128, 256], F32)
    pexp = sb3("pexp", [128, 32, 256], BF16)
    locT = sb3("locT", [128, L], BF16)
    remT = sb3("remT", [128, L], BF16)
    remh0 = sb3("remh0", [64, L], BF16)
    fb = sb3("fb", [64, 2, 2048], BF16)
    mixhh = sb3("mixhh", [64, L], BF16)
    maskband = sb3("maskbands", [128, 256], F32)
    k.S.dma(maskband[:], k.din["maskband"])

    for hp in range(8):
        h0, h1 = 2 * hp, 2 * hp + 1
        dst_off = 0
        for (c0, w) in [(64 * h0, 64), (64 * h1, 64), (D + 64 * h0, 64),
                        (D + 64 * h1, 64), (2 * D + 64 * h0, 128)]:
            k.G.dma(wq3[:, :, dst_off:dst_off + w],
                    k.din["qkv_w"][:, c0:c0 + w].rearrange("(c p) m -> p c m", p=128))
            dst_off += w
        for m in range(3):
            for t8 in range(8):
                ps = (ps1, ps2)[t8 % 2]
                for c in range(8):
                    k.T.mm(ps[:], wq3[:, c, m * 128:(m + 1) * 128],
                           xlnT[:, c, t8 * 512:(t8 + 1) * 512], start=(c == 0), stop=(c == 7))
                k.A.act(qkvT[:, m, t8 * 512:(t8 + 1) * 512], ps[:], AF.Identity,
                        bias=vcol(k, f"qb{hp}_{m}"))
        k.G.dma(wro[:], k.din["rout_w"][:, hp * 128:(hp + 1) * 128].rearrange("(c p) m -> p c m", p=128))
        for t8 in range(8):
            ps = (ps1, ps2)[t8 % 2]
            for c in range(2):
                k.T.mm(ps[:], wro[:, c, :], latoutT[:, c, t8 * 512:(t8 + 1) * 512],
                       start=(c == 0), stop=(c == 1))
            k.A.act(remT[:, t8 * 512:(t8 + 1) * 512], ps[:], AF.Identity,
                    bias=vcol(k, "rout_b", hp))
        for kt in range(32):
            pt = (pt1, pt2)[kt % 2]
            k.T.transpose(pt[:], qkvT[:, 2, kt * 128:(kt + 1) * 128], k.identb[:])
            k.V.copy(vtok[:, kt, 0:64], pt[:, 0:64])
            k.V.copy(vtok[:, kt, 65:129], pt[:, 64:128])
        k.V.memset(vtok[:, :, 64:65], 1.0)
        k.V.memset(vtok[:, :, 129:130], 1.0)

        for hh in range(2):
            kh = qkvT[:, 1, :][hh * 64:hh * 64 + 64, :]
            qh = qkvT[:, 0, :][hh * 64:hh * 64 + 64, :]
            for kt in range(32):
                qw = min(256, L - kt * 128)
                k.T.mm(ps1[:, 0:qw], kh[:, kt * 128:(kt + 1) * 128],
                       qh[:, kt * 128:kt * 128 + qw], start=True, stop=True)
                k.V.tt(ptil[:, 0:qw], ps1[:, 0:qw], maskband[:, 0:qw], ALU.add)
                k.A.act(pexp[:, kt, 0:qw], ptil[:, 0:qw], AF.Exp, scale=0.125)
            for qt in range(32):
                first = True
                if qt > 0:
                    k.T.mm(pso[0:65, 0:128], vtok[:, qt - 1, hh * 65:hh * 65 + 65],
                           pexp[:, qt - 1, 128:256], start=True, stop=False)
                    first = False
                k.T.mm(pso[0:65, 0:128], vtok[:, qt, hh * 65:hh * 65 + 65],
                       pexp[:, qt, 0:128], start=first, stop=True)
                k.V.copy(locT[0:65, qt * 128:(qt + 1) * 128], pso[0:65, 0:128])

            lraw = locT[0:64, :]
            lsig = locT[64:65, :]
            if hh == 0:
                remh = remT[0:64, :]
            else:
                k.G.dma(remh0[:], remT[64:128, :])
                remh = remh0[:]
            for t8 in range(8):
                q0 = t8 * 512
                k.T.mm(pso[0:1, :], k.glcol[0:64, 0:1], lraw[:, q0:q0 + 512],
                       start=True, stop=True)
                k.V.copy(rowbuf[0:1, q0:q0 + 512], pso[0:1, :])
            k.S.dma(k.bounce[1:2, :], rowbuf[0:1, :])
            k.G.dma(k.rowtok[:, 0:32], k.bounce[1:2, :].rearrange("one (p c) -> (one p) c", p=128))
            for t8 in range(8):
                q0 = t8 * 512
                k.T.mm(pso[0:1, :], k.glcol[0:64, 1:2], remh[:, q0:q0 + 512],
                       start=True, stop=True)
                k.V.copy(rowbuf[0:1, q0:q0 + 512], pso[0:1, :])
            k.S.dma(k.bounce[2:3, :], rowbuf[0:1, :])
            k.G.dma(k.bounce[3:4, :], lsig)
            k.G.dma(k.rowtok[:, 32:64], k.bounce[2:3, :].rearrange("one (p c) -> (one p) c", p=128))
            k.G.dma(k.rowtok[:, 64:96], k.bounce[3:4, :].rearrange("one (p c) -> (one p) c", p=128))
            r = k.rowtok
            k.V.ts(r[:, 64:96], r[:, 64:96], 1e-30, None, ALU.add)
            k.V.recip(r[:, 64:96], r[:, 64:96])
            k.V.tt(r[:, 0:32], r[:, 0:32], r[:, 64:96], ALU.mult)
            k.V.tt(r[:, 0:32], r[:, 0:32], r[:, 32:64], ALU.add)
            k.A.act(r[:, 32:64], r[:, 0:32], AF.Tanh, scale=0.5)
            k.V.ts(r[:, 32:64], r[:, 32:64], 0.5, 0.5, ALU.mult, ALU.add)
            k.V.tt(r[:, 0:32], r[:, 32:64], r[:, 64:96], ALU.mult)
            k.V.ts(r[:, 32:64], r[:, 32:64], -1.0, 1.0, ALU.mult, ALU.add)
            k.S.dma(k.bounce[1:2, :].rearrange("one (p c) -> (one p) c", p=128), r[:, 0:32])
            k.S.dma(k.bounce[2:3, :].rearrange("one (p c) -> (one p) c", p=128), r[:, 32:64])
            for half in range(2):
                q0 = half * 2048
                bcast_row(k, k.bounce[1:2, q0:q0 + 2048], fb[:, 0, :], 64)
                bcast_row(k, k.bounce[2:3, q0:q0 + 2048], fb[:, 1, :], 64)
                k.V.tt(mixhh[:, q0:q0 + 2048], lraw[:, q0:q0 + 2048], fb[:, 0, :], ALU.mult)
                k.V.tt(fb[:, 1, :], fb[:, 1, :], remh[:, q0:q0 + 2048], ALU.mult)
                k.V.tt(mixhh[:, q0:q0 + 2048], mixhh[:, q0:q0 + 2048], fb[:, 1, :], ALU.add)
            k.S.dma(k.mixd[hp * 128 + hh * 64:hp * 128 + hh * 64 + 64, :], mixhh[:])
    sc3.close()

    sc4 = ExitStack()
    sb4 = lambda name, shape, dtype: sc4.enter_context(nc.sbuf_tensor(name, list(shape), dtype))
    wowf = sb4("wowf", [128, 8, 1024], BF16)
    mixs = sb4("mixs", [128, 8, 512], BF16)
    dt512 = sb4("dt512", [128, 512], BF16)
    stg = sb4("stg", [128, 4, 1024], BF16)
    xr = sb4("xr", [128, 1024], F32)

    k.G.dma(wowf[:], k.din["out_w"].rearrange("(c p) m -> p c m", p=128))
    for t8 in range(8):
        k.G.dma(mixs[:], k.mixd[:, t8 * 512:(t8 + 1) * 512].rearrange("(c p) m -> p c m", p=128))
        for oc in range(8):
            ps = (ps1, ps2)[oc % 2]
            for c in range(8):
                k.T.mm(ps[:], wowf[:, c, oc * 128:(oc + 1) * 128], mixs[:, c, :],
                       start=(c == 0), stop=(c == 7))
            k.A.act(dt512[:], ps[:], AF.Identity, bias=vcol(k, "out_b", oc))
            for j in range(4):
                pt = (pt1, pt2)[j % 2]
                k.T.transpose(pt[:], dt512[:, j * 128:(j + 1) * 128], k.identb[:])
                k.V.copy(stg[:, j, oc * 128:(oc + 1) * 128], pt[:])
        for j in range(4):
            it = t8 * 4 + j
            k.S.dma(xr[:], src[it * 128:(it + 1) * 128, :])
            k.V.tt(xr[:], xr[:], stg[:, j, :], ALU.add)
            k.S.dma(dst[it * 128:(it + 1) * 128, :], xr[:])
    sc4.close()
    ph.close()


def emit_ffn(k, src, dst, tag=""):
    nc = k.nc
    ph = ExitStack()
    sbt = lambda name, shape, dtype: ph.enter_context(nc.sbuf_tensor(name + tag, list(shape), dtype))
    pst = lambda name, shape, dtype=F32: ph.enter_context(nc.psum_tensor(name + tag, list(shape), dtype))

    xfT = sbt("xfT", [128, 8, 512], BF16)
    w0d = sbt("w0d", [128, 8, 4096], BF16)
    h1 = sbt("h1f", [128, 32, 512], BF16)
    h2 = sbt("h2f", [128, 32, 512], BF16)
    ka = sbt("kaf", [128, 4096], BF16)
    kb = sbt("kbf", [128, 4096], BF16)
    w1c = sbt("w1cf", [128, 2, 512], BF16)
    w2c = sbt("w2cf", [128, 2, 512], BF16)
    dfT = sbt("dfT", [128, 8, 512], BF16)
    stg = sbt("stgf", [128, 4, 1024], BF16)
    xr = sbt("xrf", [128, 1024], F32)
    xlnt = sbt("xlntf", [128, 1024], BF16)
    stat = sbt("statf", [128, 4], F32)

    pb = [pst(f"pbf{i}", [128, 512]) for i in range(4)]
    pt1 = pst("ptf1", [128, 128], BF16)
    pt2 = pst("ptf2", [128, 128], BF16)

    for c in range(8):
        k.G.dma(ka[:], k.din["k0"][c * 128:(c + 1) * 128, :])
        k.G.dma(kb[:], k.din["k0"][1024 + c * 128:1024 + (c + 1) * 128, :])
        k.V.tt(w0d[:, c, :], ka[:], kb[:], ALU.subtract)

    for tt in range(8):
        t0 = tt * 512
        for j in range(4):
            it = tt * 4 + j
            k.S.dma(xr[:], src[it * 128:(it + 1) * 128, :])
            ln_tile(k, xr[:], xlnt[:], stat)
            for c in range(8):
                pt = (pt1, pt2)[c % 2]
                k.T.transpose(pt[:], xlnt[:, c * 128:(c + 1) * 128], k.identb[:])
                k.V.copy(xfT[:, c, j * 128:(j + 1) * 128], pt[:])
        for f0 in range(32):
            ps = pb[f0 % 2]
            for c in range(8):
                k.T.mm(ps[:], w0d[:, c, f0 * 128:(f0 + 1) * 128], xfT[:, c, :],
                       start=(c == 0), stop=(c == 7))
            k.A.act(h1[:, f0, :], ps[:], AF.Relu, scale=vcol(k, "s0x4"))
        for f1g in range(8):
            for fc in range(32):
                wb = w1c[:, fc % 2, :]
                k.G.dma(wb, k.din["k1"][fc * 128:(fc + 1) * 128,
                                        f1g * 512:(f1g + 1) * 512])
                k.V.ts(wb, wb, 2.0, -1.0, ALU.mult, ALU.add)
                for f1 in range(4):
                    k.T.mm(pb[f1][:], wb[:, f1 * 128:(f1 + 1) * 128], h1[:, fc, :],
                           start=(fc == 0), stop=(fc == 31))
            for f1 in range(4):
                k.A.act(h2[:, f1g * 4 + f1, :], pb[f1][:], AF.Relu, scale=vcol(k, "s1x2"))
        for dgrp in range(2):
            for fc in range(32):
                wb = w2c[:, fc % 2, :]
                k.G.dma(wb, k.din["k2"][fc * 128:(fc + 1) * 128,
                                        dgrp * 512:(dgrp + 1) * 512])
                k.V.ts(wb, wb, 2.0, -1.0, ALU.mult, ALU.add)
                for do in range(4):
                    k.T.mm(pb[do][:], wb[:, do * 128:(do + 1) * 128], h2[:, fc, :],
                           start=(fc == 0), stop=(fc == 31))
            for do in range(4):
                k.A.act(dfT[:, dgrp * 4 + do, :], pb[do][:], AF.Identity,
                        scale=vcol(k, "s2"))
        for oc in range(8):
            for j in range(4):
                pt = (pt1, pt2)[j % 2]
                k.T.transpose(pt[:], dfT[:, oc, j * 128:(j + 1) * 128], k.identb[:])
                k.V.copy(stg[:, j, oc * 128:(oc + 1) * 128], pt[:])
        for j in range(4):
            it = tt * 4 + j
            k.S.dma(xr[:], src[it * 128:(it + 1) * 128, :])
            k.V.tt(xr[:], xr[:], stg[:, j, :], ALU.add)
            k.S.dma(dst[it * 128:(it + 1) * 128, :], xr[:])
    ph.close()


def emit_attn_b(k, src, dst):
    nc = k.nc
    ph = ExitStack()
    sbt = lambda name, shape, dtype: ph.enter_context(nc.sbuf_tensor(name, list(shape), dtype))
    pst = lambda name, shape, dtype=F32: ph.enter_context(nc.psum_tensor(name, list(shape), dtype))

    rstds = sbt("rstds", [128, 32], F32)
    pb = [pst(f"pbb{i}", [128, 512]) for i in range(4)]
    pt1 = pst("ptb1", [128, 128], BF16)
    pt2 = pst("ptb2", [128, 128], BF16)

    onecol = sbt("onecolb", [128, 1], BF16)
    k.V.memset(onecol[:], 1.0)

    scP = ExitStack()
    sbP = lambda name, shape, dtype: scP.enter_context(nc.sbuf_tensor(name, list(shape), dtype))
    xrP = sbP("xrP", [128, 1024], F32)
    uf = sbP("ufP", [128, 1024], F32)
    statP = sbP("statP", [128, 4], F32)
    xlntP = sbP("xlntP", [128, 1024], BF16)
    for it in range(32):
        k.S.dma(xrP[:], src[it * 128:(it + 1) * 128, :])
        ln_tile(k, xrP[:], xlntP[:], statP)
        k.V.ts(uf[:], xrP[:], statP[:, 2:3], None, ALU.mult)
        k.S.dma(k.xln2[it * 128:(it + 1) * 128, :], uf[:])
    scP.close()
    src = k.xln2

    scQ = ExitStack()
    sbQ = lambda name, shape, dtype: scQ.enter_context(nc.sbuf_tensor(name, list(shape), dtype))
    QT = sbQ("QTb", [128, 8, L], BF16)
    KT = sbQ("KTb", [128, 2, L], BF16)
    VT = sbQ("VTb", [128, 2, L], BF16)

    scA = ExitStack()
    sbA = lambda name, shape, dtype: scA.enter_context(nc.sbuf_tensor(name, list(shape), dtype))
    xaT = sbA("xaT", [128, 8, L], BF16)
    xr = sbA("xrb", [128, 1024], F32)
    xat = sbA("xat", [128, 1024], BF16)
    stat = sbA("statb", [128, 4], F32)
    for it in range(32):
        k.S.dma(xr[:], src[it * 128:(it + 1) * 128, :])
        rms_rstd(k, xr[:], stat)
        k.V.copy(rstds[:, it:it + 1], stat[:, 2:3])
        k.V.ts(xat[:], xr[:], stat[:, 2:3], None, ALU.mult)
        for c in range(8):
            pt = (pt1, pt2)[c % 2]
            k.T.transpose(pt[:], xat[:, c * 128:(c + 1) * 128], k.identb[:])
            k.V.copy(xaT[:, c, it * 128:(it + 1) * 128], pt[:])
    wq = sbA("wqb", [128, 8, 1024], BF16)
    k.G.dma(wq[:], k.din["Wq"].rearrange("(c p) m -> p c m", p=128))
    nw = sbA("nwb", [128, 8], F32)
    for c in range(8):
        k.S.dma(nw[:, c:c + 1], k.din["nattn_w"][c * 128:(c + 1) * 128].unsqueeze(1))
    for c in range(8):
        k.V.ts(wq[:, c, :], wq[:, c, :], nw[:, c:c + 1], None, ALU.mult)
    for oc in range(8):
        for t8 in range(8):
            ps = pb[t8 % 2]
            for c in range(8):
                k.T.mm(ps[:], wq[:, c, oc * 128:(oc + 1) * 128],
                       xaT[:, c, t8 * 512:(t8 + 1) * 512], start=(c == 0), stop=(c == 7))
            k.A.act(QT[:, oc, t8 * 512:(t8 + 1) * 512], ps[:], AF.Identity)
    wkv = sbA("wkvb", [128, 8, 512], BF16)
    k.G.dma(wkv[:, :, 0:256], k.din["Wk"].rearrange("(c p) m -> p c m", p=128))
    k.G.dma(wkv[:, :, 256:512], k.din["Wv"].rearrange("(c p) m -> p c m", p=128))
    for c in range(8):
        k.V.ts(wkv[:, c, :], wkv[:, c, :], nw[:, c:c + 1], None, ALU.mult)
    for oc in range(4):
        dstt = (KT, KT, VT, VT)[oc]
        for t8 in range(8):
            ps = pb[t8 % 2]
            for c in range(8):
                k.T.mm(ps[:], wkv[:, c, oc * 128:(oc + 1) * 128],
                       xaT[:, c, t8 * 512:(t8 + 1) * 512], start=(c == 0), stop=(c == 7))
            k.A.act(dstt[:, oc % 2, t8 * 512:(t8 + 1) * 512], ps[:], AF.Identity)
    scA.close()

    scC = ExitStack()
    sbC = lambda name, shape, dtype: scC.enter_context(nc.sbuf_tensor(name, list(shape), dtype))
    sqb = sbC("sqb", [128, L], BF16)
    rowb = sbC("rowbb", [1, L], F32)
    rsb = sbC("rsbc", [128, L], F32)
    rot = sbC("rotb", [128, L], BF16)
    cosT = sbC("cosTb", [128, L], BF16)
    sinT = sbC("sinTb", [128, L], BF16)
    qn2 = sbC("qn2", [128, 2], F32)
    k.S.dma(cosT[:], k.din["cosT"])
    k.S.dma(sinT[:], k.din["sinT"])
    k.S.dma(qn2[0:64, 0:1], k.din["qn_w"].unsqueeze(1))
    k.S.dma(qn2[64:128, 0:1], k.din["qn_w"].unsqueeze(1))
    k.S.dma(qn2[0:64, 1:2], k.din["kn_w"].unsqueeze(1))
    k.S.dma(qn2[64:128, 1:2], k.din["kn_w"].unsqueeze(1))

    def qknorm_rope(T, nchunks, gcol):
        for ch in range(nchunks):
            k.A.act(sqb[:], T[:, ch, :], AF.Square)
            for half in range(2):
                b = half * 64
                for p8 in range(8):
                    sl = slice(p8 * 512, (p8 + 1) * 512)
                    k.T.mm(pb[2][0:1, :], onecol[b:b + 64, 0:1], sqb[b:b + 64, sl],
                           start=True, stop=True)
                    k.A.act(rowb[0:1, sl], pb[2][0:1, :], AF.Ln, bias=1e-6, scale=1.0 / 64)
                k.A.act(rowb[0:1, :], rowb[0:1, :], AF.Exp, scale=-0.5)
                k.S.dma(k.bounce[4:5, :], rowb[0:1, :])
                k.G.dma(rsb[b:b + 64, :], k.bounce[4:5, :].partition_broadcast(64).squeeze(1))
                hsl = T[:, ch, :][b:b + 64, :]
                k.V.stt(hsl, hsl, qn2[b:b + 64, gcol:gcol + 1], rsb[b:b + 64, :],
                        ALU.mult, ALU.mult)
                k.S.dma(rot[b:b + 32, :], hsl[32:64, :])
                k.S.dma(rot[b + 32:b + 64, :], hsl[0:32, :])
                k.V.tt(rot[b:b + 64, :], rot[b:b + 64, :], sinT[b:b + 64, :], ALU.mult)
                k.V.tt(hsl, hsl, cosT[b:b + 64, :], ALU.mult)
                k.V.tt(hsl, hsl, rot[b:b + 64, :], ALU.add)

    qknorm_rope(QT, 8, 0)
    qknorm_rope(KT, 2, 1)
    scC.close()

    scD = ExitStack()
    sbD = lambda name, shape, dtype: scD.enter_context(nc.sbuf_tensor(name, list(shape), dtype))
    vtok = sbD("vtokb", [128, 2, 32, 130], BF16)
    KTd = sbD("KTdb", [128, 4, L], BF16)
    pexpb = sbD("pexpb", [128, 2, 512], BF16)
    sigt = sbD("sigtb", [128, 512], F32)
    rsb2 = sbD("rsb2", [64, 512], F32)
    anorm = sbD("anormb", [64, 512], BF16)
    for j in range(4):
        srcK = KT[:, j // 2, :][(j % 2) * 64:(j % 2) * 64 + 64, :]
        k.S.dma(KTd[0:64, j, :], srcK)
        k.S.dma(KTd[64:128, j, :], srcK)
    for pr in range(2):
        for kt in range(32):
            pt = (pt1, pt2)[kt % 2]
            k.T.transpose(pt[:], VT[:, pr, kt * 128:(kt + 1) * 128], k.identb[:])
            k.V.copy(vtok[:, pr, kt, 0:64], pt[:, 0:64])
            k.V.copy(vtok[:, pr, kt, 65:129], pt[:, 64:128])
        k.V.memset(vtok[:, pr, :, 64:65], 1.0)
        k.V.memset(vtok[:, pr, :, 129:130], 1.0)

    psO = pst("psOb", [128, 512])
    for h in range(16):
        kv = h // 4
        qch, qb = h // 2, (h % 2) * 64
        kch = kv // 2
        qhsl = QT[:, qch, :][qb:qb + 64, :]
        khsl = KTd[:, kv, :][qb:qb + 64, :]
        for qp in range(8):
            q0 = qp * 512
            for kt in range(32):
                psA = pb[kt % 2]
                k.T.mm(psA[:], khsl[:, kt * 128:(kt + 1) * 128], qhsl[:, q0:q0 + 512],
                       start=True, stop=True)
                pe = pexpb[:, kt % 2, :]
                k.A.act(pe, psA[:], AF.Exp, scale=0.125, bias=vcol(k, "Mb"))
                k.T.mm(psO[0:65, :], vtok[:, kch, kt, (kv % 2) * 65:(kv % 2) * 65 + 65],
                       pe, start=(kt == 0), stop=(kt == 31))
            k.V.ts(sigt[64:65, :], psO[64:65, :], 1e-30, None, ALU.add)
            k.V.recip(sigt[64:65, :], sigt[64:65, :])
            k.S.dma(k.bounce[4:5, q0:q0 + 512], sigt[64:65, :])
            k.G.dma(rsb2[:], k.bounce[4:5, q0:q0 + 512].partition_broadcast(64).squeeze(1))
            k.V.tt(anorm[:], psO[0:64, :], rsb2[:], ALU.mult)
            k.S.dma(k.mixd[h * 64:(h + 1) * 64, q0:q0 + 512], anorm[:])

    scE = ExitStack()
    sbE = lambda name, shape, dtype: scE.enter_context(nc.sbuf_tensor(name, list(shape), dtype))
    wo = sbE("wob2", [128, 8, 1024], BF16)
    mixs = sbE("mixsb", [128, 8, 512], BF16)
    dt512 = sbE("dt512b", [128, 512], BF16)
    stg = sbE("stgb", [128, 4, 1024], BF16)
    k.G.dma(wo[:], k.din["Wo"].rearrange("(c p) m -> p c m", p=128))
    for t8 in range(8):
        k.G.dma(mixs[:], k.mixd[:, t8 * 512:(t8 + 1) * 512].rearrange("(c p) m -> p c m", p=128))
        for oc in range(8):
            ps = pb[oc % 2]
            for c in range(8):
                k.T.mm(ps[:], wo[:, c, oc * 128:(oc + 1) * 128],
                       mixs[:, c, :], start=(c == 0), stop=(c == 7))
            k.A.act(dt512[:], ps[:], AF.Identity, scale=0.5)
            for j in range(4):
                pt = (pt1, pt2)[j % 2]
                k.T.transpose(pt[:], dt512[:, j * 128:(j + 1) * 128], k.identb[:])
                k.V.copy(stg[:, j, oc * 128:(oc + 1) * 128], pt[:])
        for j in range(4):
            it = t8 * 4 + j
            k.S.dma(k.attnd[it * 128:(it + 1) * 128, :], stg[:, j, :])
    scE.close()
    scD.close()
    scQ.close()

    scF = ExitStack()
    sbF = lambda name, shape, dtype: scF.enter_context(nc.sbuf_tensor(name, list(shape), dtype))
    xnT = sbF("xnT", [128, 8, L], BF16)
    dTT = sbF("dTT", [128, 8, L], BF16)
    aA = sbF("aAb", [128, 8, 128], F32)
    scFP = ExitStack()
    sbF = lambda name, shape, dtype: scFP.enter_context(nc.sbuf_tensor(name, list(shape), dtype))
    BT = sbF("BTb", [128, L], BF16)
    CT = sbF("CTb", [128, L], BF16)
    impT = sbF("impTb", [128, L], BF16)
    drT = sbF("drTb", [64, L], BF16)
    wssm = sbF("wssmb", [128, 1024], F32)
    bcast_row(k, k.din["ssmn_w"].unsqueeze(0), wssm[:], 128)
    xnt = sbF("xntb", [128, 1024], BF16)
    xrF = sbF("xrF", [128, 1024], F32)
    for it in range(32):
        k.S.dma(xrF[:], src[it * 128:(it + 1) * 128, :])
        k.V.stt(xnt[:], xrF[:], rstds[:, it:it + 1], wssm[:], ALU.mult, ALU.mult)
        for c in range(8):
            pt = (pt1, pt2)[c % 2]
            k.T.transpose(pt[:], xnt[:, c * 128:(c + 1) * 128], k.identb[:])
            k.V.copy(xnT[:, c, it * 128:(it + 1) * 128], pt[:])
    wx = sbF("wxb2", [128, 8, 320], BF16)
    k.G.dma(wx[:], k.din["Wx"].rearrange("(c p) m -> p c m", p=128))
    for t8 in range(8):
        sl = slice(t8 * 512, (t8 + 1) * 512)
        for c in range(8):
            k.T.mm(pb[0][0:64, :], wx[:, c, 0:64], xnT[:, c, sl], start=(c == 0), stop=(c == 7))
        k.A.act(drT[:, sl], pb[0][0:64, :], AF.Identity)
        for c in range(8):
            k.T.mm(pb[1][:], wx[:, c, 64:192], xnT[:, c, sl], start=(c == 0), stop=(c == 7))
        k.A.act(BT[:, sl], pb[1][:], AF.Identity)
        for c in range(8):
            k.T.mm(pb[2][:], wx[:, c, 192:320], xnT[:, c, sl], start=(c == 0), stop=(c == 7))
        k.A.act(CT[:, sl], pb[2][:], AF.Identity)
    wdt = sbF("wdtb", [64, 8, 128], BF16)
    k.G.dma(wdt[:], k.din["Wdt"].rearrange("r (c m) -> r c m", m=128))
    spl = sbF("splb", [128, 512], BF16)
    for oc in range(8):
        for t8 in range(8):
            sl = slice(t8 * 512, (t8 + 1) * 512)
            ps = pb[t8 % 2]
            k.T.mm(ps[:], wdt[:, oc, :], drT[:, sl], start=True, stop=True)
            k.A.act(spl[:], ps[:], AF.Abs, bias=vcol(k, "bdt", oc))
            k.A.act(spl[:], spl[:], AF.Exp, scale=-1.0)
            k.A.act(spl[:], spl[:], AF.Ln, bias=1.0)
            k.A.act(dTT[:, oc, sl], ps[:], AF.Relu, bias=vcol(k, "bdt", oc))
            k.V.tt(dTT[:, oc, sl], dTT[:, oc, sl], spl[:], ALU.add)
    wsel = sbF("wselb", [128, 8, 1024], BF16)
    k.G.dma(wsel[:], k.din["Wsel"].rearrange("(c p) m -> p c m", p=128))
    selc = sbF("selcb", [128, 512], BF16)
    for oc in range(8):
        for t8 in range(8):
            sl = slice(t8 * 512, (t8 + 1) * 512)
            ps = pb[2 + t8 % 2]
            for c in range(8):
                k.T.mm(ps[:], wsel[:, c, oc * 128:(oc + 1) * 128], xnT[:, c, sl],
                       start=(c == 0), stop=(c == 7))
            k.A.act(selc[:], ps[:], AF.Sigmoid, scale=vcol(k, "selg", oc),
                    bias=vcol(k, "bselg", oc))
            k.V.tt(xnT[:, oc, sl], xnT[:, oc, sl], selc[:], ALU.mult)
    wimp = sbF("wimpb", [128, 8, 128], BF16)
    k.G.dma(wimp[:], k.din["Wimp"].rearrange("(c p) m -> p c m", p=128))
    for t8 in range(8):
        sl = slice(t8 * 512, (t8 + 1) * 512)
        ps = pb[t8 % 2]
        for c in range(8):
            k.T.mm(ps[:], wimp[:, c, :], xnT[:, c, sl], start=(c == 0), stop=(c == 7))
        k.A.act(impT[:, sl], ps[:], AF.Sigmoid, bias=vcol(k, "bimp", 0))
    k.G.dma(wsel[:], k.din["Wg"].rearrange("(c p) m -> p c m", p=128))
    for oc in range(8):
        for t8 in range(8):
            sl = slice(t8 * 512, (t8 + 1) * 512)
            ps = pb[2 + t8 % 2]
            for c in range(8):
                k.T.mm(ps[:], wsel[:, c, oc * 128:(oc + 1) * 128], xnT[:, c, sl],
                       start=(c == 0), stop=(c == 7))
            k.A.act(selc[:], ps[:], AF.Sigmoid, bias=vcol(k, "bg", oc))
            k.V.tt(xnT[:, oc, sl], xnT[:, oc, sl], selc[:], ALU.mult)
    k.V.tt(BT[:], BT[:], impT[:], ALU.mult)
    k.V.tt(CT[:], CT[:], impT[:], ALU.mult)
    for c in range(8):
        k.V.tt(xnT[:, c, :], xnT[:, c, :], dTT[:, c, :], ALU.mult)
    k.S.dma(k.dramB[0:128, :], BT[:])
    k.S.dma(k.dramB[128:256, :], CT[:])
    k.G.dma(aA[:], k.din["A_log"].rearrange("(c p) n -> p c n", p=128))
    for c in range(8):
        k.A.act(aA[:, c, :], aA[:, c, :], AF.Exp)
        k.V.ts(aA[:, c, :], aA[:, c, :], -1.0, None, ALU.mult)
    scFP.close()

    scG = ExitStack()
    sbG = lambda name, shape, dtype: scG.enter_context(nc.sbuf_tensor(name, list(shape), dtype))
    SEG = 1024
    NSEG = L // SEG
    yacc = sbG("yaccb", [128, 8, SEG], BF16)
    carry = sbG("carryb", [128, 1024], F32)
    Bb = [sbG(f"Bbb{i}", [128, SEG], BF16) for i in range(2)]
    Cb = [sbG(f"Cbb{i}", [128, SEG], BF16) for i in range(2)]
    dab = sbG("dabb", [128, SEG], BF16)
    ub = sbG("ubb", [128, SEG], BF16)
    stile = sbG("stileb", [128, SEG], BF16)
    ztile = sbG("ztileb", [128, SEG], BF16)
    xd128 = sbG("xd128b", [128, 128], BF16)
    sd512 = sbG("sd512b", [128, 512], BF16)
    ssmstg = sbG("ssmstgb", [128, 4, 1024], BF16)
    xrG = sbG("xrG", [128, 1024], F32)
    atile = sbG("atileb", [128, 1024], BF16)
    dpb = sbG("dpbb", [128, 1024], F32)
    wout = sbG("woutb", [128, 8, 1024], BF16)
    bcast_row(k, k.din["Dp"].unsqueeze(0), dpb[:], 128)
    k.G.dma(wout[:], k.din["Wout"].rearrange("(c p) m -> p c m", p=128))
    k.V.memset(carry[:], 0.0)

    for seg in range(NSEG):
        s0 = seg * SEG
        for c in range(8):
            k.V.memset(yacc[:, c, :], 0.0)
        for n in range(128):
            bb, cb = Bb[n % 2], Cb[n % 2]
            k.S.dma(bb[:], k.dramB[n:n + 1, s0:s0 + SEG].partition_broadcast(128).squeeze(1))
            k.S.dma(cb[:], k.dramB[128 + n:129 + n, s0:s0 + SEG].partition_broadcast(128).squeeze(1))
            for c in range(8):
                col = n * 8 + c
                k.A.act(dab[:], dTT[:, c, s0:s0 + SEG], AF.Exp, scale=aA[:, c, n:n + 1])
                k.V.tt(ub[:], xnT[:, c, s0:s0 + SEG], bb[:], ALU.mult)
                k.V.scan(stile[:], dab[:], ub[:], carry[:, col:col + 1], ALU.mult, ALU.add)
                k.V.copy(carry[:, col:col + 1], stile[:, SEG - 1:SEG])
                k.V.tt(ztile[:], stile[:], cb[:], ALU.mult)
                k.V.tt(yacc[:, c, :], yacc[:, c, :], ztile[:], ALU.add)
        for j in range(SEG // 128):
            it = (s0 // 128) + j
            k.S.dma(xrG[:], src[it * 128:(it + 1) * 128, :])
            k.V.tt(atile[:], xrG[:], dpb[:], ALU.mult)
            for c in range(8):
                pt = (pt1, pt2)[c % 2]
                k.T.transpose(pt[:], atile[:, c * 128:(c + 1) * 128], k.identb[:])
                k.V.copy(xd128[:], pt[:])
                k.V.tt(yacc[:, c, j * 128:(j + 1) * 128],
                       yacc[:, c, j * 128:(j + 1) * 128], xd128[:], ALU.add)
        for t4 in range(SEG // 512):
            sl = slice(t4 * 512, (t4 + 1) * 512)
            for oc in range(8):
                ps = pb[oc % 2]
                for c in range(8):
                    k.T.mm(ps[:], wout[:, c, oc * 128:(oc + 1) * 128], yacc[:, c, sl],
                           start=(c == 0), stop=(c == 7))
                k.A.act(sd512[:], ps[:], AF.Identity, scale=0.5)
                for j in range(4):
                    pt = (pt1, pt2)[j % 2]
                    k.T.transpose(pt[:], sd512[:, j * 128:(j + 1) * 128], k.identb[:])
                    k.V.copy(ssmstg[:, j, oc * 128:(oc + 1) * 128], pt[:])
            for j in range(4):
                it = (s0 + t4 * 512) // 128 + j
                k.S.dma(xrG[:], src[it * 128:(it + 1) * 128, :])
                k.G.dma(atile[:], k.attnd[it * 128:(it + 1) * 128, :])
                k.V.tt(xrG[:], xrG[:], atile[:], ALU.add)
                k.V.tt(xrG[:], xrG[:], ssmstg[:, j, :], ALU.add)
                k.S.dma(k.xln2[it * 128:(it + 1) * 128, :], xrG[:])
    scG.close()
    scF.close()
    ph.close()


def emit_swiglu(k, src, src_outer, dst):
    nc = k.nc
    ph = ExitStack()
    sbt = lambda name, shape, dtype: ph.enter_context(nc.sbuf_tensor(name, list(shape), dtype))
    pst = lambda name, shape, dtype=F32: ph.enter_context(nc.psum_tensor(name, list(shape), dtype))

    xfT = sbt("xfTs", [128, 8, 512], BF16)
    hbl = sbt("hbls", [128, 32, 512], BF16)
    w2r = sbt("w2rs", [128, 32, 1024], BF16)
    w1a = sbt("w1as", [128, 8, 128], BF16)
    w1g = sbt("w1gs", [128, 8, 128], BF16)
    sa = sbt("sas", [128, 512], BF16)
    sg = sbt("sgs", [128, 512], BF16)
    dfT = sbt("dfTs", [128, 8, 512], BF16)
    stg = sbt("stgs", [128, 4, 1024], BF16)
    xr = sbt("xrs", [128, 1024], F32)
    xo = sbt("xos", [128, 1024], F32)
    xlnt = sbt("xlnts", [128, 1024], BF16)
    stat = sbt("stats", [128, 4], F32)
    wffn = sbt("wffns", [128, 1024], F32)
    bcast_row(k, k.din["nffn_w"].unsqueeze(0), wffn[:], 128)

    pb = [pst(f"pbs{i}", [128, 512]) for i in range(4)]
    pt1 = pst("pts1", [128, 128], BF16)
    pt2 = pst("pts2", [128, 128], BF16)

    k.G.dma(w2r[:], k.din["W2"].rearrange("(c p) m -> p c m", p=128))
    for tt in range(8):
        for j in range(4):
            it = tt * 4 + j
            k.S.dma(xr[:], src[it * 128:(it + 1) * 128, :])
            rms_rstd(k, xr[:], stat)
            k.V.stt(xlnt[:], xr[:], stat[:, 2:3], wffn[:], ALU.mult, ALU.mult)
            for c in range(8):
                pt = (pt1, pt2)[c % 2]
                k.T.transpose(pt[:], xlnt[:, c * 128:(c + 1) * 128], k.identb[:])
                k.V.copy(xfT[:, c, j * 128:(j + 1) * 128], pt[:])
        for fc in range(32):
            k.G.dma(w1a[:], k.din["W1"][:, fc * 128:(fc + 1) * 128].rearrange("(c p) m -> p c m", p=128))
            k.G.dma(w1g[:], k.din["W1"][:, 4096 + fc * 128:4096 + (fc + 1) * 128].rearrange("(c p) m -> p c m", p=128))
            for c in range(8):
                k.T.mm(pb[0][:], w1a[:, c, :], xfT[:, c, :], start=(c == 0), stop=(c == 7))
            for c in range(8):
                k.T.mm(pb[1][:], w1g[:, c, :], xfT[:, c, :], start=(c == 0), stop=(c == 7))
            k.A.act(sa[:], pb[0][:], AF.Silu)
            k.A.act(sg[:], pb[1][:], AF.Identity)
            k.V.tt(hbl[:, fc, :], sa[:], sg[:], ALU.mult)
        for dgrp in range(2):
            for fc in range(32):
                for do in range(4):
                    k.T.mm(pb[do][:], w2r[:, fc, dgrp * 512 + do * 128:dgrp * 512 + (do + 1) * 128],
                           hbl[:, fc, :], start=(fc == 0), stop=(fc == 31))
            for do in range(4):
                k.A.act(dfT[:, dgrp * 4 + do, :], pb[do][:], AF.Identity)
        for oc in range(8):
            for j in range(4):
                pt = (pt1, pt2)[j % 2]
                k.T.transpose(pt[:], dfT[:, oc, j * 128:(j + 1) * 128], k.identb[:])
                k.V.copy(stg[:, j, oc * 128:(oc + 1) * 128], pt[:])
        for j in range(4):
            it = tt * 4 + j
            k.S.dma(xr[:], src[it * 128:(it + 1) * 128, :])
            k.S.dma(xo[:], src_outer[it * 128:(it + 1) * 128, :])
            k.V.tt(xr[:], xr[:], stg[:, j, :], ALU.add)
            k.V.tt(xr[:], xr[:], xo[:], ALU.add)
            k.S.dma(dst[it * 128:(it + 1) * 128, :], xr[:])
    ph.close()



def _consts():
    c = {}
    c["ident_b"] = np.eye(128, dtype=ml_dtypes.bfloat16)
    inv = 1.0 / (10000.0 ** (np.arange(0, 64, 2, dtype=np.float64) / 64.0))
    fr = np.arange(L, dtype=np.float64)[:, None] * inv[None, :]
    emb = np.concatenate([fr, fr], axis=1)
    cos = np.cos(emb).T
    sin = np.sin(emb).T
    sinr = sin.copy()
    sinr[0:32, :] *= -1.0
    c["cosT"] = np.ascontiguousarray(np.concatenate([cos, cos], 0)).astype(ml_dtypes.bfloat16)
    c["sinT"] = np.ascontiguousarray(np.concatenate([sinr, sinr], 0)).astype(ml_dtypes.bfloat16)
    kp = np.arange(128)[:, None]
    qo = np.arange(256)[None, :]
    allowed = (qo - kp > 0) & (qo - kp <= 128)
    c["maskband"] = np.where(allowed, 0.0, NEG).astype(np.float32)
    lat = np.arange(64)[:, None]
    tq = np.arange(L)[None, :]
    c["masklatT"] = np.where(lat < (tq // 64), 0.0, NEG).astype(np.float32)
    return c


def _in_map(inputs, b):
    m = {
        "x": inputs["L0_attn_a_x"][b],
        "qkv_w": inputs["L0_attn_a_qkv_w"], "qkv_b": inputs["L0_attn_a_qkv_b"],
        "rq_w": inputs["L0_attn_a_rq_w"], "rq_b": inputs["L0_attn_a_rq_b"],
        "rkv_w": inputs["L0_attn_a_rkv_w"], "rkv_b": inputs["L0_attn_a_rkv_b"],
        "rout_w": inputs["L0_attn_a_rout_w"], "rout_b": inputs["L0_attn_a_rout_b"],
        "out_w": inputs["L0_attn_a_out_w"], "out_b": inputs["L0_attn_a_out_b"],
        "gl_w": inputs["L0_attn_a_gl_w"], "gr_w": inputs["L0_attn_a_gr_w"],
        "nattn_w": inputs["L0_attn_b_norm_attn_w"],
        "Wq": inputs["L0_attn_b_Wq"], "Wk": inputs["L0_attn_b_Wk"],
        "Wv": inputs["L0_attn_b_Wv"], "Wo": inputs["L0_attn_b_Wo"],
        "qn_w": inputs["L0_attn_b_qn_w"], "kn_w": inputs["L0_attn_b_kn_w"],
        "ssmn_w": inputs["L0_attn_b_ssm_norm_w"],
        "Wx": inputs["L0_attn_b_Wx"], "Wdt": inputs["L0_attn_b_Wdt"],
        "bdt": inputs["L0_attn_b_bdt"], "A_log": inputs["L0_attn_b_A_log"],
        "Dp": inputs["L0_attn_b_Dp"], "Wsel": inputs["L0_attn_b_Wsel"],
        "bsel": inputs["L0_attn_b_bsel"], "selg": inputs["L0_attn_b_sel_gate"],
        "Wimp": inputs["L0_attn_b_Wimp"], "bimp": inputs["L0_attn_b_bimp"],
        "Wg": inputs["L0_attn_b_Wg"], "bg": inputs["L0_attn_b_bg"],
        "Wout": inputs["L0_attn_b_Wout"], "nffn_w": inputs["L0_attn_b_norm_ffn_w"],
        "W1": inputs["L0_attn_b_W1"], "W2": inputs["L0_attn_b_W2"],
        "k0": np.asarray(inputs["L0_ffn_k0"]).view(np.uint8),
        "k1": np.asarray(inputs["L0_ffn_k1"]).view(np.uint8),
        "k2": np.asarray(inputs["L0_ffn_k2"]).view(np.uint8),
        "s0": np.asarray(inputs["L0_ffn_s0"]).reshape(1),
        "s1": np.asarray(inputs["L0_ffn_s1"]).reshape(1),
        "s2": np.asarray(inputs["L0_ffn_s2"]).reshape(1),
    }
    m = {n: np.ascontiguousarray(np.asarray(v)) for n, v in m.items()}
    m.update(_consts())
    return m


def run_upto(inputs, upto, cores=None):
    k = build_program(upto)
    cores = cores if cores is not None else list(range(8))
    in_maps = [_in_map(inputs, b) for b in cores]
    res = run_bass_kernel_spmd(k.nc, in_maps, core_ids=list(range(len(cores))))
    return np.stack([np.asarray(res.results[i]["out"]) for i in range(len(cores))], axis=0)


def kernel(**inputs):
    k = build_program("all")
    in_maps = [_in_map(inputs, b) for b in range(8)]
    res = run_bass_kernel_spmd(k.nc, in_maps, core_ids=list(range(8)))
    return np.stack([np.asarray(res.results[b]["out"]) for b in range(8)], axis=0)


# version 02 (final)
# speedup vs baseline: 1.0275x; 1.0275x over previous
import math
from contextlib import ExitStack

import numpy as np
import ml_dtypes
import concourse.bass as bass
import concourse.mybir as mybir
from concourse.bass_utils import run_bass_kernel_spmd

F32 = mybir.dt.float32
BF16 = mybir.dt.bfloat16
U8 = mybir.dt.uint8
AF = mybir.ActivationFunctionType
ALU = mybir.AluOpType
AX = mybir.AxisListType

L = 4096
D = 1024
NEG = -1e9


class Ctr:
    def __init__(self, h):
        self.h = h
        self.n = 0

    def inc(self, k=1):
        self.n += k
        return self.n


class K:
    pass


def _nm(ap):
    try:
        return ap.name()
    except Exception:
        return None


class Eng:

    def __init__(self, k, raw, ctr, is_pe=False):
        self.k = k
        self.raw = raw
        self.ctr = ctr
        self.is_pe = is_pe

    def _waits(self, outs, ins):
        k = self.k
        waits = {}

        def add(tok):
            if tok is None:
                return
            h, v = tok
            cur = waits.get(id(h))
            if cur is None or cur[1] < v:
                waits[id(h)] = (h, v)

        for ap in ins:
            add(k.lastw.get(_nm(ap)))
        for ap in outs:
            n = _nm(ap)
            add(k.lastw.get(n))
            for t in k.readers.get(n, ()):
                add(t)
        wm = k.watermark.setdefault(id(self.raw), {})
        for h, v in waits.values():
            if self.is_pe and h is k.sT.h:
                continue
            if wm.get(id(h), -1) >= v:
                continue
            self.raw.wait_ge(h, v)
            wm[id(h)] = v

    def _record(self, tok, outs, ins):
        k = self.k
        for ap in outs:
            n = _nm(ap)
            k.lastw[n] = tok
            k.readers[n] = []
        for ap in ins:
            k.readers.setdefault(_nm(ap), []).append(tok)

    def op(self, outs, ins, call, quiet=False, amt=1):
        self._waits(outs, ins)
        inst = call()
        if quiet:
            self.k.pending.setdefault(id(self), []).append((outs, ins))
            return None
        inst.then_inc(self.ctr.h, amt)
        tok = (self.ctr.h, self.ctr.inc(amt))
        for (o2, i2) in self.k.pending.pop(id(self), []):
            self._record(tok, o2, i2)
        self._record(tok, outs, ins)
        return tok

    def dma(self, out, in_):
        return self.op([out], [in_], lambda: self.raw.dma_start(out=out, in_=in_), amt=16)

    def tt(self, out, a, b, alu):
        return self.op([out], [a, b], lambda: self.raw.tensor_tensor(out, a, b, alu))

    def ts(self, out, a, s1, s2, op0, op1=None, accum=None):
        ins = [a] + [x for x in (s1, s2) if hasattr(x, "name")]
        outs = [out] + ([accum] if accum is not None else [])
        if op1 is None:
            return self.op(outs, ins, lambda: self.raw.tensor_scalar(
                out, a, s1, None, op0, accum_out=accum))
        return self.op(outs, ins, lambda: self.raw.tensor_scalar(
            out, a, s1, s2, op0, op1, accum_out=accum))

    def stt(self, out, a, s, b, op0, op1):
        ins = [a, b] + ([s] if hasattr(s, "name") else [])
        return self.op([out], ins, lambda: self.raw.scalar_tensor_tensor(out, a, s, b, op0, op1))

    def scan(self, out, d0, d1, init, op0, op1):
        ins = [d0, d1] + ([init] if hasattr(init, "name") else [])
        return self.op([out], ins, lambda: self.raw.tensor_tensor_scan(out, d0, d1, init, op0, op1))

    def red(self, out, a, alu, absval=False):
        return self.op([out], [a], lambda: self.raw.tensor_reduce(
            out, a, AX.X, alu, apply_absolute_value=absval))

    def copy(self, out, a):
        return self.op([out], [a], lambda: self.raw.tensor_copy(out, a))

    def memset(self, out, v):
        return self.op([out], [], lambda: self.raw.memset(out, v))

    def recip(self, out, a):
        return self.op([out], [a], lambda: self.raw.reciprocal(out, a))

    def act(self, out, in_, func, bias=0.0, scale=1.0, accum=None):
        ins = [in_] + [x for x in (bias, scale) if hasattr(x, "name")]
        outs = [out] + ([accum] if accum is not None else [])
        return self.op(outs, ins, lambda: self.raw.activation(
            out, in_, func, bias=bias, scale=scale, accum_out=accum))

    def mm(self, out, lhsT, rhs, start, stop):
        return self.op([out], [lhsT, rhs],
                       lambda: self.raw.matmul(out, lhsT, rhs, start=start, stop=stop),
                       quiet=not stop)

    def transpose(self, out, in_, ident):
        return self.op([out], [in_, ident], lambda: self.raw.transpose(out, in_, ident))

    def barrier(self, toks):
        wm = self.k.watermark.setdefault(id(self.raw), {})
        for t in toks:
            if t is None:
                continue
            h, v = t
            if wm.get(id(h), -1) >= v:
                continue
            self.raw.wait_ge(h, v)
            wm[id(h)] = v

    def wait_all(self, tensors):
        toks = []
        for ap in tensors:
            t = self.k.lastw.get(_nm(ap))
            if t:
                toks.append(t)
        self.barrier(toks)



def build_program(upto="all"):
    nc = bass.Bass("TRN2", target_bir_lowering=False, debug=False)
    k = K()
    k.nc = nc
    k.lastw = {}
    k.readers = {}
    k.watermark = {}
    k.pending = {}
    dt_in = {}

    def din(name, shape, dtype=F32):
        dt_in[name] = nc.dram_tensor(name, list(shape), dtype, kind="ExternalInput").ap()
        return dt_in[name]

    din("x", [L, D])
    din("qkv_w", [D, 3 * D]); din("qkv_b", [3 * D])
    din("rq_w", [D, 256]); din("rq_b", [256])
    din("rkv_w", [D, 512]); din("rkv_b", [512])
    din("rout_w", [256, D]); din("rout_b", [D])
    din("out_w", [D, D]); din("out_b", [D])
    din("gl_w", [64, 1]); din("gr_w", [64, 1])
    din("nattn_w", [D]); din("Wq", [D, D]); din("Wk", [D, 256]); din("Wv", [D, 256])
    din("Wo", [D, D]); din("qn_w", [64]); din("kn_w", [64]); din("ssmn_w", [D])
    din("Wx", [D, 320]); din("Wdt", [64, D]); din("bdt", [D]); din("A_log", [D, 128])
    din("Dp", [D]); din("Wsel", [D, D]); din("bsel", [D]); din("selg", [D])
    din("Wimp", [D, 128]); din("bimp", [128]); din("Wg", [D, D]); din("bg", [D])
    din("Wout", [D, D]); din("nffn_w", [D]); din("W1", [D, 8192]); din("W2", [4096, D])
    din("k0", [2048, 4096], U8); din("k1", [4096, 4096], U8); din("k2", [4096, D], U8)
    din("s0", [1]); din("s1", [1]); din("s2", [1])
    din("ident_b", [128, 128], BF16)
    din("cosT", [128, L], BF16); din("sinT", [128, L], BF16)
    din("maskband", [128, 256])
    din("masklatT", [64, L])
    k.din = dt_in

    k.out_d = nc.dram_tensor("out", [L, D], F32, kind="ExternalOutput").ap()
    k.xres = nc.dram_tensor("xres", [L, D], F32, kind="Internal").ap()
    k.attnd = nc.dram_tensor("attnd", [L, D], BF16, kind="Internal").ap()
    k.mixd = nc.dram_tensor("mixd", [D, L], BF16, kind="Internal").ap()
    k.bounce = nc.dram_tensor("bounce", [16, L], F32, kind="Internal").ap()
    k.dramB = nc.dram_tensor("dramB", [256, L], BF16, kind="Internal").ap()
    k.xln2 = nc.dram_tensor("xln2", [L, D], F32, kind="Internal").ap()
    k.k1d = nc.dram_tensor("k1d", [4096, 4096], BF16, kind="Internal").ap()
    k.k2d = nc.dram_tensor("k2d", [4096, 1024], BF16, kind="Internal").ap()

    stack = ExitStack()
    k.stack = stack
    k.sD = Ctr(stack.enter_context(nc.semaphore("sD")))
    k.sG = Ctr(stack.enter_context(nc.semaphore("sG")))
    k.sT = Ctr(stack.enter_context(nc.semaphore("sT")))
    k.sV = Ctr(stack.enter_context(nc.semaphore("sV")))
    k.sA = Ctr(stack.enter_context(nc.semaphore("sA")))

    sb = lambda name, shape, dtype: stack.enter_context(nc.sbuf_tensor(name, list(shape), dtype))
    k.sb = sb
    k.identb = sb("identb", [128, 128], BF16)
    k.vec = sb("vec", [128, 96], F32)
    k.vecmap = {}
    k.veccol = [0]
    k.glcol = sb("glcol", [128, 2], BF16)
    k.rowtok = sb("rowtok", [128, 160], F32)
    k.cpool = sb("cpool", [128, 16], F32)
    k.junk = sb("junkt", [128, 1024], BF16)

    k.blk = stack.enter_context(nc.Block())
    stack.enter_context(nc.allow_low_precision(
        reason="bf16 PE transposes and bf16 copies; matmul accumulation is fp32 PSUM"))

    k.T = Eng(k, nc.tensor, k.sT, is_pe=True)
    k.V = Eng(k, nc.vector, k.sV)
    k.A = Eng(k, nc.scalar, k.sA)
    k.S = Eng(k, nc.sync, k.sD)
    k.G = Eng(k, nc.gpsimd, k.sG)

    emit_setup(k)
    emit_attn_a(k, src=k.din["x"], dst=(k.out_d if upto == "a" else k.xres))
    if upto != "a":
        emit_ffn(k, src=k.xres, dst=(k.out_d if upto == "f1" else k.xres))
    if upto not in ("a", "f1"):
        emit_attn_b(k, src=k.xres, dst=None)
        emit_swiglu(k, src=k.xln2, src_outer=k.xres,
                    dst=(k.out_d if upto == "b" else k.xres))
    if upto not in ("a", "f1", "b"):
        emit_ffn(k, src=k.xres, dst=k.out_d, tag="B")
    k.S.wait_all([k.out_d])
    nc.sync.wait_ge(k.sD.h, k.sD.n)
    nc.sync.wait_ge(k.sG.h, k.sG.n)
    stack.close()
    return k



CONSTS = [0.0, 1.0, -1.0, 0.5, -0.5, 1e-5, 1e-6, 1e-30, 0.125, 1.0 / 64,
          1.0 / 1024, 2.0, 4.0, 0.25, -0.125, 1e30]


def alloc_vec(k, name, ncols):
    c0 = k.veccol[0]
    assert c0 + ncols <= 96, (name, c0)
    k.vecmap[name] = (c0, ncols)
    k.veccol[0] = c0 + ncols
    return c0


def vcol(k, name, j=0):
    c0, n = k.vecmap[name]
    assert j < n, (name, j, n)
    return k.vec[:, c0 + j:c0 + j + 1]


def load_vec_chunks(k, name, dram, F):
    nchunk = (F + 127) // 128
    c0 = alloc_vec(k, name, nchunk)
    for c in range(nchunk):
        n = min(128, F - c * 128)
        k.S.dma(k.vec[0:n, c0 + c:c0 + c + 1], dram[c * 128:c * 128 + n].unsqueeze(1))


def load_vec_pair(k, name, dram, off_top, off_bot):
    c0 = alloc_vec(k, name, 1)
    k.S.dma(k.vec[0:64, c0:c0 + 1], dram[off_top:off_top + 64].unsqueeze(1))
    k.S.dma(k.vec[64:128, c0:c0 + 1], dram[off_bot:off_bot + 64].unsqueeze(1))


def bcast_row(k, dram_row, sb_tile, n_part):
    return k.G.dma(sb_tile, dram_row.partition_broadcast(n_part).squeeze(1))


def emit_setup(k):
    nc = k.nc
    for i, v in enumerate(CONSTS):
        k.V.memset(k.cpool[:, i:i + 1], v)
        nc.const_aps.register(F32, v, k.cpool[:, i:i + 1])
    k.S.dma(k.identb[:], k.din["ident_b"])
    k.G.dma(k.glcol[0:64, 0:1], k.din["gl_w"])
    k.G.dma(k.glcol[0:64, 1:2], k.din["gr_w"])
    k.G.dma(k.glcol[64:128, 0:1], k.din["gl_w"])
    k.G.dma(k.glcol[64:128, 1:2], k.din["gr_w"])

    qkv_b = k.din["qkv_b"]
    for hp in range(8):
        h0, h1 = 2 * hp, 2 * hp + 1
        load_vec_pair(k, f"qb{hp}_0", qkv_b, 64 * h0, 64 * h1)
        load_vec_pair(k, f"qb{hp}_1", qkv_b, D + 64 * h0, D + 64 * h1)
        load_vec_pair(k, f"qb{hp}_2", qkv_b, 2 * D + 64 * h0, 2 * D + 64 * h1)
    load_vec_chunks(k, "rq_b", k.din["rq_b"], 256)
    load_vec_chunks(k, "rkv_b", k.din["rkv_b"], 512)
    load_vec_chunks(k, "rout_b", k.din["rout_b"], D)
    load_vec_chunks(k, "out_b", k.din["out_b"], D)
    load_vec_chunks(k, "bdt", k.din["bdt"], D)
    load_vec_chunks(k, "bsel", k.din["bsel"], D)
    load_vec_chunks(k, "selg", k.din["selg"], D)
    load_vec_chunks(k, "bg", k.din["bg"], D)
    load_vec_chunks(k, "bimp", k.din["bimp"], 128)
    for s in ("s0", "s1", "s2"):
        c0 = alloc_vec(k, s, 1)
        bcast_row(k, k.din[s].unsqueeze(0), k.vec[:, c0:c0 + 1], 128)
    c_bselg = alloc_vec(k, "bselg", 8)
    c_s0x4 = alloc_vec(k, "s0x4", 1)
    c_s1x2 = alloc_vec(k, "s1x2", 1)
    c_mb = alloc_vec(k, "Mb", 1)
    k.S.dma(k.rowtok[0:1, 96:160], k.din["qn_w"].unsqueeze(0))
    k.S.dma(k.rowtok[0:1, 32:96], k.din["kn_w"].unsqueeze(0))

    b0 = k.vecmap["bsel"][0]
    g0 = k.vecmap["selg"][0]
    k.V.tt(k.vec[:, c_bselg:c_bselg + 8], k.vec[:, b0:b0 + 8], k.vec[:, g0:g0 + 8], ALU.mult)
    k.V.ts(k.vec[:, c_s0x4:c_s0x4 + 1], vcol(k, "s0"), 4.0, None, ALU.mult)
    k.V.ts(k.vec[:, c_s1x2:c_s1x2 + 1], vcol(k, "s1"), 2.0, None, ALU.mult)
    r = k.rowtok
    k.V.red(r[0:1, 0:1], r[0:1, 96:160], ALU.max, absval=True)
    k.V.red(r[0:1, 1:2], r[0:1, 32:96], ALU.max, absval=True)
    k.V.tt(r[0:1, 2:3], r[0:1, 0:1], r[0:1, 1:2], ALU.mult)
    k.V.ts(r[0:1, 2:3], r[0:1, 2:3], 8.0, -40.0, ALU.mult, ALU.add)
    k.V.ts(r[0:1, 2:3], r[0:1, 2:3], 0.0, None, ALU.max)
    k.V.ts(r[0:1, 2:3], r[0:1, 2:3], -1.0, None, ALU.mult)
    k.S.dma(k.bounce[0:1, 0:1], r[0:1, 2:3])
    bcast_row(k, k.bounce[0:1, 0:1], k.vec[:, c_mb:c_mb + 1], 128)



def ln_tile(k, xb, xlnt, stat, eps=1e-5):
    k.V.red(stat[:, 0:1], xb, ALU.add)
    k.V.ts(stat[:, 3:4], stat[:, 0:1], 1.0 / D, None, ALU.mult)
    k.V.ts(xb, xb, stat[:, 3:4], None, ALU.subtract)
    k.A.act(k.junk[:], xb, AF.Square, accum=stat[:, 1:2])
    k.A.act(stat[:, 2:3], stat[:, 1:2], AF.Ln, bias=eps, scale=1.0 / D)
    k.A.act(stat[:, 2:3], stat[:, 2:3], AF.Exp, scale=-0.5)
    k.V.ts(xlnt, xb, stat[:, 2:3], None, ALU.mult)


def rms_rstd(k, xb, stat, eps=1e-6):
    k.A.act(k.junk[:], xb, AF.Square, accum=stat[:, 1:2])
    k.A.act(stat[:, 2:3], stat[:, 1:2], AF.Ln, bias=eps, scale=1.0 / D)
    k.A.act(stat[:, 2:3], stat[:, 2:3], AF.Exp, scale=-0.5)



def emit_attn_a(k, src, dst):
    nc = k.nc
    ph = ExitStack()
    sbt = lambda name, shape, dtype: ph.enter_context(nc.sbuf_tensor(name, list(shape), dtype))
    pst = lambda name, shape, dtype=F32: ph.enter_context(nc.psum_tensor(name, list(shape), dtype))

    xlnT = sbt("xlnT", [128, 8, L], BF16)
    latoutT = sbt("latoutT", [128, 2, L], BF16)
    rowbuf = sbt("rowbufa", [1, L], F32)

    ps1 = pst("ps1", [128, 512])
    ps2 = pst("ps2", [128, 512])
    pt1 = pst("pt1", [128, 128], BF16)
    pt2 = pst("pt2", [128, 128], BF16)
    pso = pst("pso", [128, 512])

    sc1 = ExitStack()
    sb1 = lambda name, shape, dtype: sc1.enter_context(nc.sbuf_tensor(name, list(shape), dtype))
    xta = sb1("xta", [128, 1024], F32)
    xtb = sb1("xtb", [128, 1024], F32)
    stat = sb1("stat", [128, 4], F32)
    xlnt = sb1("xlnt", [128, 1024], BF16)
    for it in range(32):
        xb = (xta, xtb)[it % 2][:]
        k.S.dma(xb, src[it * 128:(it + 1) * 128, :])
        ln_tile(k, xb, xlnt[:], stat)
        for c in range(8):
            pt = (pt1, pt2)[c % 2]
            k.T.transpose(pt[:], xlnt[:, c * 128:(c + 1) * 128], k.identb[:])
            k.V.copy(xlnT[:, c, it * 128:(it + 1) * 128], pt[:])
    sc1.close()

    sc2 = ExitStack()
    sb2 = lambda name, shape, dtype: sc2.enter_context(nc.sbuf_tensor(name, list(shape), dtype))
    rqT = sb2("rqT", [128, 2, L], BF16)
    pooledT = sb2("pooledT", [128, 8, 64], BF16)
    rkvT = sb2("rkvT", [128, 4, 64], BF16)
    rvtok = sb2("rvtok", [64, 2, 130], BF16)
    wrq = sb2("wrq", [128, 8, 256], BF16)
    wrkv = sb2("wrkv", [128, 8, 512], BF16)
    platil = sb2("platil", [64, 512], F32)
    plat = sb2("plat", [64, 512], BF16)
    lat0 = sb2("lat0", [64, 512], BF16)
    sigtmp = sb2("sigtmp", [128, 512], F32)
    fbl = sb2("fbl", [64, 512], F32)
    masklatT = sb2("masklatTs", [64, L], F32)
    k.S.dma(masklatT[:], k.din["masklatT"])

    for c in range(8):
        k.V.red(pooledT[:, c, :].unsqueeze(2),
                xlnT[:, c, :].rearrange("p (a b) -> p a b", b=64), ALU.add)

    k.G.dma(wrq[:], k.din["rq_w"].rearrange("(c p) m -> p c m", p=128))
    for m in range(2):
        for t8 in range(8):
            ps = (ps1, ps2)[t8 % 2]
            for c in range(8):
                k.T.mm(ps[:], wrq[:, c, m * 128:(m + 1) * 128],
                       xlnT[:, c, t8 * 512:(t8 + 1) * 512], start=(c == 0), stop=(c == 7))
            k.A.act(rqT[:, m, t8 * 512:(t8 + 1) * 512], ps[:], AF.Identity,
                    bias=vcol(k, "rq_b", m))
    k.G.dma(wrkv[:], k.din["rkv_w"].rearrange("(c p) m -> p c m", p=128))
    for m in range(4):
        for c in range(8):
            k.T.mm(ps1[:, 0:64], wrkv[:, c, m * 128:(m + 1) * 128], pooledT[:, c, :],
                   start=(c == 0), stop=(c == 7))
        k.A.act(rkvT[:, m, :], ps1[:, 0:64], AF.Identity,
                bias=vcol(k, "rkv_b", m), scale=1.0 / 64)

    for j in range(2):
        pt = (pt1, pt2)[j % 2]
        k.T.transpose(pt[0:64, :], rkvT[:, 2 + j, :], k.identb[:])
        k.V.copy(rvtok[:, j, 0:64], pt[0:64, 0:64])
        k.V.copy(rvtok[:, j, 65:129], pt[0:64, 64:128])
        k.V.memset(rvtok[:, j, 64:65], 1.0)
        k.V.memset(rvtok[:, j, 129:130], 1.0)

    for h in range(4):
        rkh = rkvT[:, h // 2, :][(h % 2) * 64:(h % 2) * 64 + 64, :]
        rqh = rqT[:, h // 2, :][(h % 2) * 64:(h % 2) * 64 + 64, :]
        for t8 in range(8):
            q0 = t8 * 512
            k.T.mm(ps1[0:64, :], rkh, rqh[:, q0:q0 + 512], start=True, stop=True)
            k.V.tt(platil[:], ps1[0:64, :], masklatT[:, q0:q0 + 512], ALU.add)
            k.A.act(plat[:], platil[:], AF.Exp, scale=0.125)
            k.T.mm(pso[0:65, :], rvtok[:, h // 2, (h % 2) * 65:(h % 2) * 65 + 65],
                   plat[:], start=True, stop=True)
            k.V.ts(sigtmp[64:65, :], pso[64:65, :], 1e-30, None, ALU.add)
            k.V.recip(sigtmp[64:65, :], sigtmp[64:65, :])
            k.S.dma(k.bounce[0:1, q0:q0 + 512], sigtmp[64:65, :])
            bcast_row(k, k.bounce[0:1, q0:q0 + 512], fbl[:, :], 64)
            if h % 2 == 0:
                k.V.tt(latoutT[0:64, h // 2, q0:q0 + 512], pso[0:64, :], fbl[:, :], ALU.mult)
            else:
                k.V.tt(lat0[:], pso[0:64, :], fbl[:, :], ALU.mult)
                k.G.dma(latoutT[64:128, h // 2, q0:q0 + 512], lat0[:])
    sc2.close()

    sc3 = ExitStack()
    sb3 = lambda name, shape, dtype: sc3.enter_context(nc.sbuf_tensor(name, list(shape), dtype))
    qkvT = sb3("qkvT", [128, 3, L], BF16)
    wq3 = sb3("wq3", [128, 8, 384], BF16)
    wro = sb3("wro", [128, 2, 128], BF16)
    vtok = sb3("vtok", [128, 32, 130], BF16)
    ptil = sb3("ptil", [128, 256], F32)
    pexp = sb3("pexp", [128, 32, 256], BF16)
    locT = sb3("locT", [128, L], BF16)
    remT = sb3("remT", [128, L], BF16)
    remh0 = sb3("remh0", [64, L], BF16)
    fb = sb3("fb", [64, 2, 2048], BF16)
    mixhh = sb3("mixhh", [64, L], BF16)
    maskband = sb3("maskbands", [128, 256], F32)
    k.S.dma(maskband[:], k.din["maskband"])

    for hp in range(8):
        h0, h1 = 2 * hp, 2 * hp + 1
        dst_off = 0
        for (c0, w) in [(64 * h0, 64), (64 * h1, 64), (D + 64 * h0, 64),
                        (D + 64 * h1, 64), (2 * D + 64 * h0, 128)]:
            k.G.dma(wq3[:, :, dst_off:dst_off + w],
                    k.din["qkv_w"][:, c0:c0 + w].rearrange("(c p) m -> p c m", p=128))
            dst_off += w
        for m in range(3):
            for t8 in range(8):
                ps = (ps1, ps2)[t8 % 2]
                for c in range(8):
                    k.T.mm(ps[:], wq3[:, c, m * 128:(m + 1) * 128],
                           xlnT[:, c, t8 * 512:(t8 + 1) * 512], start=(c == 0), stop=(c == 7))
                k.A.act(qkvT[:, m, t8 * 512:(t8 + 1) * 512], ps[:], AF.Identity,
                        bias=vcol(k, f"qb{hp}_{m}"))
        k.G.dma(wro[:], k.din["rout_w"][:, hp * 128:(hp + 1) * 128].rearrange("(c p) m -> p c m", p=128))
        for t8 in range(8):
            ps = (ps1, ps2)[t8 % 2]
            for c in range(2):
                k.T.mm(ps[:], wro[:, c, :], latoutT[:, c, t8 * 512:(t8 + 1) * 512],
                       start=(c == 0), stop=(c == 1))
            k.A.act(remT[:, t8 * 512:(t8 + 1) * 512], ps[:], AF.Identity,
                    bias=vcol(k, "rout_b", hp))
        for kt in range(32):
            pt = (pt1, pt2)[kt % 2]
            k.T.transpose(pt[:], qkvT[:, 2, kt * 128:(kt + 1) * 128], k.identb[:])
            k.V.copy(vtok[:, kt, 0:64], pt[:, 0:64])
            k.V.copy(vtok[:, kt, 65:129], pt[:, 64:128])
        k.V.memset(vtok[:, :, 64:65], 1.0)
        k.V.memset(vtok[:, :, 129:130], 1.0)

        for hh in range(2):
            kh = qkvT[:, 1, :][hh * 64:hh * 64 + 64, :]
            qh = qkvT[:, 0, :][hh * 64:hh * 64 + 64, :]
            for kt in range(32):
                qw = min(256, L - kt * 128)
                k.T.mm(ps1[:, 0:qw], kh[:, kt * 128:(kt + 1) * 128],
                       qh[:, kt * 128:kt * 128 + qw], start=True, stop=True)
                k.V.tt(ptil[:, 0:qw], ps1[:, 0:qw], maskband[:, 0:qw], ALU.add)
                k.A.act(pexp[:, kt, 0:qw], ptil[:, 0:qw], AF.Exp, scale=0.125)
            for qt in range(32):
                first = True
                if qt > 0:
                    k.T.mm(pso[0:65, 0:128], vtok[:, qt - 1, hh * 65:hh * 65 + 65],
                           pexp[:, qt - 1, 128:256], start=True, stop=False)
                    first = False
                k.T.mm(pso[0:65, 0:128], vtok[:, qt, hh * 65:hh * 65 + 65],
                       pexp[:, qt, 0:128], start=first, stop=True)
                k.V.copy(locT[0:65, qt * 128:(qt + 1) * 128], pso[0:65, 0:128])

            lraw = locT[0:64, :]
            lsig = locT[64:65, :]
            if hh == 0:
                remh = remT[0:64, :]
            else:
                k.G.dma(remh0[:], remT[64:128, :])
                remh = remh0[:]
            for t8 in range(8):
                q0 = t8 * 512
                k.T.mm(pso[0:1, :], k.glcol[0:64, 0:1], lraw[:, q0:q0 + 512],
                       start=True, stop=True)
                k.V.copy(rowbuf[0:1, q0:q0 + 512], pso[0:1, :])
            k.S.dma(k.bounce[1:2, :], rowbuf[0:1, :])
            k.G.dma(k.rowtok[:, 0:32], k.bounce[1:2, :].rearrange("one (p c) -> (one p) c", p=128))
            for t8 in range(8):
                q0 = t8 * 512
                k.T.mm(pso[0:1, :], k.glcol[0:64, 1:2], remh[:, q0:q0 + 512],
                       start=True, stop=True)
                k.V.copy(rowbuf[0:1, q0:q0 + 512], pso[0:1, :])
            k.S.dma(k.bounce[2:3, :], rowbuf[0:1, :])
            k.G.dma(k.bounce[3:4, :], lsig)
            k.G.dma(k.rowtok[:, 32:64], k.bounce[2:3, :].rearrange("one (p c) -> (one p) c", p=128))
            k.G.dma(k.rowtok[:, 64:96], k.bounce[3:4, :].rearrange("one (p c) -> (one p) c", p=128))
            r = k.rowtok
            k.V.ts(r[:, 64:96], r[:, 64:96], 1e-30, None, ALU.add)
            k.V.recip(r[:, 64:96], r[:, 64:96])
            k.V.tt(r[:, 0:32], r[:, 0:32], r[:, 64:96], ALU.mult)
            k.V.tt(r[:, 0:32], r[:, 0:32], r[:, 32:64], ALU.add)
            k.A.act(r[:, 32:64], r[:, 0:32], AF.Tanh, scale=0.5)
            k.V.ts(r[:, 32:64], r[:, 32:64], 0.5, 0.5, ALU.mult, ALU.add)
            k.V.tt(r[:, 0:32], r[:, 32:64], r[:, 64:96], ALU.mult)
            k.V.ts(r[:, 32:64], r[:, 32:64], -1.0, 1.0, ALU.mult, ALU.add)
            k.S.dma(k.bounce[1:2, :].rearrange("one (p c) -> (one p) c", p=128), r[:, 0:32])
            k.S.dma(k.bounce[2:3, :].rearrange("one (p c) -> (one p) c", p=128), r[:, 32:64])
            for half in range(2):
                q0 = half * 2048
                bcast_row(k, k.bounce[1:2, q0:q0 + 2048], fb[:, 0, :], 64)
                bcast_row(k, k.bounce[2:3, q0:q0 + 2048], fb[:, 1, :], 64)
                k.V.tt(mixhh[:, q0:q0 + 2048], lraw[:, q0:q0 + 2048], fb[:, 0, :], ALU.mult)
                k.V.tt(fb[:, 1, :], fb[:, 1, :], remh[:, q0:q0 + 2048], ALU.mult)
                k.V.tt(mixhh[:, q0:q0 + 2048], mixhh[:, q0:q0 + 2048], fb[:, 1, :], ALU.add)
            k.S.dma(k.mixd[hp * 128 + hh * 64:hp * 128 + hh * 64 + 64, :], mixhh[:])
    sc3.close()

    sc4 = ExitStack()
    sb4 = lambda name, shape, dtype: sc4.enter_context(nc.sbuf_tensor(name, list(shape), dtype))
    wowf = sb4("wowf", [128, 8, 1024], BF16)
    mixs = sb4("mixs", [128, 8, 512], BF16)
    dt512 = sb4("dt512", [128, 512], BF16)
    stg = sb4("stg", [128, 4, 1024], BF16)
    xr = sb4("xr", [128, 1024], F32)

    k.G.dma(wowf[:], k.din["out_w"].rearrange("(c p) m -> p c m", p=128))
    for t8 in range(8):
        k.G.dma(mixs[:], k.mixd[:, t8 * 512:(t8 + 1) * 512].rearrange("(c p) m -> p c m", p=128))
        for oc in range(8):
            ps = (ps1, ps2)[oc % 2]
            for c in range(8):
                k.T.mm(ps[:], wowf[:, c, oc * 128:(oc + 1) * 128], mixs[:, c, :],
                       start=(c == 0), stop=(c == 7))
            k.A.act(dt512[:], ps[:], AF.Identity, bias=vcol(k, "out_b", oc))
            for j in range(4):
                pt = (pt1, pt2)[j % 2]
                k.T.transpose(pt[:], dt512[:, j * 128:(j + 1) * 128], k.identb[:])
                k.V.copy(stg[:, j, oc * 128:(oc + 1) * 128], pt[:])
        for j in range(4):
            it = t8 * 4 + j
            k.S.dma(xr[:], src[it * 128:(it + 1) * 128, :])
            k.V.tt(xr[:], xr[:], stg[:, j, :], ALU.add)
            k.S.dma(dst[it * 128:(it + 1) * 128, :], xr[:])
    sc4.close()
    ph.close()


def emit_ffn(k, src, dst, tag=""):
    nc = k.nc
    ph = ExitStack()
    sbt = lambda name, shape, dtype: ph.enter_context(nc.sbuf_tensor(name + tag, list(shape), dtype))
    pst = lambda name, shape, dtype=F32: ph.enter_context(nc.psum_tensor(name + tag, list(shape), dtype))

    xfT = sbt("xfT", [128, 8, 512], BF16)
    w0d = sbt("w0d", [128, 8, 4096], BF16)
    h1 = sbt("h1f", [128, 32, 512], BF16)
    h2 = sbt("h2f", [128, 32, 512], BF16)
    ka = sbt("kaf", [128, 4096], BF16)
    kb = sbt("kbf", [128, 4096], BF16)
    w1c = sbt("w1cf", [128, 2, 512], BF16)
    w2c = sbt("w2cf", [128, 2, 512], BF16)
    dfT = sbt("dfT", [128, 8, 512], BF16)
    stg = sbt("stgf", [128, 4, 1024], BF16)
    xr = sbt("xrf", [128, 1024], F32)
    xlnt = sbt("xlntf", [128, 1024], BF16)
    stat = sbt("statf", [128, 4], F32)

    pb = [pst(f"pbf{i}", [128, 512]) for i in range(4)]
    pt1 = pst("ptf1", [128, 128], BF16)
    pt2 = pst("ptf2", [128, 128], BF16)

    for c in range(8):
        k.G.dma(ka[:], k.din["k0"][c * 128:(c + 1) * 128, :])
        k.G.dma(kb[:], k.din["k0"][1024 + c * 128:1024 + (c + 1) * 128, :])
        k.V.tt(w0d[:, c, :], ka[:], kb[:], ALU.subtract)

    if not getattr(k, "kconv_done", False):
        k.kconv_done = True
        for c in range(32):
            kab = (ka, kb)[c % 2]
            k.G.dma(kab[:], k.din["k1"][c * 128:(c + 1) * 128, :])
            k.V.ts(kab[:], kab[:], 2.0, -1.0, ALU.mult, ALU.add)
            k.S.dma(k.k1d[c * 128:(c + 1) * 128, :], kab[:])
        for c in range(32):
            kab = (ka, kb)[c % 2]
            k.G.dma(kab[:, :1024], k.din["k2"][c * 128:(c + 1) * 128, :])
            k.V.ts(kab[:, :1024], kab[:, :1024], 2.0, -1.0, ALU.mult, ALU.add)
            k.S.dma(k.k2d[c * 128:(c + 1) * 128, :], kab[:, :1024])

    for tt in range(8):
        t0 = tt * 512
        for j in range(4):
            it = tt * 4 + j
            k.S.dma(xr[:], src[it * 128:(it + 1) * 128, :])
            ln_tile(k, xr[:], xlnt[:], stat)
            for c in range(8):
                pt = (pt1, pt2)[c % 2]
                k.T.transpose(pt[:], xlnt[:, c * 128:(c + 1) * 128], k.identb[:])
                k.V.copy(xfT[:, c, j * 128:(j + 1) * 128], pt[:])
        for f0 in range(32):
            ps = pb[f0 % 2]
            for c in range(8):
                k.T.mm(ps[:], w0d[:, c, f0 * 128:(f0 + 1) * 128], xfT[:, c, :],
                       start=(c == 0), stop=(c == 7))
            k.A.act(h1[:, f0, :], ps[:], AF.Relu, scale=vcol(k, "s0x4"))
        for f1g in range(8):
            for fc in range(32):
                wb = w1c[:, fc % 2, :]
                k.S.dma(wb, k.k1d[fc * 128:(fc + 1) * 128,
                                  f1g * 512:(f1g + 1) * 512])
                for f1 in range(4):
                    k.T.mm(pb[f1][:], wb[:, f1 * 128:(f1 + 1) * 128], h1[:, fc, :],
                           start=(fc == 0), stop=(fc == 31))
            for f1 in range(4):
                k.A.act(h2[:, f1g * 4 + f1, :], pb[f1][:], AF.Relu, scale=vcol(k, "s1x2"))
        for dgrp in range(2):
            for fc in range(32):
                wb = w2c[:, fc % 2, :]
                k.S.dma(wb, k.k2d[fc * 128:(fc + 1) * 128,
                                  dgrp * 512:(dgrp + 1) * 512])
                for do in range(4):
                    k.T.mm(pb[do][:], wb[:, do * 128:(do + 1) * 128], h2[:, fc, :],
                           start=(fc == 0), stop=(fc == 31))
            for do in range(4):
                k.A.act(dfT[:, dgrp * 4 + do, :], pb[do][:], AF.Identity,
                        scale=vcol(k, "s2"))
        for oc in range(8):
            for j in range(4):
                pt = (pt1, pt2)[j % 2]
                k.T.transpose(pt[:], dfT[:, oc, j * 128:(j + 1) * 128], k.identb[:])
                k.V.copy(stg[:, j, oc * 128:(oc + 1) * 128], pt[:])
        for j in range(4):
            it = tt * 4 + j
            k.S.dma(xr[:], src[it * 128:(it + 1) * 128, :])
            k.V.tt(xr[:], xr[:], stg[:, j, :], ALU.add)
            k.S.dma(dst[it * 128:(it + 1) * 128, :], xr[:])
    ph.close()


def emit_attn_b(k, src, dst):
    nc = k.nc
    ph = ExitStack()
    sbt = lambda name, shape, dtype: ph.enter_context(nc.sbuf_tensor(name, list(shape), dtype))
    pst = lambda name, shape, dtype=F32: ph.enter_context(nc.psum_tensor(name, list(shape), dtype))

    rstds = sbt("rstds", [128, 32], F32)
    pb = [pst(f"pbb{i}", [128, 512]) for i in range(4)]
    pt1 = pst("ptb1", [128, 128], BF16)
    pt2 = pst("ptb2", [128, 128], BF16)

    onecol = sbt("onecolb", [128, 1], BF16)
    k.V.memset(onecol[:], 1.0)

    scP = ExitStack()
    sbP = lambda name, shape, dtype: scP.enter_context(nc.sbuf_tensor(name, list(shape), dtype))
    xrP = sbP("xrP", [128, 1024], F32)
    uf = sbP("ufP", [128, 1024], F32)
    statP = sbP("statP", [128, 4], F32)
    xlntP = sbP("xlntP", [128, 1024], BF16)
    for it in range(32):
        k.S.dma(xrP[:], src[it * 128:(it + 1) * 128, :])
        ln_tile(k, xrP[:], xlntP[:], statP)
        k.V.ts(uf[:], xrP[:], statP[:, 2:3], None, ALU.mult)
        k.S.dma(k.xln2[it * 128:(it + 1) * 128, :], uf[:])
    scP.close()
    src = k.xln2

    scQ = ExitStack()
    sbQ = lambda name, shape, dtype: scQ.enter_context(nc.sbuf_tensor(name, list(shape), dtype))
    QT = sbQ("QTb", [128, 8, L], BF16)
    KT = sbQ("KTb", [128, 2, L], BF16)
    VT = sbQ("VTb", [128, 2, L], BF16)

    scA = ExitStack()
    sbA = lambda name, shape, dtype: scA.enter_context(nc.sbuf_tensor(name, list(shape), dtype))
    xaT = sbA("xaT", [128, 8, L], BF16)
    xr = sbA("xrb", [128, 1024], F32)
    xat = sbA("xat", [128, 1024], BF16)
    stat = sbA("statb", [128, 4], F32)
    for it in range(32):
        k.S.dma(xr[:], src[it * 128:(it + 1) * 128, :])
        rms_rstd(k, xr[:], stat)
        k.V.copy(rstds[:, it:it + 1], stat[:, 2:3])
        k.V.ts(xat[:], xr[:], stat[:, 2:3], None, ALU.mult)
        for c in range(8):
            pt = (pt1, pt2)[c % 2]
            k.T.transpose(pt[:], xat[:, c * 128:(c + 1) * 128], k.identb[:])
            k.V.copy(xaT[:, c, it * 128:(it + 1) * 128], pt[:])
    wq = sbA("wqb", [128, 8, 1024], BF16)
    k.G.dma(wq[:], k.din["Wq"].rearrange("(c p) m -> p c m", p=128))
    nw = sbA("nwb", [128, 8], F32)
    for c in range(8):
        k.S.dma(nw[:, c:c + 1], k.din["nattn_w"][c * 128:(c + 1) * 128].unsqueeze(1))
    for c in range(8):
        k.V.ts(wq[:, c, :], wq[:, c, :], nw[:, c:c + 1], None, ALU.mult)
    for oc in range(8):
        for t8 in range(8):
            ps = pb[t8 % 2]
            for c in range(8):
                k.T.mm(ps[:], wq[:, c, oc * 128:(oc + 1) * 128],
                       xaT[:, c, t8 * 512:(t8 + 1) * 512], start=(c == 0), stop=(c == 7))
            k.A.act(QT[:, oc, t8 * 512:(t8 + 1) * 512], ps[:], AF.Identity)
    wkv = sbA("wkvb", [128, 8, 512], BF16)
    k.G.dma(wkv[:, :, 0:256], k.din["Wk"].rearrange("(c p) m -> p c m", p=128))
    k.G.dma(wkv[:, :, 256:512], k.din["Wv"].rearrange("(c p) m -> p c m", p=128))
    for c in range(8):
        k.V.ts(wkv[:, c, :], wkv[:, c, :], nw[:, c:c + 1], None, ALU.mult)
    for oc in range(4):
        dstt = (KT, KT, VT, VT)[oc]
        for t8 in range(8):
            ps = pb[t8 % 2]
            for c in range(8):
                k.T.mm(ps[:], wkv[:, c, oc * 128:(oc + 1) * 128],
                       xaT[:, c, t8 * 512:(t8 + 1) * 512], start=(c == 0), stop=(c == 7))
            k.A.act(dstt[:, oc % 2, t8 * 512:(t8 + 1) * 512], ps[:], AF.Identity)
    scA.close()

    scC = ExitStack()
    sbC = lambda name, shape, dtype: scC.enter_context(nc.sbuf_tensor(name, list(shape), dtype))
    sqb = sbC("sqb", [128, L], BF16)
    rowb = sbC("rowbb", [1, L], F32)
    rsb = sbC("rsbc", [128, L], F32)
    rot = sbC("rotb", [128, L], BF16)
    cosT = sbC("cosTb", [128, L], BF16)
    sinT = sbC("sinTb", [128, L], BF16)
    qn2 = sbC("qn2", [128, 2], F32)
    k.S.dma(cosT[:], k.din["cosT"])
    k.S.dma(sinT[:], k.din["sinT"])
    k.S.dma(qn2[0:64, 0:1], k.din["qn_w"].unsqueeze(1))
    k.S.dma(qn2[64:128, 0:1], k.din["qn_w"].unsqueeze(1))
    k.S.dma(qn2[0:64, 1:2], k.din["kn_w"].unsqueeze(1))
    k.S.dma(qn2[64:128, 1:2], k.din["kn_w"].unsqueeze(1))

    def qknorm_rope(T, nchunks, gcol):
        for ch in range(nchunks):
            k.A.act(sqb[:], T[:, ch, :], AF.Square)
            for half in range(2):
                b = half * 64
                for p8 in range(8):
                    sl = slice(p8 * 512, (p8 + 1) * 512)
                    k.T.mm(pb[2][0:1, :], onecol[b:b + 64, 0:1], sqb[b:b + 64, sl],
                           start=True, stop=True)
                    k.A.act(rowb[0:1, sl], pb[2][0:1, :], AF.Ln, bias=1e-6, scale=1.0 / 64)
                k.A.act(rowb[0:1, :], rowb[0:1, :], AF.Exp, scale=-0.5)
                k.S.dma(k.bounce[4:5, :], rowb[0:1, :])
                k.G.dma(rsb[b:b + 64, :], k.bounce[4:5, :].partition_broadcast(64).squeeze(1))
                hsl = T[:, ch, :][b:b + 64, :]
                k.V.stt(hsl, hsl, qn2[b:b + 64, gcol:gcol + 1], rsb[b:b + 64, :],
                        ALU.mult, ALU.mult)
                k.S.dma(rot[b:b + 32, :], hsl[32:64, :])
                k.S.dma(rot[b + 32:b + 64, :], hsl[0:32, :])
                k.V.tt(rot[b:b + 64, :], rot[b:b + 64, :], sinT[b:b + 64, :], ALU.mult)
                k.V.tt(hsl, hsl, cosT[b:b + 64, :], ALU.mult)
                k.V.tt(hsl, hsl, rot[b:b + 64, :], ALU.add)

    qknorm_rope(QT, 8, 0)
    qknorm_rope(KT, 2, 1)
    scC.close()

    scD = ExitStack()
    sbD = lambda name, shape, dtype: scD.enter_context(nc.sbuf_tensor(name, list(shape), dtype))
    vtok = sbD("vtokb", [128, 2, 32, 130], BF16)
    KTd = sbD("KTdb", [128, 4, L], BF16)
    pexpb = sbD("pexpb", [128, 2, 512], BF16)
    sigt = sbD("sigtb", [128, 512], F32)
    rsb2 = sbD("rsb2", [64, 512], F32)
    anorm = sbD("anormb", [64, 512], BF16)
    for j in range(4):
        srcK = KT[:, j // 2, :][(j % 2) * 64:(j % 2) * 64 + 64, :]
        k.S.dma(KTd[0:64, j, :], srcK)
        k.S.dma(KTd[64:128, j, :], srcK)
    for pr in range(2):
        for kt in range(32):
            pt = (pt1, pt2)[kt % 2]
            k.T.transpose(pt[:], VT[:, pr, kt * 128:(kt + 1) * 128], k.identb[:])
            k.V.copy(vtok[:, pr, kt, 0:64], pt[:, 0:64])
            k.V.copy(vtok[:, pr, kt, 65:129], pt[:, 64:128])
        k.V.memset(vtok[:, pr, :, 64:65], 1.0)
        k.V.memset(vtok[:, pr, :, 129:130], 1.0)

    psO = pst("psOb", [128, 512])
    for h in range(16):
        kv = h // 4
        qch, qb = h // 2, (h % 2) * 64
        kch = kv // 2
        qhsl = QT[:, qch, :][qb:qb + 64, :]
        khsl = KTd[:, kv, :][qb:qb + 64, :]
        for qp in range(8):
            q0 = qp * 512
            for kt in range(32):
                psA = pb[kt % 2]
                k.T.mm(psA[:], khsl[:, kt * 128:(kt + 1) * 128], qhsl[:, q0:q0 + 512],
                       start=True, stop=True)
                pe = pexpb[:, kt % 2, :]
                k.A.act(pe, psA[:], AF.Exp, scale=0.125, bias=vcol(k, "Mb"))
                k.T.mm(psO[0:65, :], vtok[:, kch, kt, (kv % 2) * 65:(kv % 2) * 65 + 65],
                       pe, start=(kt == 0), stop=(kt == 31))
            k.V.ts(sigt[64:65, :], psO[64:65, :], 1e-30, None, ALU.add)
            k.V.recip(sigt[64:65, :], sigt[64:65, :])
            k.S.dma(k.bounce[4:5, q0:q0 + 512], sigt[64:65, :])
            k.G.dma(rsb2[:], k.bounce[4:5, q0:q0 + 512].partition_broadcast(64).squeeze(1))
            k.V.tt(anorm[:], psO[0:64, :], rsb2[:], ALU.mult)
            k.S.dma(k.mixd[h * 64:(h + 1) * 64, q0:q0 + 512], anorm[:])

    scE = ExitStack()
    sbE = lambda name, shape, dtype: scE.enter_context(nc.sbuf_tensor(name, list(shape), dtype))
    wo = sbE("wob2", [128, 8, 1024], BF16)
    mixs = sbE("mixsb", [128, 8, 512], BF16)
    dt512 = sbE("dt512b", [128, 512], BF16)
    stg = sbE("stgb", [128, 4, 1024], BF16)
    k.G.dma(wo[:], k.din["Wo"].rearrange("(c p) m -> p c m", p=128))
    for t8 in range(8):
        k.G.dma(mixs[:], k.mixd[:, t8 * 512:(t8 + 1) * 512].rearrange("(c p) m -> p c m", p=128))
        for oc in range(8):
            ps = pb[oc % 2]
            for c in range(8):
                k.T.mm(ps[:], wo[:, c, oc * 128:(oc + 1) * 128],
                       mixs[:, c, :], start=(c == 0), stop=(c == 7))
            k.A.act(dt512[:], ps[:], AF.Identity, scale=0.5)
            for j in range(4):
                pt = (pt1, pt2)[j % 2]
                k.T.transpose(pt[:], dt512[:, j * 128:(j + 1) * 128], k.identb[:])
                k.V.copy(stg[:, j, oc * 128:(oc + 1) * 128], pt[:])
        for j in range(4):
            it = t8 * 4 + j
            k.S.dma(k.attnd[it * 128:(it + 1) * 128, :], stg[:, j, :])
    scE.close()
    scD.close()
    scQ.close()

    scF = ExitStack()
    sbF = lambda name, shape, dtype: scF.enter_context(nc.sbuf_tensor(name, list(shape), dtype))
    xnT = sbF("xnT", [128, 8, L], BF16)
    dTT = sbF("dTT", [128, 8, L], BF16)
    aA = sbF("aAb", [128, 8, 128], F32)
    scFP = ExitStack()
    sbF = lambda name, shape, dtype: scFP.enter_context(nc.sbuf_tensor(name, list(shape), dtype))
    BT = sbF("BTb", [128, L], BF16)
    CT = sbF("CTb", [128, L], BF16)
    impT = sbF("impTb", [128, L], BF16)
    drT = sbF("drTb", [64, L], BF16)
    wssm = sbF("wssmb", [128, 1024], F32)
    bcast_row(k, k.din["ssmn_w"].unsqueeze(0), wssm[:], 128)
    xnt = sbF("xntb", [128, 1024], BF16)
    xrF = sbF("xrF", [128, 1024], F32)
    for it in range(32):
        k.S.dma(xrF[:], src[it * 128:(it + 1) * 128, :])
        k.V.stt(xnt[:], xrF[:], rstds[:, it:it + 1], wssm[:], ALU.mult, ALU.mult)
        for c in range(8):
            pt = (pt1, pt2)[c % 2]
            k.T.transpose(pt[:], xnt[:, c * 128:(c + 1) * 128], k.identb[:])
            k.V.copy(xnT[:, c, it * 128:(it + 1) * 128], pt[:])
    wx = sbF("wxb2", [128, 8, 320], BF16)
    k.G.dma(wx[:], k.din["Wx"].rearrange("(c p) m -> p c m", p=128))
    for t8 in range(8):
        sl = slice(t8 * 512, (t8 + 1) * 512)
        for c in range(8):
            k.T.mm(pb[0][0:64, :], wx[:, c, 0:64], xnT[:, c, sl], start=(c == 0), stop=(c == 7))
        k.A.act(drT[:, sl], pb[0][0:64, :], AF.Identity)
        for c in range(8):
            k.T.mm(pb[1][:], wx[:, c, 64:192], xnT[:, c, sl], start=(c == 0), stop=(c == 7))
        k.A.act(BT[:, sl], pb[1][:], AF.Identity)
        for c in range(8):
            k.T.mm(pb[2][:], wx[:, c, 192:320], xnT[:, c, sl], start=(c == 0), stop=(c == 7))
        k.A.act(CT[:, sl], pb[2][:], AF.Identity)
    wdt = sbF("wdtb", [64, 8, 128], BF16)
    k.G.dma(wdt[:], k.din["Wdt"].rearrange("r (c m) -> r c m", m=128))
    spl = sbF("splb", [128, 512], BF16)
    for oc in range(8):
        for t8 in range(8):
            sl = slice(t8 * 512, (t8 + 1) * 512)
            ps = pb[t8 % 2]
            k.T.mm(ps[:], wdt[:, oc, :], drT[:, sl], start=True, stop=True)
            k.A.act(spl[:], ps[:], AF.Abs, bias=vcol(k, "bdt", oc))
            k.A.act(spl[:], spl[:], AF.Exp, scale=-1.0)
            k.A.act(spl[:], spl[:], AF.Ln, bias=1.0)
            k.A.act(dTT[:, oc, sl], ps[:], AF.Relu, bias=vcol(k, "bdt", oc))
            k.V.tt(dTT[:, oc, sl], dTT[:, oc, sl], spl[:], ALU.add)
    wsel = sbF("wselb", [128, 8, 1024], BF16)
    k.G.dma(wsel[:], k.din["Wsel"].rearrange("(c p) m -> p c m", p=128))
    selc = sbF("selcb", [128, 512], BF16)
    for oc in range(8):
        for t8 in range(8):
            sl = slice(t8 * 512, (t8 + 1) * 512)
            ps = pb[2 + t8 % 2]
            for c in range(8):
                k.T.mm(ps[:], wsel[:, c, oc * 128:(oc + 1) * 128], xnT[:, c, sl],
                       start=(c == 0), stop=(c == 7))
            k.A.act(selc[:], ps[:], AF.Sigmoid, scale=vcol(k, "selg", oc),
                    bias=vcol(k, "bselg", oc))
            k.V.tt(xnT[:, oc, sl], xnT[:, oc, sl], selc[:], ALU.mult)
    wimp = sbF("wimpb", [128, 8, 128], BF16)
    k.G.dma(wimp[:], k.din["Wimp"].rearrange("(c p) m -> p c m", p=128))
    for t8 in range(8):
        sl = slice(t8 * 512, (t8 + 1) * 512)
        ps = pb[t8 % 2]
        for c in range(8):
            k.T.mm(ps[:], wimp[:, c, :], xnT[:, c, sl], start=(c == 0), stop=(c == 7))
        k.A.act(impT[:, sl], ps[:], AF.Sigmoid, bias=vcol(k, "bimp", 0))
    k.G.dma(wsel[:], k.din["Wg"].rearrange("(c p) m -> p c m", p=128))
    for oc in range(8):
        for t8 in range(8):
            sl = slice(t8 * 512, (t8 + 1) * 512)
            ps = pb[2 + t8 % 2]
            for c in range(8):
                k.T.mm(ps[:], wsel[:, c, oc * 128:(oc + 1) * 128], xnT[:, c, sl],
                       start=(c == 0), stop=(c == 7))
            k.A.act(selc[:], ps[:], AF.Sigmoid, bias=vcol(k, "bg", oc))
            k.V.tt(xnT[:, oc, sl], xnT[:, oc, sl], selc[:], ALU.mult)
    k.V.tt(BT[:], BT[:], impT[:], ALU.mult)
    k.V.tt(CT[:], CT[:], impT[:], ALU.mult)
    for c in range(8):
        k.V.tt(xnT[:, c, :], xnT[:, c, :], dTT[:, c, :], ALU.mult)
    k.S.dma(k.dramB[0:128, :], BT[:])
    k.S.dma(k.dramB[128:256, :], CT[:])
    k.G.dma(aA[:], k.din["A_log"].rearrange("(c p) n -> p c n", p=128))
    for c in range(8):
        k.A.act(aA[:, c, :], aA[:, c, :], AF.Exp)
        k.V.ts(aA[:, c, :], aA[:, c, :], -1.0, None, ALU.mult)
    scFP.close()

    scG = ExitStack()
    sbG = lambda name, shape, dtype: scG.enter_context(nc.sbuf_tensor(name, list(shape), dtype))
    SEG = 1024
    NSEG = L // SEG
    yacc = sbG("yaccb", [128, 8, SEG], BF16)
    carry = sbG("carryb", [128, 1024], F32)
    Bb = [sbG(f"Bbb{i}", [128, SEG], BF16) for i in range(2)]
    Cb = [sbG(f"Cbb{i}", [128, SEG], BF16) for i in range(2)]
    dab = sbG("dabb", [128, SEG], BF16)
    ub = sbG("ubb", [128, SEG], BF16)
    stile = sbG("stileb", [128, SEG], BF16)
    ztile = sbG("ztileb", [128, SEG], BF16)
    xd128 = sbG("xd128b", [128, 128], BF16)
    sd512 = sbG("sd512b", [128, 512], BF16)
    ssmstg = sbG("ssmstgb", [128, 4, 1024], BF16)
    xrG = sbG("xrG", [128, 1024], F32)
    atile = sbG("atileb", [128, 1024], BF16)
    dpb = sbG("dpbb", [128, 1024], F32)
    wout = sbG("woutb", [128, 8, 1024], BF16)
    bcast_row(k, k.din["Dp"].unsqueeze(0), dpb[:], 128)
    k.G.dma(wout[:], k.din["Wout"].rearrange("(c p) m -> p c m", p=128))
    k.V.memset(carry[:], 0.0)

    for seg in range(NSEG):
        s0 = seg * SEG
        for c in range(8):
            k.V.memset(yacc[:, c, :], 0.0)
        for n in range(128):
            bb, cb = Bb[n % 2], Cb[n % 2]
            k.S.dma(bb[:], k.dramB[n:n + 1, s0:s0 + SEG].partition_broadcast(128).squeeze(1))
            k.S.dma(cb[:], k.dramB[128 + n:129 + n, s0:s0 + SEG].partition_broadcast(128).squeeze(1))
            for c in range(8):
                col = n * 8 + c
                k.A.act(dab[:], dTT[:, c, s0:s0 + SEG], AF.Exp, scale=aA[:, c, n:n + 1])
                k.V.tt(ub[:], xnT[:, c, s0:s0 + SEG], bb[:], ALU.mult)
                k.V.scan(stile[:], dab[:], ub[:], carry[:, col:col + 1], ALU.mult, ALU.add)
                k.V.copy(carry[:, col:col + 1], stile[:, SEG - 1:SEG])
                k.V.tt(ztile[:], stile[:], cb[:], ALU.mult)
                k.V.tt(yacc[:, c, :], yacc[:, c, :], ztile[:], ALU.add)
        for j in range(SEG // 128):
            it = (s0 // 128) + j
            k.S.dma(xrG[:], src[it * 128:(it + 1) * 128, :])
            k.V.tt(atile[:], xrG[:], dpb[:], ALU.mult)
            for c in range(8):
                pt = (pt1, pt2)[c % 2]
                k.T.transpose(pt[:], atile[:, c * 128:(c + 1) * 128], k.identb[:])
                k.V.copy(xd128[:], pt[:])
                k.V.tt(yacc[:, c, j * 128:(j + 1) * 128],
                       yacc[:, c, j * 128:(j + 1) * 128], xd128[:], ALU.add)
        for t4 in range(SEG // 512):
            sl = slice(t4 * 512, (t4 + 1) * 512)
            for oc in range(8):
                ps = pb[oc % 2]
                for c in range(8):
                    k.T.mm(ps[:], wout[:, c, oc * 128:(oc + 1) * 128], yacc[:, c, sl],
                           start=(c == 0), stop=(c == 7))
                k.A.act(sd512[:], ps[:], AF.Identity, scale=0.5)
                for j in range(4):
                    pt = (pt1, pt2)[j % 2]
                    k.T.transpose(pt[:], sd512[:, j * 128:(j + 1) * 128], k.identb[:])
                    k.V.copy(ssmstg[:, j, oc * 128:(oc + 1) * 128], pt[:])
            for j in range(4):
                it = (s0 + t4 * 512) // 128 + j
                k.S.dma(xrG[:], src[it * 128:(it + 1) * 128, :])
                k.G.dma(atile[:], k.attnd[it * 128:(it + 1) * 128, :])
                k.V.tt(xrG[:], xrG[:], atile[:], ALU.add)
                k.V.tt(xrG[:], xrG[:], ssmstg[:, j, :], ALU.add)
                k.S.dma(k.xln2[it * 128:(it + 1) * 128, :], xrG[:])
    scG.close()
    scF.close()
    ph.close()


def emit_swiglu(k, src, src_outer, dst):
    nc = k.nc
    ph = ExitStack()
    sbt = lambda name, shape, dtype: ph.enter_context(nc.sbuf_tensor(name, list(shape), dtype))
    pst = lambda name, shape, dtype=F32: ph.enter_context(nc.psum_tensor(name, list(shape), dtype))

    xfT = sbt("xfTs", [128, 8, 512], BF16)
    hbl = sbt("hbls", [128, 32, 512], BF16)
    w2r = sbt("w2rs", [128, 32, 1024], BF16)
    w1a = sbt("w1as", [128, 8, 128], BF16)
    w1g = sbt("w1gs", [128, 8, 128], BF16)
    sa = sbt("sas", [128, 512], BF16)
    sg = sbt("sgs", [128, 512], BF16)
    dfT = sbt("dfTs", [128, 8, 512], BF16)
    stg = sbt("stgs", [128, 4, 1024], BF16)
    xr = sbt("xrs", [128, 1024], F32)
    xo = sbt("xos", [128, 1024], F32)
    xlnt = sbt("xlnts", [128, 1024], BF16)
    stat = sbt("stats", [128, 4], F32)
    wffn = sbt("wffns", [128, 1024], F32)
    bcast_row(k, k.din["nffn_w"].unsqueeze(0), wffn[:], 128)

    pb = [pst(f"pbs{i}", [128, 512]) for i in range(4)]
    pt1 = pst("pts1", [128, 128], BF16)
    pt2 = pst("pts2", [128, 128], BF16)

    k.G.dma(w2r[:], k.din["W2"].rearrange("(c p) m -> p c m", p=128))
    for tt in range(8):
        for j in range(4):
            it = tt * 4 + j
            k.S.dma(xr[:], src[it * 128:(it + 1) * 128, :])
            rms_rstd(k, xr[:], stat)
            k.V.stt(xlnt[:], xr[:], stat[:, 2:3], wffn[:], ALU.mult, ALU.mult)
            for c in range(8):
                pt = (pt1, pt2)[c % 2]
                k.T.transpose(pt[:], xlnt[:, c * 128:(c + 1) * 128], k.identb[:])
                k.V.copy(xfT[:, c, j * 128:(j + 1) * 128], pt[:])
        for fc in range(32):
            k.G.dma(w1a[:], k.din["W1"][:, fc * 128:(fc + 1) * 128].rearrange("(c p) m -> p c m", p=128))
            k.G.dma(w1g[:], k.din["W1"][:, 4096 + fc * 128:4096 + (fc + 1) * 128].rearrange("(c p) m -> p c m", p=128))
            for c in range(8):
                k.T.mm(pb[0][:], w1a[:, c, :], xfT[:, c, :], start=(c == 0), stop=(c == 7))
            for c in range(8):
                k.T.mm(pb[1][:], w1g[:, c, :], xfT[:, c, :], start=(c == 0), stop=(c == 7))
            k.A.act(sa[:], pb[0][:], AF.Silu)
            k.A.act(sg[:], pb[1][:], AF.Identity)
            k.V.tt(hbl[:, fc, :], sa[:], sg[:], ALU.mult)
        for dgrp in range(2):
            for fc in range(32):
                for do in range(4):
                    k.T.mm(pb[do][:], w2r[:, fc, dgrp * 512 + do * 128:dgrp * 512 + (do + 1) * 128],
                           hbl[:, fc, :], start=(fc == 0), stop=(fc == 31))
            for do in range(4):
                k.A.act(dfT[:, dgrp * 4 + do, :], pb[do][:], AF.Identity)
        for oc in range(8):
            for j in range(4):
                pt = (pt1, pt2)[j % 2]
                k.T.transpose(pt[:], dfT[:, oc, j * 128:(j + 1) * 128], k.identb[:])
                k.V.copy(stg[:, j, oc * 128:(oc + 1) * 128], pt[:])
        for j in range(4):
            it = tt * 4 + j
            k.S.dma(xr[:], src[it * 128:(it + 1) * 128, :])
            k.S.dma(xo[:], src_outer[it * 128:(it + 1) * 128, :])
            k.V.tt(xr[:], xr[:], stg[:, j, :], ALU.add)
            k.V.tt(xr[:], xr[:], xo[:], ALU.add)
            k.S.dma(dst[it * 128:(it + 1) * 128, :], xr[:])
    ph.close()



def _consts():
    c = {}
    c["ident_b"] = np.eye(128, dtype=ml_dtypes.bfloat16)
    inv = 1.0 / (10000.0 ** (np.arange(0, 64, 2, dtype=np.float64) / 64.0))
    fr = np.arange(L, dtype=np.float64)[:, None] * inv[None, :]
    emb = np.concatenate([fr, fr], axis=1)
    cos = np.cos(emb).T
    sin = np.sin(emb).T
    sinr = sin.copy()
    sinr[0:32, :] *= -1.0
    c["cosT"] = np.ascontiguousarray(np.concatenate([cos, cos], 0)).astype(ml_dtypes.bfloat16)
    c["sinT"] = np.ascontiguousarray(np.concatenate([sinr, sinr], 0)).astype(ml_dtypes.bfloat16)
    kp = np.arange(128)[:, None]
    qo = np.arange(256)[None, :]
    allowed = (qo - kp > 0) & (qo - kp <= 128)
    c["maskband"] = np.where(allowed, 0.0, NEG).astype(np.float32)
    lat = np.arange(64)[:, None]
    tq = np.arange(L)[None, :]
    c["masklatT"] = np.where(lat < (tq // 64), 0.0, NEG).astype(np.float32)
    return c


def _in_map(inputs, b):
    m = {
        "x": inputs["L0_attn_a_x"][b],
        "qkv_w": inputs["L0_attn_a_qkv_w"], "qkv_b": inputs["L0_attn_a_qkv_b"],
        "rq_w": inputs["L0_attn_a_rq_w"], "rq_b": inputs["L0_attn_a_rq_b"],
        "rkv_w": inputs["L0_attn_a_rkv_w"], "rkv_b": inputs["L0_attn_a_rkv_b"],
        "rout_w": inputs["L0_attn_a_rout_w"], "rout_b": inputs["L0_attn_a_rout_b"],
        "out_w": inputs["L0_attn_a_out_w"], "out_b": inputs["L0_attn_a_out_b"],
        "gl_w": inputs["L0_attn_a_gl_w"], "gr_w": inputs["L0_attn_a_gr_w"],
        "nattn_w": inputs["L0_attn_b_norm_attn_w"],
        "Wq": inputs["L0_attn_b_Wq"], "Wk": inputs["L0_attn_b_Wk"],
        "Wv": inputs["L0_attn_b_Wv"], "Wo": inputs["L0_attn_b_Wo"],
        "qn_w": inputs["L0_attn_b_qn_w"], "kn_w": inputs["L0_attn_b_kn_w"],
        "ssmn_w": inputs["L0_attn_b_ssm_norm_w"],
        "Wx": inputs["L0_attn_b_Wx"], "Wdt": inputs["L0_attn_b_Wdt"],
        "bdt": inputs["L0_attn_b_bdt"], "A_log": inputs["L0_attn_b_A_log"],
        "Dp": inputs["L0_attn_b_Dp"], "Wsel": inputs["L0_attn_b_Wsel"],
        "bsel": inputs["L0_attn_b_bsel"], "selg": inputs["L0_attn_b_sel_gate"],
        "Wimp": inputs["L0_attn_b_Wimp"], "bimp": inputs["L0_attn_b_bimp"],
        "Wg": inputs["L0_attn_b_Wg"], "bg": inputs["L0_attn_b_bg"],
        "Wout": inputs["L0_attn_b_Wout"], "nffn_w": inputs["L0_attn_b_norm_ffn_w"],
        "W1": inputs["L0_attn_b_W1"], "W2": inputs["L0_attn_b_W2"],
        "k0": np.asarray(inputs["L0_ffn_k0"]).view(np.uint8),
        "k1": np.asarray(inputs["L0_ffn_k1"]).view(np.uint8),
        "k2": np.asarray(inputs["L0_ffn_k2"]).view(np.uint8),
        "s0": np.asarray(inputs["L0_ffn_s0"]).reshape(1),
        "s1": np.asarray(inputs["L0_ffn_s1"]).reshape(1),
        "s2": np.asarray(inputs["L0_ffn_s2"]).reshape(1),
    }
    m = {n: np.ascontiguousarray(np.asarray(v)) for n, v in m.items()}
    m.update(_consts())
    return m


def run_upto(inputs, upto, cores=None):
    k = build_program(upto)
    cores = cores if cores is not None else list(range(8))
    in_maps = [_in_map(inputs, b) for b in cores]
    res = run_bass_kernel_spmd(k.nc, in_maps, core_ids=list(range(len(cores))))
    return np.stack([np.asarray(res.results[i]["out"]) for i in range(len(cores))], axis=0)


def kernel(**inputs):
    k = build_program("all")
    in_maps = [_in_map(inputs, b) for b in range(8)]
    res = run_bass_kernel_spmd(k.nc, in_maps, core_ids=list(range(8)))
    return np.stack([np.asarray(res.results[b]["out"]) for b in range(8)], axis=0)


# version 06
# speedup vs baseline: 1.0320x; 1.0044x over previous
import math
from contextlib import ExitStack

import numpy as np
import ml_dtypes
import concourse.bass as bass
import concourse.mybir as mybir
from concourse.bass_utils import run_bass_kernel_spmd

F32 = mybir.dt.float32
BF16 = mybir.dt.bfloat16
U8 = mybir.dt.uint8
AF = mybir.ActivationFunctionType
ALU = mybir.AluOpType
AX = mybir.AxisListType

L = 4096
D = 1024
NEG = -1e9


class Ctr:
    def __init__(self, h):
        self.h = h
        self.n = 0

    def inc(self, k=1):
        self.n += k
        return self.n


class K:
    pass


def _nm(ap):
    try:
        return ap.name()
    except Exception:
        return None


class Eng:

    def __init__(self, k, raw, ctr, is_pe=False):
        self.k = k
        self.raw = raw
        self.ctr = ctr
        self.is_pe = is_pe

    def _waits(self, outs, ins):
        k = self.k
        waits = {}

        def add(tok):
            if tok is None:
                return
            h, v = tok
            cur = waits.get(id(h))
            if cur is None or cur[1] < v:
                waits[id(h)] = (h, v)

        for ap in ins:
            add(k.lastw.get(_nm(ap)))
        for ap in outs:
            n = _nm(ap)
            add(k.lastw.get(n))
            for t in k.readers.get(n, ()):
                add(t)
        wm = k.watermark.setdefault(id(self.raw), {})
        for h, v in waits.values():
            if self.is_pe and h is k.sT.h:
                continue
            if wm.get(id(h), -1) >= v:
                continue
            self.raw.wait_ge(h, v)
            wm[id(h)] = v

    def _record(self, tok, outs, ins):
        k = self.k
        for ap in outs:
            n = _nm(ap)
            k.lastw[n] = tok
            k.readers[n] = []
        for ap in ins:
            k.readers.setdefault(_nm(ap), []).append(tok)

    def op(self, outs, ins, call, quiet=False, amt=1):
        self._waits(outs, ins)
        inst = call()
        if quiet:
            self.k.pending.setdefault(id(self), []).append((outs, ins))
            return None
        inst.then_inc(self.ctr.h, amt)
        tok = (self.ctr.h, self.ctr.inc(amt))
        for (o2, i2) in self.k.pending.pop(id(self), []):
            self._record(tok, o2, i2)
        self._record(tok, outs, ins)
        return tok

    def dma(self, out, in_):
        return self.op([out], [in_], lambda: self.raw.dma_start(out=out, in_=in_), amt=16)

    def tt(self, out, a, b, alu):
        return self.op([out], [a, b], lambda: self.raw.tensor_tensor(out, a, b, alu))

    def ts(self, out, a, s1, s2, op0, op1=None, accum=None):
        ins = [a] + [x for x in (s1, s2) if hasattr(x, "name")]
        outs = [out] + ([accum] if accum is not None else [])
        if op1 is None:
            return self.op(outs, ins, lambda: self.raw.tensor_scalar(
                out, a, s1, None, op0, accum_out=accum))
        return self.op(outs, ins, lambda: self.raw.tensor_scalar(
            out, a, s1, s2, op0, op1, accum_out=accum))

    def stt(self, out, a, s, b, op0, op1):
        ins = [a, b] + ([s] if hasattr(s, "name") else [])
        return self.op([out], ins, lambda: self.raw.scalar_tensor_tensor(out, a, s, b, op0, op1))

    def scan(self, out, d0, d1, init, op0, op1):
        ins = [d0, d1] + ([init] if hasattr(init, "name") else [])
        return self.op([out], ins, lambda: self.raw.tensor_tensor_scan(out, d0, d1, init, op0, op1))

    def red(self, out, a, alu, absval=False):
        return self.op([out], [a], lambda: self.raw.tensor_reduce(
            out, a, AX.X, alu, apply_absolute_value=absval))

    def copy(self, out, a):
        return self.op([out], [a], lambda: self.raw.tensor_copy(out, a))

    def memset(self, out, v):
        return self.op([out], [], lambda: self.raw.memset(out, v))

    def recip(self, out, a):
        return self.op([out], [a], lambda: self.raw.reciprocal(out, a))

    def act(self, out, in_, func, bias=0.0, scale=1.0, accum=None):
        ins = [in_] + [x for x in (bias, scale) if hasattr(x, "name")]
        outs = [out] + ([accum] if accum is not None else [])
        return self.op(outs, ins, lambda: self.raw.activation(
            out, in_, func, bias=bias, scale=scale, accum_out=accum))

    def mm(self, out, lhsT, rhs, start, stop):
        return self.op([out], [lhsT, rhs],
                       lambda: self.raw.matmul(out, lhsT, rhs, start=start, stop=stop),
                       quiet=not stop)

    def transpose(self, out, in_, ident):
        return self.op([out], [in_, ident], lambda: self.raw.transpose(out, in_, ident))

    def barrier(self, toks):
        wm = self.k.watermark.setdefault(id(self.raw), {})
        for t in toks:
            if t is None:
                continue
            h, v = t
            if wm.get(id(h), -1) >= v:
                continue
            self.raw.wait_ge(h, v)
            wm[id(h)] = v

    def wait_all(self, tensors):
        toks = []
        for ap in tensors:
            t = self.k.lastw.get(_nm(ap))
            if t:
                toks.append(t)
        self.barrier(toks)



def build_program(upto="all"):
    nc = bass.Bass("TRN2", target_bir_lowering=False, debug=False)
    k = K()
    k.nc = nc
    k.lastw = {}
    k.readers = {}
    k.watermark = {}
    k.pending = {}
    dt_in = {}

    def din(name, shape, dtype=F32):
        dt_in[name] = nc.dram_tensor(name, list(shape), dtype, kind="ExternalInput").ap()
        return dt_in[name]

    din("x", [L, D])
    din("qkv_w", [D, 3 * D]); din("qkv_b", [3 * D])
    din("rq_w", [D, 256]); din("rq_b", [256])
    din("rkv_w", [D, 512]); din("rkv_b", [512])
    din("rout_w", [256, D]); din("rout_b", [D])
    din("out_w", [D, D]); din("out_b", [D])
    din("gl_w", [64, 1]); din("gr_w", [64, 1])
    din("nattn_w", [D]); din("Wq", [D, D]); din("Wk", [D, 256]); din("Wv", [D, 256])
    din("Wo", [D, D]); din("qn_w", [64]); din("kn_w", [64]); din("ssmn_w", [D])
    din("Wx", [D, 320]); din("Wdt", [64, D]); din("bdt", [D]); din("A_log", [D, 128])
    din("Dp", [D]); din("Wsel", [D, D]); din("bsel", [D]); din("selg", [D])
    din("Wimp", [D, 128]); din("bimp", [128]); din("Wg", [D, D]); din("bg", [D])
    din("Wout", [D, D]); din("nffn_w", [D]); din("W1", [D, 8192]); din("W2", [4096, D])
    din("k0", [2048, 4096], U8); din("k1", [4096, 4096], U8); din("k2", [4096, D], U8)
    din("s0", [1]); din("s1", [1]); din("s2", [1])
    din("ident_b", [128, 128], BF16)
    din("cosT", [128, L], BF16); din("sinT", [128, L], BF16)
    din("maskband", [128, 256])
    din("masklatT", [64, L])
    k.din = dt_in

    k.out_d = nc.dram_tensor("out", [L, D], F32, kind="ExternalOutput").ap()
    k.xres = nc.dram_tensor("xres", [L, D], F32, kind="Internal").ap()
    k.attnd = nc.dram_tensor("attnd", [L, D], BF16, kind="Internal").ap()
    k.mixd = nc.dram_tensor("mixd", [D, L], BF16, kind="Internal").ap()
    k.bounce = nc.dram_tensor("bounce", [16, L], F32, kind="Internal").ap()
    k.dramB = nc.dram_tensor("dramB", [256, L], BF16, kind="Internal").ap()
    k.xln2 = nc.dram_tensor("xln2", [L, D], F32, kind="Internal").ap()
    k.k1d = nc.dram_tensor("k1d", [4096, 4096], BF16, kind="Internal").ap()
    k.k2d = nc.dram_tensor("k2d", [4096, 1024], BF16, kind="Internal").ap()

    stack = ExitStack()
    k.stack = stack
    k.sD = Ctr(stack.enter_context(nc.semaphore("sD")))
    k.sG = Ctr(stack.enter_context(nc.semaphore("sG")))
    k.sT = Ctr(stack.enter_context(nc.semaphore("sT")))
    k.sV = Ctr(stack.enter_context(nc.semaphore("sV")))
    k.sA = Ctr(stack.enter_context(nc.semaphore("sA")))

    sb = lambda name, shape, dtype: stack.enter_context(nc.sbuf_tensor(name, list(shape), dtype))
    k.sb = sb
    k.identb = sb("identb", [128, 128], BF16)
    k.vec = sb("vec", [128, 96], F32)
    k.vecmap = {}
    k.veccol = [0]
    k.glcol = sb("glcol", [128, 2], BF16)
    k.rowtok = sb("rowtok", [128, 160], F32)
    k.cpool = sb("cpool", [128, 16], F32)
    k.junk = sb("junkt", [128, 1024], BF16)

    k.blk = stack.enter_context(nc.Block())
    stack.enter_context(nc.allow_low_precision(
        reason="bf16 PE transposes and bf16 copies; matmul accumulation is fp32 PSUM"))

    k.T = Eng(k, nc.tensor, k.sT, is_pe=True)
    k.V = Eng(k, nc.vector, k.sV)
    k.A = Eng(k, nc.scalar, k.sA)
    k.S = Eng(k, nc.sync, k.sD)
    k.G = Eng(k, nc.gpsimd, k.sG)

    emit_setup(k)
    emit_attn_a(k, src=k.din["x"], dst=(k.out_d if upto == "a" else k.xres))
    if upto != "a":
        emit_ffn(k, src=k.xres, dst=(k.out_d if upto == "f1" else k.xres))
    if upto not in ("a", "f1"):
        emit_attn_b(k, src=k.xres, dst=None)
        emit_swiglu(k, src=k.xln2, src_outer=k.xres,
                    dst=(k.out_d if upto == "b" else k.xres))
    if upto not in ("a", "f1", "b"):
        emit_ffn(k, src=k.xres, dst=k.out_d, tag="B")
    k.S.wait_all([k.out_d])
    nc.sync.wait_ge(k.sD.h, k.sD.n)
    nc.sync.wait_ge(k.sG.h, k.sG.n)
    stack.close()
    return k



CONSTS = [0.0, 1.0, -1.0, 0.5, -0.5, 1e-5, 1e-6, 1e-30, 0.125, 1.0 / 64,
          1.0 / 1024, 2.0, 4.0, 0.25, -0.125, 1e30]


def alloc_vec(k, name, ncols):
    c0 = k.veccol[0]
    assert c0 + ncols <= 96, (name, c0)
    k.vecmap[name] = (c0, ncols)
    k.veccol[0] = c0 + ncols
    return c0


def vcol(k, name, j=0):
    c0, n = k.vecmap[name]
    assert j < n, (name, j, n)
    return k.vec[:, c0 + j:c0 + j + 1]


def load_vec_chunks(k, name, dram, F):
    nchunk = (F + 127) // 128
    c0 = alloc_vec(k, name, nchunk)
    for c in range(nchunk):
        n = min(128, F - c * 128)
        k.S.dma(k.vec[0:n, c0 + c:c0 + c + 1], dram[c * 128:c * 128 + n].unsqueeze(1))


def load_vec_pair(k, name, dram, off_top, off_bot):
    c0 = alloc_vec(k, name, 1)
    k.S.dma(k.vec[0:64, c0:c0 + 1], dram[off_top:off_top + 64].unsqueeze(1))
    k.S.dma(k.vec[64:128, c0:c0 + 1], dram[off_bot:off_bot + 64].unsqueeze(1))


def bcast_row(k, dram_row, sb_tile, n_part):
    return k.G.dma(sb_tile, dram_row.partition_broadcast(n_part).squeeze(1))


def emit_setup(k):
    nc = k.nc
    for i, v in enumerate(CONSTS):
        k.V.memset(k.cpool[:, i:i + 1], v)
        nc.const_aps.register(F32, v, k.cpool[:, i:i + 1])
    k.S.dma(k.identb[:], k.din["ident_b"])
    k.G.dma(k.glcol[0:64, 0:1], k.din["gl_w"])
    k.G.dma(k.glcol[0:64, 1:2], k.din["gr_w"])
    k.G.dma(k.glcol[64:128, 0:1], k.din["gl_w"])
    k.G.dma(k.glcol[64:128, 1:2], k.din["gr_w"])

    qkv_b = k.din["qkv_b"]
    for hp in range(8):
        h0, h1 = 2 * hp, 2 * hp + 1
        load_vec_pair(k, f"qb{hp}_0", qkv_b, 64 * h0, 64 * h1)
        load_vec_pair(k, f"qb{hp}_1", qkv_b, D + 64 * h0, D + 64 * h1)
        load_vec_pair(k, f"qb{hp}_2", qkv_b, 2 * D + 64 * h0, 2 * D + 64 * h1)
    load_vec_chunks(k, "rq_b", k.din["rq_b"], 256)
    load_vec_chunks(k, "rkv_b", k.din["rkv_b"], 512)
    load_vec_chunks(k, "rout_b", k.din["rout_b"], D)
    load_vec_chunks(k, "out_b", k.din["out_b"], D)
    load_vec_chunks(k, "bdt", k.din["bdt"], D)
    load_vec_chunks(k, "bsel", k.din["bsel"], D)
    load_vec_chunks(k, "selg", k.din["selg"], D)
    load_vec_chunks(k, "bg", k.din["bg"], D)
    load_vec_chunks(k, "bimp", k.din["bimp"], 128)
    for s in ("s0", "s1", "s2"):
        c0 = alloc_vec(k, s, 1)
        bcast_row(k, k.din[s].unsqueeze(0), k.vec[:, c0:c0 + 1], 128)
    c_bselg = alloc_vec(k, "bselg", 8)
    c_s0x4 = alloc_vec(k, "s0x4", 1)
    c_s1x2 = alloc_vec(k, "s1x2", 1)
    c_mb = alloc_vec(k, "Mb", 1)
    k.S.dma(k.rowtok[0:1, 96:160], k.din["qn_w"].unsqueeze(0))
    k.S.dma(k.rowtok[0:1, 32:96], k.din["kn_w"].unsqueeze(0))

    b0 = k.vecmap["bsel"][0]
    g0 = k.vecmap["selg"][0]
    k.V.tt(k.vec[:, c_bselg:c_bselg + 8], k.vec[:, b0:b0 + 8], k.vec[:, g0:g0 + 8], ALU.mult)
    k.V.ts(k.vec[:, c_s0x4:c_s0x4 + 1], vcol(k, "s0"), 4.0, None, ALU.mult)
    k.V.ts(k.vec[:, c_s1x2:c_s1x2 + 1], vcol(k, "s1"), 2.0, None, ALU.mult)
    r = k.rowtok
    k.V.red(r[0:1, 0:1], r[0:1, 96:160], ALU.max, absval=True)
    k.V.red(r[0:1, 1:2], r[0:1, 32:96], ALU.max, absval=True)
    k.V.tt(r[0:1, 2:3], r[0:1, 0:1], r[0:1, 1:2], ALU.mult)
    k.V.ts(r[0:1, 2:3], r[0:1, 2:3], 8.0, -40.0, ALU.mult, ALU.add)
    k.V.ts(r[0:1, 2:3], r[0:1, 2:3], 0.0, None, ALU.max)
    k.V.ts(r[0:1, 2:3], r[0:1, 2:3], -1.0, None, ALU.mult)
    k.S.dma(k.bounce[0:1, 0:1], r[0:1, 2:3])
    bcast_row(k, k.bounce[0:1, 0:1], k.vec[:, c_mb:c_mb + 1], 128)



def ln_tile(k, xb, xlnt, stat, eps=1e-5):
    k.V.red(stat[:, 0:1], xb, ALU.add)
    k.V.ts(stat[:, 3:4], stat[:, 0:1], 1.0 / D, None, ALU.mult)
    k.V.ts(xb, xb, stat[:, 3:4], None, ALU.subtract)
    k.A.act(k.junk[:], xb, AF.Square, accum=stat[:, 1:2])
    k.A.act(stat[:, 2:3], stat[:, 1:2], AF.Ln, bias=eps, scale=1.0 / D)
    k.A.act(stat[:, 2:3], stat[:, 2:3], AF.Exp, scale=-0.5)
    k.V.ts(xlnt, xb, stat[:, 2:3], None, ALU.mult)


def rms_rstd(k, xb, stat, eps=1e-6):
    k.A.act(k.junk[:], xb, AF.Square, accum=stat[:, 1:2])
    k.A.act(stat[:, 2:3], stat[:, 1:2], AF.Ln, bias=eps, scale=1.0 / D)
    k.A.act(stat[:, 2:3], stat[:, 2:3], AF.Exp, scale=-0.5)



def emit_attn_a(k, src, dst):
    nc = k.nc
    ph = ExitStack()
    sbt = lambda name, shape, dtype: ph.enter_context(nc.sbuf_tensor(name, list(shape), dtype))
    pst = lambda name, shape, dtype=F32: ph.enter_context(nc.psum_tensor(name, list(shape), dtype))

    xlnT = sbt("xlnT", [128, 8, L], BF16)
    latoutT = sbt("latoutT", [128, 2, L], BF16)
    rowbuf = sbt("rowbufa", [1, L], F32)

    ps1 = pst("ps1", [128, 512])
    ps2 = pst("ps2", [128, 512])
    pt1 = pst("pt1", [128, 128], BF16)
    pt2 = pst("pt2", [128, 128], BF16)
    pso = pst("pso", [128, 512])

    sc1 = ExitStack()
    sb1 = lambda name, shape, dtype: sc1.enter_context(nc.sbuf_tensor(name, list(shape), dtype))
    xta = sb1("xta", [128, 1024], F32)
    xtb = sb1("xtb", [128, 1024], F32)
    stat = sb1("stat", [128, 4], F32)
    xlnt = sb1("xlnt", [128, 1024], BF16)
    for it in range(32):
        xb = (xta, xtb)[it % 2][:]
        k.S.dma(xb, src[it * 128:(it + 1) * 128, :])
        ln_tile(k, xb, xlnt[:], stat)
        for c in range(8):
            pt = (pt1, pt2)[c % 2]
            k.T.transpose(pt[:], xlnt[:, c * 128:(c + 1) * 128], k.identb[:])
            k.V.copy(xlnT[:, c, it * 128:(it + 1) * 128], pt[:])
    sc1.close()

    sc2 = ExitStack()
    sb2 = lambda name, shape, dtype: sc2.enter_context(nc.sbuf_tensor(name, list(shape), dtype))
    rqT = sb2("rqT", [128, 2, L], BF16)
    pooledT = sb2("pooledT", [128, 8, 64], BF16)
    rkvT = sb2("rkvT", [128, 4, 64], BF16)
    rvtok = sb2("rvtok", [64, 2, 130], BF16)
    wrq = sb2("wrq", [128, 8, 256], BF16)
    wrkv = sb2("wrkv", [128, 8, 512], BF16)
    platil = sb2("platil", [64, 512], F32)
    plat = sb2("plat", [64, 512], BF16)
    lat0 = sb2("lat0", [64, 512], BF16)
    sigtmp = sb2("sigtmp", [128, 512], F32)
    fbl = sb2("fbl", [64, 512], F32)
    masklatT = sb2("masklatTs", [64, L], F32)
    k.S.dma(masklatT[:], k.din["masklatT"])

    for c in range(8):
        k.V.red(pooledT[:, c, :].unsqueeze(2),
                xlnT[:, c, :].rearrange("p (a b) -> p a b", b=64), ALU.add)

    k.G.dma(wrq[:], k.din["rq_w"].rearrange("(c p) m -> p c m", p=128))
    for m in range(2):
        for t8 in range(8):
            ps = (ps1, ps2)[t8 % 2]
            for c in range(8):
                k.T.mm(ps[:], wrq[:, c, m * 128:(m + 1) * 128],
                       xlnT[:, c, t8 * 512:(t8 + 1) * 512], start=(c == 0), stop=(c == 7))
            k.A.act(rqT[:, m, t8 * 512:(t8 + 1) * 512], ps[:], AF.Identity,
                    bias=vcol(k, "rq_b", m))
    k.G.dma(wrkv[:], k.din["rkv_w"].rearrange("(c p) m -> p c m", p=128))
    for m in range(4):
        for c in range(8):
            k.T.mm(ps1[:, 0:64], wrkv[:, c, m * 128:(m + 1) * 128], pooledT[:, c, :],
                   start=(c == 0), stop=(c == 7))
        k.A.act(rkvT[:, m, :], ps1[:, 0:64], AF.Identity,
                bias=vcol(k, "rkv_b", m), scale=1.0 / 64)

    for j in range(2):
        pt = (pt1, pt2)[j % 2]
        k.T.transpose(pt[0:64, :], rkvT[:, 2 + j, :], k.identb[:])
        k.V.copy(rvtok[:, j, 0:64], pt[0:64, 0:64])
        k.V.copy(rvtok[:, j, 65:129], pt[0:64, 64:128])
        k.V.memset(rvtok[:, j, 64:65], 1.0)
        k.V.memset(rvtok[:, j, 129:130], 1.0)

    for h in range(4):
        rkh = rkvT[:, h // 2, :][(h % 2) * 64:(h % 2) * 64 + 64, :]
        rqh = rqT[:, h // 2, :][(h % 2) * 64:(h % 2) * 64 + 64, :]
        for t8 in range(8):
            q0 = t8 * 512
            k.T.mm(ps1[0:64, :], rkh, rqh[:, q0:q0 + 512], start=True, stop=True)
            k.V.tt(platil[:], ps1[0:64, :], masklatT[:, q0:q0 + 512], ALU.add)
            k.A.act(plat[:], platil[:], AF.Exp, scale=0.125)
            k.T.mm(pso[0:65, :], rvtok[:, h // 2, (h % 2) * 65:(h % 2) * 65 + 65],
                   plat[:], start=True, stop=True)
            k.V.ts(sigtmp[64:65, :], pso[64:65, :], 1e-30, None, ALU.add)
            k.V.recip(sigtmp[64:65, :], sigtmp[64:65, :])
            k.S.dma(k.bounce[0:1, q0:q0 + 512], sigtmp[64:65, :])
            bcast_row(k, k.bounce[0:1, q0:q0 + 512], fbl[:, :], 64)
            if h % 2 == 0:
                k.V.tt(latoutT[0:64, h // 2, q0:q0 + 512], pso[0:64, :], fbl[:, :], ALU.mult)
            else:
                k.V.tt(lat0[:], pso[0:64, :], fbl[:, :], ALU.mult)
                k.G.dma(latoutT[64:128, h // 2, q0:q0 + 512], lat0[:])
    sc2.close()

    sc3 = ExitStack()
    sb3 = lambda name, shape, dtype: sc3.enter_context(nc.sbuf_tensor(name, list(shape), dtype))
    qkvT = sb3("qkvT", [128, 3, L], BF16)
    wq3 = sb3("wq3", [128, 8, 384], BF16)
    wro = sb3("wro", [128, 2, 128], BF16)
    vtok = sb3("vtok", [128, 32, 130], BF16)
    ptil = sb3("ptil", [128, 256], F32)
    pexp = sb3("pexp", [128, 32, 256], BF16)
    locT = sb3("locT", [128, L], BF16)
    remT = sb3("remT", [128, L], BF16)
    remh0 = sb3("remh0", [64, L], BF16)
    fb = sb3("fb", [64, 2, 2048], BF16)
    mixhh = sb3("mixhh", [64, L], BF16)
    maskband = sb3("maskbands", [128, 256], F32)
    k.S.dma(maskband[:], k.din["maskband"])

    for hp in range(8):
        h0, h1 = 2 * hp, 2 * hp + 1
        dst_off = 0
        for (c0, w) in [(64 * h0, 64), (64 * h1, 64), (D + 64 * h0, 64),
                        (D + 64 * h1, 64), (2 * D + 64 * h0, 128)]:
            k.G.dma(wq3[:, :, dst_off:dst_off + w],
                    k.din["qkv_w"][:, c0:c0 + w].rearrange("(c p) m -> p c m", p=128))
            dst_off += w
        for m in range(3):
            for t8 in range(8):
                ps = (ps1, ps2)[t8 % 2]
                for c in range(8):
                    k.T.mm(ps[:], wq3[:, c, m * 128:(m + 1) * 128],
                           xlnT[:, c, t8 * 512:(t8 + 1) * 512], start=(c == 0), stop=(c == 7))
                k.A.act(qkvT[:, m, t8 * 512:(t8 + 1) * 512], ps[:], AF.Identity,
                        bias=vcol(k, f"qb{hp}_{m}"))
        k.G.dma(wro[:], k.din["rout_w"][:, hp * 128:(hp + 1) * 128].rearrange("(c p) m -> p c m", p=128))
        for t8 in range(8):
            ps = (ps1, ps2)[t8 % 2]
            for c in range(2):
                k.T.mm(ps[:], wro[:, c, :], latoutT[:, c, t8 * 512:(t8 + 1) * 512],
                       start=(c == 0), stop=(c == 1))
            k.A.act(remT[:, t8 * 512:(t8 + 1) * 512], ps[:], AF.Identity,
                    bias=vcol(k, "rout_b", hp))
        for kt in range(32):
            pt = (pt1, pt2)[kt % 2]
            k.T.transpose(pt[:], qkvT[:, 2, kt * 128:(kt + 1) * 128], k.identb[:])
            k.V.copy(vtok[:, kt, 0:64], pt[:, 0:64])
            k.V.copy(vtok[:, kt, 65:129], pt[:, 64:128])
        k.V.memset(vtok[:, :, 64:65], 1.0)
        k.V.memset(vtok[:, :, 129:130], 1.0)

        for hh in range(2):
            kh = qkvT[:, 1, :][hh * 64:hh * 64 + 64, :]
            qh = qkvT[:, 0, :][hh * 64:hh * 64 + 64, :]
            for kt in range(32):
                qw = min(256, L - kt * 128)
                k.T.mm(ps1[:, 0:qw], kh[:, kt * 128:(kt + 1) * 128],
                       qh[:, kt * 128:kt * 128 + qw], start=True, stop=True)
                k.V.tt(ptil[:, 0:qw], ps1[:, 0:qw], maskband[:, 0:qw], ALU.add)
                k.A.act(pexp[:, kt, 0:qw], ptil[:, 0:qw], AF.Exp, scale=0.125)
            for qt in range(32):
                first = True
                if qt > 0:
                    k.T.mm(pso[0:65, 0:128], vtok[:, qt - 1, hh * 65:hh * 65 + 65],
                           pexp[:, qt - 1, 128:256], start=True, stop=False)
                    first = False
                k.T.mm(pso[0:65, 0:128], vtok[:, qt, hh * 65:hh * 65 + 65],
                       pexp[:, qt, 0:128], start=first, stop=True)
                k.V.copy(locT[0:65, qt * 128:(qt + 1) * 128], pso[0:65, 0:128])

            lraw = locT[0:64, :]
            lsig = locT[64:65, :]
            if hh == 0:
                remh = remT[0:64, :]
            else:
                k.G.dma(remh0[:], remT[64:128, :])
                remh = remh0[:]
            for t8 in range(8):
                q0 = t8 * 512
                k.T.mm(pso[0:1, :], k.glcol[0:64, 0:1], lraw[:, q0:q0 + 512],
                       start=True, stop=True)
                k.V.copy(rowbuf[0:1, q0:q0 + 512], pso[0:1, :])
            k.S.dma(k.bounce[1:2, :], rowbuf[0:1, :])
            k.G.dma(k.rowtok[:, 0:32], k.bounce[1:2, :].rearrange("one (p c) -> (one p) c", p=128))
            for t8 in range(8):
                q0 = t8 * 512
                k.T.mm(pso[0:1, :], k.glcol[0:64, 1:2], remh[:, q0:q0 + 512],
                       start=True, stop=True)
                k.V.copy(rowbuf[0:1, q0:q0 + 512], pso[0:1, :])
            k.S.dma(k.bounce[2:3, :], rowbuf[0:1, :])
            k.G.dma(k.bounce[3:4, :], lsig)
            k.G.dma(k.rowtok[:, 32:64], k.bounce[2:3, :].rearrange("one (p c) -> (one p) c", p=128))
            k.G.dma(k.rowtok[:, 64:96], k.bounce[3:4, :].rearrange("one (p c) -> (one p) c", p=128))
            r = k.rowtok
            k.V.ts(r[:, 64:96], r[:, 64:96], 1e-30, None, ALU.add)
            k.V.recip(r[:, 64:96], r[:, 64:96])
            k.V.tt(r[:, 0:32], r[:, 0:32], r[:, 64:96], ALU.mult)
            k.V.tt(r[:, 0:32], r[:, 0:32], r[:, 32:64], ALU.add)
            k.A.act(r[:, 32:64], r[:, 0:32], AF.Tanh, scale=0.5)
            k.V.ts(r[:, 32:64], r[:, 32:64], 0.5, 0.5, ALU.mult, ALU.add)
            k.V.tt(r[:, 0:32], r[:, 32:64], r[:, 64:96], ALU.mult)
            k.V.ts(r[:, 32:64], r[:, 32:64], -1.0, 1.0, ALU.mult, ALU.add)
            k.S.dma(k.bounce[1:2, :].rearrange("one (p c) -> (one p) c", p=128), r[:, 0:32])
            k.S.dma(k.bounce[2:3, :].rearrange("one (p c) -> (one p) c", p=128), r[:, 32:64])
            for half in range(2):
                q0 = half * 2048
                bcast_row(k, k.bounce[1:2, q0:q0 + 2048], fb[:, 0, :], 64)
                bcast_row(k, k.bounce[2:3, q0:q0 + 2048], fb[:, 1, :], 64)
                k.V.tt(mixhh[:, q0:q0 + 2048], lraw[:, q0:q0 + 2048], fb[:, 0, :], ALU.mult)
                k.V.tt(fb[:, 1, :], fb[:, 1, :], remh[:, q0:q0 + 2048], ALU.mult)
                k.V.tt(mixhh[:, q0:q0 + 2048], mixhh[:, q0:q0 + 2048], fb[:, 1, :], ALU.add)
            k.S.dma(k.mixd[hp * 128 + hh * 64:hp * 128 + hh * 64 + 64, :], mixhh[:])
    sc3.close()

    sc4 = ExitStack()
    sb4 = lambda name, shape, dtype: sc4.enter_context(nc.sbuf_tensor(name, list(shape), dtype))
    wowf = sb4("wowf", [128, 8, 1024], BF16)
    mixs = sb4("mixs", [128, 8, 512], BF16)
    dt512 = sb4("dt512", [128, 512], BF16)
    stg = sb4("stg", [128, 4, 1024], BF16)
    xr = sb4("xr", [128, 1024], F32)

    k.G.dma(wowf[:], k.din["out_w"].rearrange("(c p) m -> p c m", p=128))
    for t8 in range(8):
        k.G.dma(mixs[:], k.mixd[:, t8 * 512:(t8 + 1) * 512].rearrange("(c p) m -> p c m", p=128))
        for oc in range(8):
            ps = (ps1, ps2)[oc % 2]
            for c in range(8):
                k.T.mm(ps[:], wowf[:, c, oc * 128:(oc + 1) * 128], mixs[:, c, :],
                       start=(c == 0), stop=(c == 7))
            k.A.act(dt512[:], ps[:], AF.Identity, bias=vcol(k, "out_b", oc))
            for j in range(4):
                pt = (pt1, pt2)[j % 2]
                k.T.transpose(pt[:], dt512[:, j * 128:(j + 1) * 128], k.identb[:])
                k.V.copy(stg[:, j, oc * 128:(oc + 1) * 128], pt[:])
        for j in range(4):
            it = t8 * 4 + j
            k.S.dma(xr[:], src[it * 128:(it + 1) * 128, :])
            k.V.tt(xr[:], xr[:], stg[:, j, :], ALU.add)
            k.S.dma(dst[it * 128:(it + 1) * 128, :], xr[:])
    sc4.close()
    ph.close()


def emit_ffn(k, src, dst, tag=""):
    nc = k.nc
    ph = ExitStack()
    sbt = lambda name, shape, dtype: ph.enter_context(nc.sbuf_tensor(name + tag, list(shape), dtype))
    pst = lambda name, shape, dtype=F32: ph.enter_context(nc.psum_tensor(name + tag, list(shape), dtype))

    xfT = sbt("xfT", [128, 8, 512], BF16)
    w0d = sbt("w0d", [128, 8, 4096], BF16)
    h1 = sbt("h1f", [128, 32, 512], BF16)
    h2 = sbt("h2f", [128, 32, 512], BF16)
    ka = sbt("kaf", [128, 4096], BF16)
    kb = sbt("kbf", [128, 4096], BF16)
    w1c = sbt("w1cf", [128, 2, 512], BF16)
    w2c = sbt("w2cf", [128, 2, 512], BF16)
    dfT = sbt("dfT", [128, 8, 512], BF16)
    stg = sbt("stgf", [128, 4, 1024], BF16)
    xr = sbt("xrf", [128, 1024], F32)
    xlnt = sbt("xlntf", [128, 1024], BF16)
    stat = sbt("statf", [128, 4], F32)

    pb = [pst(f"pbf{i}", [128, 512]) for i in range(4)]
    pt1 = pst("ptf1", [128, 128], BF16)
    pt2 = pst("ptf2", [128, 128], BF16)

    for c in range(8):
        k.G.dma(ka[:], k.din["k0"][c * 128:(c + 1) * 128, :])
        k.G.dma(kb[:], k.din["k0"][1024 + c * 128:1024 + (c + 1) * 128, :])
        k.V.tt(w0d[:, c, :], ka[:], kb[:], ALU.subtract)

    if not getattr(k, "kconv_done", False):
        k.kconv_done = True
        for c in range(32):
            kab = (ka, kb)[c % 2]
            k.G.dma(kab[:], k.din["k1"][c * 128:(c + 1) * 128, :])
            k.V.ts(kab[:], kab[:], 2.0, -1.0, ALU.mult, ALU.add)
            k.S.dma(k.k1d[c * 128:(c + 1) * 128, :], kab[:])
        for c in range(32):
            kab = (ka, kb)[c % 2]
            k.G.dma(kab[:, :1024], k.din["k2"][c * 128:(c + 1) * 128, :])
            k.V.ts(kab[:, :1024], kab[:, :1024], 2.0, -1.0, ALU.mult, ALU.add)
            k.S.dma(k.k2d[c * 128:(c + 1) * 128, :], kab[:, :1024])

    for tt in range(8):
        t0 = tt * 512
        for j in range(4):
            it = tt * 4 + j
            k.S.dma(xr[:], src[it * 128:(it + 1) * 128, :])
            ln_tile(k, xr[:], xlnt[:], stat)
            for c in range(8):
                pt = (pt1, pt2)[c % 2]
                k.T.transpose(pt[:], xlnt[:, c * 128:(c + 1) * 128], k.identb[:])
                k.V.copy(xfT[:, c, j * 128:(j + 1) * 128], pt[:])
        for f0 in range(32):
            ps = pb[f0 % 2]
            for c in range(8):
                k.T.mm(ps[:], w0d[:, c, f0 * 128:(f0 + 1) * 128], xfT[:, c, :],
                       start=(c == 0), stop=(c == 7))
            k.A.act(h1[:, f0, :], ps[:], AF.Relu, scale=vcol(k, "s0x4"))
        for f1g in range(8):
            for fc in range(32):
                wb = w1c[:, fc % 2, :]
                k.S.dma(wb, k.k1d[fc * 128:(fc + 1) * 128,
                                  f1g * 512:(f1g + 1) * 512])
                for f1 in range(4):
                    k.T.mm(pb[f1][:], wb[:, f1 * 128:(f1 + 1) * 128], h1[:, fc, :],
                           start=(fc == 0), stop=(fc == 31))
            for f1 in range(4):
                k.A.act(h2[:, f1g * 4 + f1, :], pb[f1][:], AF.Relu, scale=vcol(k, "s1x2"))
        for dgrp in range(2):
            for fc in range(32):
                wb = w2c[:, fc % 2, :]
                k.S.dma(wb, k.k2d[fc * 128:(fc + 1) * 128,
                                  dgrp * 512:(dgrp + 1) * 512])
                for do in range(4):
                    k.T.mm(pb[do][:], wb[:, do * 128:(do + 1) * 128], h2[:, fc, :],
                           start=(fc == 0), stop=(fc == 31))
            for do in range(4):
                k.A.act(dfT[:, dgrp * 4 + do, :], pb[do][:], AF.Identity,
                        scale=vcol(k, "s2"))
        for oc in range(8):
            for j in range(4):
                pt = (pt1, pt2)[j % 2]
                k.T.transpose(pt[:], dfT[:, oc, j * 128:(j + 1) * 128], k.identb[:])
                k.V.copy(stg[:, j, oc * 128:(oc + 1) * 128], pt[:])
        for j in range(4):
            it = tt * 4 + j
            k.S.dma(xr[:], src[it * 128:(it + 1) * 128, :])
            k.V.tt(xr[:], xr[:], stg[:, j, :], ALU.add)
            k.S.dma(dst[it * 128:(it + 1) * 128, :], xr[:])
    ph.close()


def emit_attn_b(k, src, dst):
    nc = k.nc
    ph = ExitStack()
    sbt = lambda name, shape, dtype: ph.enter_context(nc.sbuf_tensor(name, list(shape), dtype))
    pst = lambda name, shape, dtype=F32: ph.enter_context(nc.psum_tensor(name, list(shape), dtype))

    rstds = sbt("rstds", [128, 32], F32)
    pb = [pst(f"pbb{i}", [128, 512]) for i in range(4)]
    pt1 = pst("ptb1", [128, 128], BF16)
    pt2 = pst("ptb2", [128, 128], BF16)

    onecol = sbt("onecolb", [128, 1], BF16)
    k.V.memset(onecol[:], 1.0)

    scP = ExitStack()
    sbP = lambda name, shape, dtype: scP.enter_context(nc.sbuf_tensor(name, list(shape), dtype))
    xrP = sbP("xrP", [128, 1024], F32)
    uf = sbP("ufP", [128, 1024], F32)
    statP = sbP("statP", [128, 4], F32)
    xlntP = sbP("xlntP", [128, 1024], BF16)
    for it in range(32):
        k.S.dma(xrP[:], src[it * 128:(it + 1) * 128, :])
        ln_tile(k, xrP[:], xlntP[:], statP)
        k.V.ts(uf[:], xrP[:], statP[:, 2:3], None, ALU.mult)
        k.S.dma(k.xln2[it * 128:(it + 1) * 128, :], uf[:])
    scP.close()
    src = k.xln2

    scQ = ExitStack()
    sbQ = lambda name, shape, dtype: scQ.enter_context(nc.sbuf_tensor(name, list(shape), dtype))
    QT = sbQ("QTb", [128, 8, L], BF16)
    KT = sbQ("KTb", [128, 2, L], BF16)
    VT = sbQ("VTb", [128, 2, L], BF16)

    scA = ExitStack()
    sbA = lambda name, shape, dtype: scA.enter_context(nc.sbuf_tensor(name, list(shape), dtype))
    xaT = sbA("xaT", [128, 8, L], BF16)
    xr = sbA("xrb", [128, 1024], F32)
    xat = sbA("xat", [128, 1024], BF16)
    stat = sbA("statb", [128, 4], F32)
    for it in range(32):
        k.S.dma(xr[:], src[it * 128:(it + 1) * 128, :])
        rms_rstd(k, xr[:], stat)
        k.V.copy(rstds[:, it:it + 1], stat[:, 2:3])
        k.V.ts(xat[:], xr[:], stat[:, 2:3], None, ALU.mult)
        for c in range(8):
            pt = (pt1, pt2)[c % 2]
            k.T.transpose(pt[:], xat[:, c * 128:(c + 1) * 128], k.identb[:])
            k.V.copy(xaT[:, c, it * 128:(it + 1) * 128], pt[:])
    wq = sbA("wqb", [128, 8, 1024], BF16)
    k.G.dma(wq[:], k.din["Wq"].rearrange("(c p) m -> p c m", p=128))
    nw = sbA("nwb", [128, 8], F32)
    for c in range(8):
        k.S.dma(nw[:, c:c + 1], k.din["nattn_w"][c * 128:(c + 1) * 128].unsqueeze(1))
    for c in range(8):
        k.V.ts(wq[:, c, :], wq[:, c, :], nw[:, c:c + 1], None, ALU.mult)
    for oc in range(8):
        for t8 in range(8):
            ps = pb[t8 % 2]
            for c in range(8):
                k.T.mm(ps[:], wq[:, c, oc * 128:(oc + 1) * 128],
                       xaT[:, c, t8 * 512:(t8 + 1) * 512], start=(c == 0), stop=(c == 7))
            k.A.act(QT[:, oc, t8 * 512:(t8 + 1) * 512], ps[:], AF.Identity)
    wkv = sbA("wkvb", [128, 8, 512], BF16)
    k.G.dma(wkv[:, :, 0:256], k.din["Wk"].rearrange("(c p) m -> p c m", p=128))
    k.G.dma(wkv[:, :, 256:512], k.din["Wv"].rearrange("(c p) m -> p c m", p=128))
    for c in range(8):
        k.V.ts(wkv[:, c, :], wkv[:, c, :], nw[:, c:c + 1], None, ALU.mult)
    for oc in range(4):
        dstt = (KT, KT, VT, VT)[oc]
        for t8 in range(8):
            ps = pb[t8 % 2]
            for c in range(8):
                k.T.mm(ps[:], wkv[:, c, oc * 128:(oc + 1) * 128],
                       xaT[:, c, t8 * 512:(t8 + 1) * 512], start=(c == 0), stop=(c == 7))
            k.A.act(dstt[:, oc % 2, t8 * 512:(t8 + 1) * 512], ps[:], AF.Identity)
    scA.close()

    scC = ExitStack()
    sbC = lambda name, shape, dtype: scC.enter_context(nc.sbuf_tensor(name, list(shape), dtype))
    sqb = sbC("sqb", [128, L], BF16)
    rowb = sbC("rowbb", [1, L], F32)
    rsb = sbC("rsbc", [128, L], F32)
    rot = sbC("rotb", [128, L], BF16)
    cosT = sbC("cosTb", [128, L], BF16)
    sinT = sbC("sinTb", [128, L], BF16)
    qn2 = sbC("qn2", [128, 2], F32)
    k.S.dma(cosT[:], k.din["cosT"])
    k.S.dma(sinT[:], k.din["sinT"])
    k.S.dma(qn2[0:64, 0:1], k.din["qn_w"].unsqueeze(1))
    k.S.dma(qn2[64:128, 0:1], k.din["qn_w"].unsqueeze(1))
    k.S.dma(qn2[0:64, 1:2], k.din["kn_w"].unsqueeze(1))
    k.S.dma(qn2[64:128, 1:2], k.din["kn_w"].unsqueeze(1))

    def qknorm_rope(T, nchunks, gcol):
        for ch in range(nchunks):
            k.A.act(sqb[:], T[:, ch, :], AF.Square)
            for half in range(2):
                b = half * 64
                for p8 in range(8):
                    sl = slice(p8 * 512, (p8 + 1) * 512)
                    k.T.mm(pb[2][0:1, :], onecol[b:b + 64, 0:1], sqb[b:b + 64, sl],
                           start=True, stop=True)
                    k.A.act(rowb[0:1, sl], pb[2][0:1, :], AF.Ln, bias=1e-6, scale=1.0 / 64)
                k.A.act(rowb[0:1, :], rowb[0:1, :], AF.Exp, scale=-0.5)
                k.S.dma(k.bounce[4:5, :], rowb[0:1, :])
                k.G.dma(rsb[b:b + 64, :], k.bounce[4:5, :].partition_broadcast(64).squeeze(1))
                hsl = T[:, ch, :][b:b + 64, :]
                k.V.stt(hsl, hsl, qn2[b:b + 64, gcol:gcol + 1], rsb[b:b + 64, :],
                        ALU.mult, ALU.mult)
                k.S.dma(rot[b:b + 32, :], hsl[32:64, :])
                k.S.dma(rot[b + 32:b + 64, :], hsl[0:32, :])
                k.V.tt(rot[b:b + 64, :], rot[b:b + 64, :], sinT[b:b + 64, :], ALU.mult)
                k.V.tt(hsl, hsl, cosT[b:b + 64, :], ALU.mult)
                k.V.tt(hsl, hsl, rot[b:b + 64, :], ALU.add)

    qknorm_rope(QT, 8, 0)
    qknorm_rope(KT, 2, 1)
    scC.close()

    scD = ExitStack()
    sbD = lambda name, shape, dtype: scD.enter_context(nc.sbuf_tensor(name, list(shape), dtype))
    vtok = sbD("vtokb", [128, 2, 32, 130], BF16)
    KTd = sbD("KTdb", [128, 4, L], BF16)
    pexpb = sbD("pexpb", [128, 2, 512], BF16)
    sigt = sbD("sigtb", [128, 512], F32)
    rsb2 = sbD("rsb2", [64, 512], F32)
    anorm = sbD("anormb", [64, 512], BF16)
    for j in range(4):
        srcK = KT[:, j // 2, :][(j % 2) * 64:(j % 2) * 64 + 64, :]
        k.S.dma(KTd[0:64, j, :], srcK)
        k.S.dma(KTd[64:128, j, :], srcK)
    for pr in range(2):
        for kt in range(32):
            pt = (pt1, pt2)[kt % 2]
            k.T.transpose(pt[:], VT[:, pr, kt * 128:(kt + 1) * 128], k.identb[:])
            k.V.copy(vtok[:, pr, kt, 0:64], pt[:, 0:64])
            k.V.copy(vtok[:, pr, kt, 65:129], pt[:, 64:128])
        k.V.memset(vtok[:, pr, :, 64:65], 1.0)
        k.V.memset(vtok[:, pr, :, 129:130], 1.0)

    psO = pst("psOb", [128, 512])
    for h in range(16):
        kv = h // 4
        qch, qb = h // 2, (h % 2) * 64
        kch = kv // 2
        qhsl = QT[:, qch, :][qb:qb + 64, :]
        khsl = KTd[:, kv, :][qb:qb + 64, :]
        for qp in range(8):
            q0 = qp * 512
            for kt in range(32):
                psA = pb[kt % 2]
                k.T.mm(psA[:], khsl[:, kt * 128:(kt + 1) * 128], qhsl[:, q0:q0 + 512],
                       start=True, stop=True)
                pe = pexpb[:, kt % 2, :]
                k.A.act(pe, psA[:], AF.Exp, scale=0.125, bias=vcol(k, "Mb"))
                k.T.mm(psO[0:65, :], vtok[:, kch, kt, (kv % 2) * 65:(kv % 2) * 65 + 65],
                       pe, start=(kt == 0), stop=(kt == 31))
            k.V.ts(sigt[64:65, :], psO[64:65, :], 1e-30, None, ALU.add)
            k.V.recip(sigt[64:65, :], sigt[64:65, :])
            k.S.dma(k.bounce[4:5, q0:q0 + 512], sigt[64:65, :])
            k.G.dma(rsb2[:], k.bounce[4:5, q0:q0 + 512].partition_broadcast(64).squeeze(1))
            k.V.tt(anorm[:], psO[0:64, :], rsb2[:], ALU.mult)
            k.S.dma(k.mixd[h * 64:(h + 1) * 64, q0:q0 + 512], anorm[:])

    scE = ExitStack()
    sbE = lambda name, shape, dtype: scE.enter_context(nc.sbuf_tensor(name, list(shape), dtype))
    wo = sbE("wob2", [128, 8, 1024], BF16)
    mixs = sbE("mixsb", [128, 8, 512], BF16)
    dt512 = sbE("dt512b", [128, 512], BF16)
    stg = sbE("stgb", [128, 4, 1024], BF16)
    k.G.dma(wo[:], k.din["Wo"].rearrange("(c p) m -> p c m", p=128))
    for t8 in range(8):
        k.G.dma(mixs[:], k.mixd[:, t8 * 512:(t8 + 1) * 512].rearrange("(c p) m -> p c m", p=128))
        for oc in range(8):
            ps = pb[oc % 2]
            for c in range(8):
                k.T.mm(ps[:], wo[:, c, oc * 128:(oc + 1) * 128],
                       mixs[:, c, :], start=(c == 0), stop=(c == 7))
            k.A.act(dt512[:], ps[:], AF.Identity, scale=0.5)
            for j in range(4):
                pt = (pt1, pt2)[j % 2]
                k.T.transpose(pt[:], dt512[:, j * 128:(j + 1) * 128], k.identb[:])
                k.V.copy(stg[:, j, oc * 128:(oc + 1) * 128], pt[:])
        for j in range(4):
            it = t8 * 4 + j
            k.S.dma(k.attnd[it * 128:(it + 1) * 128, :], stg[:, j, :])
    scE.close()
    scD.close()
    scQ.close()

    scF = ExitStack()
    sbF = lambda name, shape, dtype: scF.enter_context(nc.sbuf_tensor(name, list(shape), dtype))
    xnT = sbF("xnT", [128, 8, L], BF16)
    dTT = sbF("dTT", [128, 8, L], BF16)
    aA = sbF("aAb", [128, 8, 128], F32)
    scFP = ExitStack()
    sbF = lambda name, shape, dtype: scFP.enter_context(nc.sbuf_tensor(name, list(shape), dtype))
    BT = sbF("BTb", [128, L], BF16)
    CT = sbF("CTb", [128, L], BF16)
    impT = sbF("impTb", [128, L], BF16)
    drT = sbF("drTb", [64, L], BF16)
    wssm = sbF("wssmb", [128, 1024], F32)
    bcast_row(k, k.din["ssmn_w"].unsqueeze(0), wssm[:], 128)
    xnt = sbF("xntb", [128, 1024], BF16)
    xrF = sbF("xrF", [128, 1024], F32)
    for it in range(32):
        k.S.dma(xrF[:], src[it * 128:(it + 1) * 128, :])
        k.V.stt(xnt[:], xrF[:], rstds[:, it:it + 1], wssm[:], ALU.mult, ALU.mult)
        for c in range(8):
            pt = (pt1, pt2)[c % 2]
            k.T.transpose(pt[:], xnt[:, c * 128:(c + 1) * 128], k.identb[:])
            k.V.copy(xnT[:, c, it * 128:(it + 1) * 128], pt[:])
    wx = sbF("wxb2", [128, 8, 320], BF16)
    k.G.dma(wx[:], k.din["Wx"].rearrange("(c p) m -> p c m", p=128))
    for t8 in range(8):
        sl = slice(t8 * 512, (t8 + 1) * 512)
        for c in range(8):
            k.T.mm(pb[0][0:64, :], wx[:, c, 0:64], xnT[:, c, sl], start=(c == 0), stop=(c == 7))
        k.A.act(drT[:, sl], pb[0][0:64, :], AF.Identity)
        for c in range(8):
            k.T.mm(pb[1][:], wx[:, c, 64:192], xnT[:, c, sl], start=(c == 0), stop=(c == 7))
        k.A.act(BT[:, sl], pb[1][:], AF.Identity)
        for c in range(8):
            k.T.mm(pb[2][:], wx[:, c, 192:320], xnT[:, c, sl], start=(c == 0), stop=(c == 7))
        k.A.act(CT[:, sl], pb[2][:], AF.Identity)
    wdt = sbF("wdtb", [64, 8, 128], BF16)
    k.G.dma(wdt[:], k.din["Wdt"].rearrange("r (c m) -> r c m", m=128))
    spl = sbF("splb", [128, 512], BF16)
    for oc in range(8):
        for t8 in range(8):
            sl = slice(t8 * 512, (t8 + 1) * 512)
            ps = pb[t8 % 2]
            k.T.mm(ps[:], wdt[:, oc, :], drT[:, sl], start=True, stop=True)
            k.A.act(spl[:], ps[:], AF.Abs, bias=vcol(k, "bdt", oc))
            k.A.act(spl[:], spl[:], AF.Exp, scale=-1.0)
            k.A.act(spl[:], spl[:], AF.Ln, bias=1.0)
            k.A.act(dTT[:, oc, sl], ps[:], AF.Relu, bias=vcol(k, "bdt", oc))
            k.V.tt(dTT[:, oc, sl], dTT[:, oc, sl], spl[:], ALU.add)
    wsel = sbF("wselb", [128, 8, 1024], BF16)
    k.G.dma(wsel[:], k.din["Wsel"].rearrange("(c p) m -> p c m", p=128))
    selc = sbF("selcb", [128, 512], BF16)
    for oc in range(8):
        for t8 in range(8):
            sl = slice(t8 * 512, (t8 + 1) * 512)
            ps = pb[2 + t8 % 2]
            for c in range(8):
                k.T.mm(ps[:], wsel[:, c, oc * 128:(oc + 1) * 128], xnT[:, c, sl],
                       start=(c == 0), stop=(c == 7))
            k.A.act(selc[:], ps[:], AF.Sigmoid, scale=vcol(k, "selg", oc),
                    bias=vcol(k, "bselg", oc))
            k.V.tt(xnT[:, oc, sl], xnT[:, oc, sl], selc[:], ALU.mult)
    wimp = sbF("wimpb", [128, 8, 128], BF16)
    k.G.dma(wimp[:], k.din["Wimp"].rearrange("(c p) m -> p c m", p=128))
    for t8 in range(8):
        sl = slice(t8 * 512, (t8 + 1) * 512)
        ps = pb[t8 % 2]
        for c in range(8):
            k.T.mm(ps[:], wimp[:, c, :], xnT[:, c, sl], start=(c == 0), stop=(c == 7))
        k.A.act(impT[:, sl], ps[:], AF.Sigmoid, bias=vcol(k, "bimp", 0))
    k.G.dma(wsel[:], k.din["Wg"].rearrange("(c p) m -> p c m", p=128))
    for oc in range(8):
        for t8 in range(8):
            sl = slice(t8 * 512, (t8 + 1) * 512)
            ps = pb[2 + t8 % 2]
            for c in range(8):
                k.T.mm(ps[:], wsel[:, c, oc * 128:(oc + 1) * 128], xnT[:, c, sl],
                       start=(c == 0), stop=(c == 7))
            k.A.act(selc[:], ps[:], AF.Sigmoid, bias=vcol(k, "bg", oc))
            k.V.tt(xnT[:, oc, sl], xnT[:, oc, sl], selc[:], ALU.mult)
    k.V.tt(BT[:], BT[:], impT[:], ALU.mult)
    k.V.tt(CT[:], CT[:], impT[:], ALU.mult)
    for c in range(8):
        k.V.tt(xnT[:, c, :], xnT[:, c, :], dTT[:, c, :], ALU.mult)
    k.S.dma(k.dramB[0:128, :], BT[:])
    k.S.dma(k.dramB[128:256, :], CT[:])
    k.G.dma(aA[:], k.din["A_log"].rearrange("(c p) n -> p c n", p=128))
    for c in range(8):
        k.A.act(aA[:, c, :], aA[:, c, :], AF.Exp)
        k.V.ts(aA[:, c, :], aA[:, c, :], -1.0, None, ALU.mult)
    scFP.close()

    scG = ExitStack()
    sbG = lambda name, shape, dtype: scG.enter_context(nc.sbuf_tensor(name, list(shape), dtype))
    SEG = 1024
    NSEG = L // SEG
    yacc = sbG("yaccb", [128, 8, SEG], BF16)
    carry = sbG("carryb", [128, 1024], F32)
    Bb = [sbG(f"Bbb{i}", [128, SEG], BF16) for i in range(2)]
    Cb = [sbG(f"Cbb{i}", [128, SEG], BF16) for i in range(2)]
    dab = [sbG(f"dabb{i}", [128, SEG], BF16) for i in range(2)]
    ub = sbG("ubb", [128, SEG], BF16)
    stile = sbG("stileb", [128, SEG], BF16)
    ztile = sbG("ztileb", [128, SEG], BF16)
    xd128 = sbG("xd128b", [128, 128], BF16)
    sd512 = sbG("sd512b", [128, 512], BF16)
    ssmstg = sbG("ssmstgb", [128, 4, 1024], BF16)
    xrG = sbG("xrG", [128, 1024], F32)
    atile = sbG("atileb", [128, 1024], BF16)
    dpb = sbG("dpbb", [128, 1024], BF16)
    wout = sbG("woutb", [128, 8, 1024], BF16)
    bcast_row(k, k.din["Dp"].unsqueeze(0), dpb[:], 128)
    k.G.dma(wout[:], k.din["Wout"].rearrange("(c p) m -> p c m", p=128))
    k.V.memset(carry[:], 0.0)

    for seg in range(NSEG):
        s0 = seg * SEG
        for c in range(8):
            k.V.memset(yacc[:, c, :], 0.0)
        for n in range(128):
            bb, cb = Bb[n % 2], Cb[n % 2]
            k.S.dma(bb[:], k.dramB[n:n + 1, s0:s0 + SEG].partition_broadcast(128).squeeze(1))
            k.S.dma(cb[:], k.dramB[128 + n:129 + n, s0:s0 + SEG].partition_broadcast(128).squeeze(1))
            for c in range(8):
                col = n * 8 + c
                db = dab[col % 2]
                k.A.act(db[:], dTT[:, c, s0:s0 + SEG], AF.Exp, scale=aA[:, c, n:n + 1])
                k.V.tt(ub[:], xnT[:, c, s0:s0 + SEG], bb[:], ALU.mult)
                k.V.scan(stile[:], db[:], ub[:], carry[:, col:col + 1], ALU.mult, ALU.add)
                k.V.copy(carry[:, col:col + 1], stile[:, SEG - 1:SEG])
                k.V.tt(ztile[:], stile[:], cb[:], ALU.mult)
                k.V.tt(yacc[:, c, :], yacc[:, c, :], ztile[:], ALU.add)
        for j in range(SEG // 128):
            it = (s0 // 128) + j
            k.S.dma(xrG[:], src[it * 128:(it + 1) * 128, :])
            k.V.tt(atile[:], xrG[:], dpb[:], ALU.mult)
            for c in range(8):
                pt = (pt1, pt2)[c % 2]
                k.T.transpose(pt[:], atile[:, c * 128:(c + 1) * 128], k.identb[:])
                k.V.copy(xd128[:], pt[:])
                k.V.tt(yacc[:, c, j * 128:(j + 1) * 128],
                       yacc[:, c, j * 128:(j + 1) * 128], xd128[:], ALU.add)
        for t4 in range(SEG // 512):
            sl = slice(t4 * 512, (t4 + 1) * 512)
            for oc in range(8):
                ps = pb[oc % 2]
                for c in range(8):
                    k.T.mm(ps[:], wout[:, c, oc * 128:(oc + 1) * 128], yacc[:, c, sl],
                           start=(c == 0), stop=(c == 7))
                k.A.act(sd512[:], ps[:], AF.Identity, scale=0.5)
                for j in range(4):
                    pt = (pt1, pt2)[j % 2]
                    k.T.transpose(pt[:], sd512[:, j * 128:(j + 1) * 128], k.identb[:])
                    k.V.copy(ssmstg[:, j, oc * 128:(oc + 1) * 128], pt[:])
            for j in range(4):
                it = (s0 + t4 * 512) // 128 + j
                k.S.dma(xrG[:], src[it * 128:(it + 1) * 128, :])
                k.G.dma(atile[:], k.attnd[it * 128:(it + 1) * 128, :])
                k.V.tt(xrG[:], xrG[:], atile[:], ALU.add)
                k.V.tt(xrG[:], xrG[:], ssmstg[:, j, :], ALU.add)
                k.S.dma(k.xln2[it * 128:(it + 1) * 128, :], xrG[:])
    scG.close()
    scF.close()
    ph.close()


def emit_swiglu(k, src, src_outer, dst):
    nc = k.nc
    ph = ExitStack()
    sbt = lambda name, shape, dtype: ph.enter_context(nc.sbuf_tensor(name, list(shape), dtype))
    pst = lambda name, shape, dtype=F32: ph.enter_context(nc.psum_tensor(name, list(shape), dtype))

    xfT = sbt("xfTs", [128, 8, 512], BF16)
    hbl = sbt("hbls", [128, 32, 512], BF16)
    w2r = sbt("w2rs", [128, 32, 1024], BF16)
    w1a = sbt("w1as", [128, 8, 128], BF16)
    w1g = sbt("w1gs", [128, 8, 128], BF16)
    sa = sbt("sas", [128, 512], BF16)
    sg = sbt("sgs", [128, 512], BF16)
    dfT = sbt("dfTs", [128, 8, 512], BF16)
    stg = sbt("stgs", [128, 4, 1024], BF16)
    xr = sbt("xrs", [128, 1024], F32)
    xo = sbt("xos", [128, 1024], F32)
    xlnt = sbt("xlnts", [128, 1024], BF16)
    stat = sbt("stats", [128, 4], F32)
    wffn = sbt("wffns", [128, 1024], F32)
    bcast_row(k, k.din["nffn_w"].unsqueeze(0), wffn[:], 128)

    pb = [pst(f"pbs{i}", [128, 512]) for i in range(4)]
    pt1 = pst("pts1", [128, 128], BF16)
    pt2 = pst("pts2", [128, 128], BF16)

    k.G.dma(w2r[:], k.din["W2"].rearrange("(c p) m -> p c m", p=128))
    for tt in range(8):
        for j in range(4):
            it = tt * 4 + j
            k.S.dma(xr[:], src[it * 128:(it + 1) * 128, :])
            rms_rstd(k, xr[:], stat)
            k.V.stt(xlnt[:], xr[:], stat[:, 2:3], wffn[:], ALU.mult, ALU.mult)
            for c in range(8):
                pt = (pt1, pt2)[c % 2]
                k.T.transpose(pt[:], xlnt[:, c * 128:(c + 1) * 128], k.identb[:])
                k.V.copy(xfT[:, c, j * 128:(j + 1) * 128], pt[:])
        for fc in range(32):
            k.G.dma(w1a[:], k.din["W1"][:, fc * 128:(fc + 1) * 128].rearrange("(c p) m -> p c m", p=128))
            k.G.dma(w1g[:], k.din["W1"][:, 4096 + fc * 128:4096 + (fc + 1) * 128].rearrange("(c p) m -> p c m", p=128))
            for c in range(8):
                k.T.mm(pb[0][:], w1a[:, c, :], xfT[:, c, :], start=(c == 0), stop=(c == 7))
            for c in range(8):
                k.T.mm(pb[1][:], w1g[:, c, :], xfT[:, c, :], start=(c == 0), stop=(c == 7))
            k.A.act(sa[:], pb[0][:], AF.Silu)
            k.A.act(sg[:], pb[1][:], AF.Identity)
            k.V.tt(hbl[:, fc, :], sa[:], sg[:], ALU.mult)
        for dgrp in range(2):
            for fc in range(32):
                for do in range(4):
                    k.T.mm(pb[do][:], w2r[:, fc, dgrp * 512 + do * 128:dgrp * 512 + (do + 1) * 128],
                           hbl[:, fc, :], start=(fc == 0), stop=(fc == 31))
            for do in range(4):
                k.A.act(dfT[:, dgrp * 4 + do, :], pb[do][:], AF.Identity)
        for oc in range(8):
            for j in range(4):
                pt = (pt1, pt2)[j % 2]
                k.T.transpose(pt[:], dfT[:, oc, j * 128:(j + 1) * 128], k.identb[:])
                k.V.copy(stg[:, j, oc * 128:(oc + 1) * 128], pt[:])
        for j in range(4):
            it = tt * 4 + j
            k.S.dma(xr[:], src[it * 128:(it + 1) * 128, :])
            k.S.dma(xo[:], src_outer[it * 128:(it + 1) * 128, :])
            k.V.tt(xr[:], xr[:], stg[:, j, :], ALU.add)
            k.V.tt(xr[:], xr[:], xo[:], ALU.add)
            k.S.dma(dst[it * 128:(it + 1) * 128, :], xr[:])
    ph.close()



def _consts():
    c = {}
    c["ident_b"] = np.eye(128, dtype=ml_dtypes.bfloat16)
    inv = 1.0 / (10000.0 ** (np.arange(0, 64, 2, dtype=np.float64) / 64.0))
    fr = np.arange(L, dtype=np.float64)[:, None] * inv[None, :]
    emb = np.concatenate([fr, fr], axis=1)
    cos = np.cos(emb).T
    sin = np.sin(emb).T
    sinr = sin.copy()
    sinr[0:32, :] *= -1.0
    c["cosT"] = np.ascontiguousarray(np.concatenate([cos, cos], 0)).astype(ml_dtypes.bfloat16)
    c["sinT"] = np.ascontiguousarray(np.concatenate([sinr, sinr], 0)).astype(ml_dtypes.bfloat16)
    kp = np.arange(128)[:, None]
    qo = np.arange(256)[None, :]
    allowed = (qo - kp > 0) & (qo - kp <= 128)
    c["maskband"] = np.where(allowed, 0.0, NEG).astype(np.float32)
    lat = np.arange(64)[:, None]
    tq = np.arange(L)[None, :]
    c["masklatT"] = np.where(lat < (tq // 64), 0.0, NEG).astype(np.float32)
    return c


def _in_map(inputs, b):
    m = {
        "x": inputs["L0_attn_a_x"][b],
        "qkv_w": inputs["L0_attn_a_qkv_w"], "qkv_b": inputs["L0_attn_a_qkv_b"],
        "rq_w": inputs["L0_attn_a_rq_w"], "rq_b": inputs["L0_attn_a_rq_b"],
        "rkv_w": inputs["L0_attn_a_rkv_w"], "rkv_b": inputs["L0_attn_a_rkv_b"],
        "rout_w": inputs["L0_attn_a_rout_w"], "rout_b": inputs["L0_attn_a_rout_b"],
        "out_w": inputs["L0_attn_a_out_w"], "out_b": inputs["L0_attn_a_out_b"],
        "gl_w": inputs["L0_attn_a_gl_w"], "gr_w": inputs["L0_attn_a_gr_w"],
        "nattn_w": inputs["L0_attn_b_norm_attn_w"],
        "Wq": inputs["L0_attn_b_Wq"], "Wk": inputs["L0_attn_b_Wk"],
        "Wv": inputs["L0_attn_b_Wv"], "Wo": inputs["L0_attn_b_Wo"],
        "qn_w": inputs["L0_attn_b_qn_w"], "kn_w": inputs["L0_attn_b_kn_w"],
        "ssmn_w": inputs["L0_attn_b_ssm_norm_w"],
        "Wx": inputs["L0_attn_b_Wx"], "Wdt": inputs["L0_attn_b_Wdt"],
        "bdt": inputs["L0_attn_b_bdt"], "A_log": inputs["L0_attn_b_A_log"],
        "Dp": inputs["L0_attn_b_Dp"], "Wsel": inputs["L0_attn_b_Wsel"],
        "bsel": inputs["L0_attn_b_bsel"], "selg": inputs["L0_attn_b_sel_gate"],
        "Wimp": inputs["L0_attn_b_Wimp"], "bimp": inputs["L0_attn_b_bimp"],
        "Wg": inputs["L0_attn_b_Wg"], "bg": inputs["L0_attn_b_bg"],
        "Wout": inputs["L0_attn_b_Wout"], "nffn_w": inputs["L0_attn_b_norm_ffn_w"],
        "W1": inputs["L0_attn_b_W1"], "W2": inputs["L0_attn_b_W2"],
        "k0": np.asarray(inputs["L0_ffn_k0"]).view(np.uint8),
        "k1": np.asarray(inputs["L0_ffn_k1"]).view(np.uint8),
        "k2": np.asarray(inputs["L0_ffn_k2"]).view(np.uint8),
        "s0": np.asarray(inputs["L0_ffn_s0"]).reshape(1),
        "s1": np.asarray(inputs["L0_ffn_s1"]).reshape(1),
        "s2": np.asarray(inputs["L0_ffn_s2"]).reshape(1),
    }
    m = {n: np.ascontiguousarray(np.asarray(v)) for n, v in m.items()}
    m.update(_consts())
    return m


def run_upto(inputs, upto, cores=None):
    k = build_program(upto)
    cores = cores if cores is not None else list(range(8))
    in_maps = [_in_map(inputs, b) for b in cores]
    res = run_bass_kernel_spmd(k.nc, in_maps, core_ids=list(range(len(cores))))
    return np.stack([np.asarray(res.results[i]["out"]) for i in range(len(cores))], axis=0)


def kernel(**inputs):
    k = build_program("all")
    in_maps = [_in_map(inputs, b) for b in range(8)]
    res = run_bass_kernel_spmd(k.nc, in_maps, core_ids=list(range(8)))
    return np.stack([np.asarray(res.results[b]["out"]) for b in range(8)], axis=0)
